# version 01
# baseline (speedup 1.0000x reference)
import numpy as np
import ml_dtypes
from contextlib import ExitStack

import concourse.bass as bass
import concourse.mybir as mybir
from concourse.bass_utils import run_bass_kernel_spmd

F32 = mybir.dt.float32
BF16 = mybir.dt.bfloat16
F16 = mybir.dt.float16
AF = mybir.ActivationFunctionType
ALU = mybir.AluOpType
AX = mybir.AxisListType.X

NCORES = 8
S, D, H, HD, FD, I = 4096, 1024, 16, 64, 256, 3072
NT = S // 128
NU = S // 512
IC = I // 128
C_MAG = 12582912.0
QMAX = 32767.0
ENGS = ("sync", "tensor", "scalar", "vector", "gpsimd")

CZERO, CONE, CNEG1, CC_, CNC, CEPS6, CEPS5, C65536 = range(8)
SC_INV_S1, SC_S1, SC_INV_S2G, SC_S2G, SC_CG, SC_INV_S2U, SC_C45, \
    SC_INV_S4, SC_INV_S5, SC_S5 = range(10)
T_G1, T_S1, T_IS1, T_GU, T_GG, T_S2U, T_IS2U, T_S2G, T_IS2G, T_CG, \
    T_G4, T_S4, T_IS4, T_C45, T_G5, T_S5V, T_IS5 = range(17)

SEM_NAMES = ("L1", "L2", "S1", "T1", "T2", "T3", "SC", "V1", "V2", "V3", "CC",
             "KA", "KB", "KC", "KD", "KE", "KF")
KS4 = ("KA", "KB", "KC", "KD")


class Build:
    def __init__(self, dbg=()):
        nc = bass.Bass("TRN2", target_bir_lowering=False)
        self.nc = nc
        self.es = ExitStack()
        self.dbg = set(dbg)
        es = self.es

        dt = nc.dram_tensor
        self.x = dt("x", [S, D], F32, kind="ExternalInput")
        self.wqT = dt("wqT", [D, D], F32, kind="ExternalInput")
        self.wkT = dt("wkT", [D, D], F32, kind="ExternalInput")
        self.wvT = dt("wvT", [D, D], F32, kind="ExternalInput")
        self.woT = dt("woT", [D, D], F32, kind="ExternalInput")
        self.wfT = dt("wfT", [HD, FD], F32, kind="ExternalInput")
        self.wup_nat = dt("wup_nat", [I // 8, D], F32, kind="ExternalInput")
        self.wgt_nat = dt("wgt_nat", [I // 8, D], F32, kind="ExternalInput")
        self.wdn_nat = dt("wdn_nat", [D // 8, I], F32, kind="ExternalInput")
        self.identbf = dt("identbf", [128, 128], BF16, kind="ExternalInput")
        self.identf32 = dt("identf32", [128, 128], F32, kind="ExternalInput")
        self.selmat = dt("selmat", [16, 1024], F32, kind="ExternalInput")
        self.identf16 = dt("identf16", [128, 128], F16, kind="ExternalInput")
        self.out = dt("out", [S, D], F32, kind="ExternalOutput")

        def scratch(name, shape, dtp):
            kind = "ExternalOutput" if name in self.dbg else "Internal"
            return dt(name, shape, dtp, kind=kind)
        self.qT_sp = scratch("qT_sp", [8, 128, S], BF16)
        self.kT_sp = scratch("kT_sp", [8, 128, S], BF16)
        self.xp_sp = scratch("xp_sp", [NT, 128, D], F32)
        self.up_sp = scratch("up_sp", [IC, NU, 128, 512], F16)
        self.gt_sp = scratch("gt_sp", [IC, NU, 128, 512], F16)
        self.m1_sp = scratch("m1_sp", [IC, NU, 128, 512], F16)
        self.m2_sp = scratch("m2_sp", [IC, NU, 128, 512], F16)
        self.agu_in = dt("agu_in", [D, I // 8], F16)
        self.agu_out = dt("agu_out", [NCORES * D, I // 8], F16)
        if "agu_out" in self.dbg:
            self.agu_dbg = dt("agu_dbg", [NCORES * D, I // 8], F16,
                              kind="ExternalOutput")
            self.agi_dbg = dt("agi_dbg", [D, I // 8], BF16,
                              kind="ExternalOutput")
        self.agg_in = dt("agg_in", [D, I // 8], F16)
        self.agg_out = dt("agg_out", [NCORES * D, I // 8], F16)
        self.agd_in = dt("agd_in", [I, D // 8], F16)
        self.agd_out = dt("agd_out", [NCORES * I, D // 8], F16)
        self.ar_in = {k: dt(f"ar{k}_in", [1, 2], F32) for k in (1, 2, 4, 5)}
        if "xn_dbg" in self.dbg:
            self.xn_dbg = dt("xn_dbg", [128, 32768], BF16, kind="ExternalOutput")
            self.st_dbg = dt("st_dbg", [128, 64], F32, kind="ExternalOutput")
        if "sc_dbg" in self.dbg:
            self.sc_dbg = dt("sc_dbg", [128, 48], F32, kind="ExternalOutput")
        if "oq_dbg" in self.dbg:
            self.oq_dbg = dt("oq_dbg", [128, 12288], F16, kind="ExternalOutput")
        self.ar_out = {k: dt(f"ar{k}_out", [1, 2], F32) for k in (1, 2, 4, 5)}

        def sb(name, shape, dtp):
            return es.enter_context(nc.sbuf_tensor(name, shape, dtp))
        self.bigA = sb("bigA", [128, 32768], BF16)
        self.vaug = sb("vaug", [128, 33280], BF16)
        self.strm = sb("strm", [128, 8192], BF16)
        self.xt = sb("xt", [128, 4096], F32)
        self.ewA = sb("ewA", [128, 3072], F32)
        self.ewB = sb("ewB", [128, 4096], BF16)
        self.stg = sb("stg", [128, 4096], F16)
        self.kvh = sb("kvh", [128, 2080], BF16)
        self.ksbd = sb("ksbd", [128, 512], BF16)
        self.kvstg = sb("kvstg", [128, 1024], F32)
        self.kvstg_b = sb("kvstg_b", [128, 1024], BF16)
        self.wfb = sb("wfb", [128, 256], BF16)
        self.cst = sb("cst", [128, 8], F32)
        self.scal = sb("scal", [128, 16], F32)
        self.sct = sb("sct", [1, 24], F32)
        self.onesb = sb("onesb", [1, 512], BF16)
        self.onesf = sb("onesf", [1, 128], F32)
        self.idb = sb("idb", [128, 128], BF16)
        self.idf = sb("idf", [128, 128], F32)
        self.idh = sb("idh", [128, 128], F16)
        self.st1m = sb("st1m", [128, 32], F32)
        self.st1r = sb("st1r", [128, 32], F32)
        self.sum1 = sb("sum1", [128, 32], F32)
        self.sq1 = sb("sq1", [128, 32], F32)
        self.st2m = self.st1m
        self.st2r = self.st1r
        self.sum2 = self.sum1
        self.sq2 = self.sq1
        self.abuf = sb("abuf", [128, 32], F32)
        self.bcbuf = sb("bcbuf", [128, 32], F32)
        self.amx = sb("amx", [128, 8], F32)
        self.dqt = sb("dqt", [128, 64], F32)

        self.pb = [es.enter_context(nc.psum_tensor(f"pb{i}", [128, 512], F32))
                   for i in range(6)]
        self.pt = es.enter_context(nc.psum_tensor("pt", [128, 1024], BF16))
        self.pt2 = es.enter_context(nc.psum_tensor("pt2", [128, 1024], BF16))

        nc.const_aps.register(F32, 0.0, self.cst[:, CZERO:CZERO + 1])
        nc.const_aps.register(F32, 1.0, self.cst[:, CONE:CONE + 1])
        nc.const_aps.register(F32, -1.0, self.cst[:, CNEG1:CNEG1 + 1])
        nc.const_aps.register(F32, C_MAG, self.cst[:, CC_:CC_ + 1])
        nc.const_aps.register(F32, -C_MAG, self.cst[:, CNC:CNC + 1])
        nc.const_aps.register(F32, 1e-5, self.cst[:, CEPS5:CEPS5 + 1])
        nc.const_aps.register(F32, 65536.0, self.cst[:, C65536:C65536 + 1])

        self.sems = {}
        self.semv = {}
        for name in SEM_NAMES:
            self.sems[name] = es.enter_context(nc.semaphore(name))
            self.semv[name] = 0

    def phase(self, fn, *args):
        base = dict(self.semv)
        with self.nc.Block() as blk:
            for eng in ENGS:
                V = dict(base)
                def body(E, eng=eng, V=V):
                    fn(eng, E, V, *args)
                getattr(blk, eng)(body)
        V = dict(base)
        fn(None, None, V, *args)
        self.semv = V

    def tick(self, instr, V, s, n=1):
        V[s] += n
        if instr is not None:
            instr.then_inc(self.sems[s], n)

    def wt(self, E, s, val):
        if E is not None:
            E.wait_ge(self.sems[s], val)


    def ph_init(self, eng, E, V):
        t, wt = self.tick, self.wt
        if eng == "sync":
            t(E.dma_start(out=self.idb[:, :], in_=self.identbf[:, :]), V, "L1", 16)
            t(E.dma_start(out=self.idf[:, :], in_=self.identf32[:, :]), V, "L1", 16)
            t(E.dma_start(out=self.idh[:, :], in_=self.identf16[:, :]), V, "L1", 16)
            t(E.dma_start(out=self.ewA[0:64, 0:256], in_=self.wfT[:, :]), V, "L1", 16)
        else:
            for _ in range(4):
                t(None, V, "L1", 16)
        if eng == "vector":
            for col, val in ((CZERO, 0.0), (CONE, 1.0), (CNEG1, -1.0), (CC_, C_MAG),
                             (CNC, -C_MAG), (CEPS6, 1e-6), (CEPS5, 1e-5),
                             (C65536, 65536.0)):
                E.memset(self.cst[:, col:col + 1], float(val))
            E.memset(self.onesb[:, :], 1.0)
            E.memset(self.onesf[:, :], 1.0)
            E.memset(self.amx[:, :], 0.0)
            E.memset(self.vaug[:, 0:NT * 1040].rearrange(
                "p (t j) -> p t j", j=65)[:, :, 64:65], 1.0)
            wt(E, "L1", V["L1"])
            E.tensor_copy(self.wfb[0:64, :], self.ewA[0:64, 0:256])
            ti = E.tensor_copy(self.wfb[64:128, :], self.ewA[0:64, 0:256])
            t(ti, V, "V1", 1)
        else:
            t(None, V, "V1", 1)

    def ph_dequant(self, eng, E, V):
        t, wt = self.tick, self.wt
        jobs = [("up", self.wup_nat, 128 * r, 0, r) for r in range(3)] + \
               [("gt", self.wgt_nat, 128 * r, 0, r) for r in range(3)] + \
               [("dn", self.wdn_nat, 0, 1024 * c3, c3) for c3 in range(3)]
        import os
        if os.environ.get("DQ_ONE"):
            jobs = jobs[:1]
        b = dict(V)
        for j, (kind, src, ro, co, idx) in enumerate(jobs):
            wsl = self.ewA[:, 0:1024]
            q1 = self.ewA[:, 1024:2048]
            q2 = self.ewA[:, 2048:3072]
            dq = self.ewB[:, 0:1024].bitcast(F16)
            stage = self.ewB[:, 1024:2048].bitcast(F16)
            if eng == "sync":
                wt(E, "V1", b["V1"] + j)
                t(E.dma_start(out=wsl, in_=src[ro:ro + 128, co:co + 1024]), V, "L1", 16)
            else:
                t(None, V, "L1", 16)
            if eng == "vector":
                wt(E, "L1", b["L1"] + 16 * (j + 1))
                wt(E, "T1", b["T1"] + j)
                bm = self.dqt[:, 0:32]
                sc = self.dqt[:, 32:64]
                E.tensor_reduce(bm.rearrange("p (b o) -> p b o", o=1),
                                wsl.rearrange("p (b u) -> p b u", u=32),
                                AX, ALU.max, apply_absolute_value=True)
                E.drain()
                E.tensor_scalar(sc, bm, 1.0 / 7.0, 1e-12, ALU.mult, ALU.max)
                E.drain()
                E.reciprocal(bm, sc)
                E.drain()
                E.tensor_tensor(q1.rearrange("p (b u) -> p b u", u=32),
                                wsl.rearrange("p (b u) -> p b u", u=32),
                                bm.unsqueeze(2).broadcast_to([128, 32, 32]),
                                ALU.mult)
                E.drain()
                E.tensor_scalar(q2, q1, C_MAG, -C_MAG, ALU.add, ALU.add)
                E.drain()
                ti = E.tensor_tensor(dq.rearrange("p (b u) -> p b u", u=32),
                                     q2.rearrange("p (b u) -> p b u", u=32),
                                     sc.unsqueeze(2).broadcast_to([128, 32, 32]),
                                     ALU.mult)
                t(ti, V, "V1", 1)
            else:
                t(None, V, "V1", 1)
            if eng == "tensor":
                wt(E, "V1", b["V1"] + j + 1)
                wt(E, "SC", b["SC"] + j)
                for kk in range(8):
                    ti = E.transpose(self.pt[:, 128 * kk:128 * kk + 128].bitcast(F16),
                                     dq[:, 128 * kk:128 * kk + 128], self.idh[:, :])
                t(ti, V, "T1", 1)
            else:
                t(None, V, "T1", 1)
            if eng == "scalar":
                wt(E, "T1", b["T1"] + j + 1)
                wt(E, "S1", b["S1"] + 128 * j)
                ti = E.activation(stage, self.pt[:, :].bitcast(F16), AF.Copy,
                                  bias=0.0, scale=self.cst[:, CONE:CONE + 1])
                t(ti, V, "SC", 1)
            else:
                t(None, V, "SC", 1)
            if eng == "gpsimd":
                wt(E, "SC", b["SC"] + j + 1)
                dst = {"up": self.agu_in, "gt": self.agg_in, "dn": self.agd_in}[kind]
                for kk in range(8):
                    blkv = stage[:, 128 * kk:128 * kk + 128]
                    if kind in ("up", "gt"):
                        ti = E.dma_start(out=dst[128 * kk:128 * kk + 128,
                                                 128 * idx:128 * idx + 128], in_=blkv)
                    else:
                        ti = E.dma_start(out=dst[1024 * idx + 128 * kk:
                                                 1024 * idx + 128 * kk + 128, 0:128],
                                         in_=blkv)
                    t(ti, V, "S1", 16)
            else:
                for kk in range(8):
                    t(None, V, "S1", 16)
        import os
        if os.environ.get("NO_AG"):
            return
        if eng == "gpsimd":
            wt(E, "S1", b["S1"] + 128 * 9)
            for (i_, o_) in ((self.agu_in, self.agu_out), (self.agg_in, self.agg_out),
                            (self.agd_in, self.agd_out)):
                ti = E.collective_compute("AllGather", ALU.bypass,
                                          replica_groups=[list(range(NCORES))],
                                          ins=[i_.ap().opt()], outs=[o_.ap().opt()])
                t(ti, V, "CC", 1)
            wt(E, "CC", V["CC"])
        else:
            for _ in range(3):
                t(None, V, "CC", 1)

    def ph_stats1(self, eng, E, V):
        t, wt = self.tick, self.wt
        b = dict(V)
        for i in range(NT):
            xin = self.xt[:, 1024 * (i % 4):1024 * (i % 4) + 1024]
            ks = KS4[i % 4]
            if eng == "sync":
                if i >= 4:
                    wt(E, "V1", b["V1"] + (i - 4) + 1)
                t(E.dma_start(out=xin, in_=self.x[128 * i:128 * i + 128, :]),
                  V, ks, 16)
            else:
                t(None, V, ks, 16)
            if eng == "vector":
                wt(E, ks, b[ks] + 16 * (i // 4 + 1))
                E.tensor_tensor(self.ewA[:, 0:1024], xin, xin, ALU.mult)
                E.tensor_reduce(self.sum1[:, i:i + 1], xin, AX, ALU.add)
                ti = E.tensor_reduce(self.sq1[:, i:i + 1], self.ewA[:, 0:1024],
                                     AX, ALU.add)
                t(ti, V, "V1", 1)
            else:
                t(None, V, "V1", 1)
        if eng == "vector":
            E.tensor_scalar(self.st1m[:, :], self.sum1[:, :], 1.0 / D, None, ALU.mult)
            E.tensor_scalar(self.ewA[:, 0:32], self.sq1[:, :], 1.0 / D, None, ALU.mult)
            E.tensor_tensor(self.ewA[:, 32:64], self.st1m[:, :], self.st1m[:, :],
                            ALU.mult)
            E.drain()
            ti = E.tensor_tensor(self.ewA[:, 0:32], self.ewA[:, 0:32],
                                 self.ewA[:, 32:64], ALU.subtract)
            t(ti, V, "V1", 1)
        else:
            t(None, V, "V1", 1)
        if eng == "scalar":
            wt(E, "V1", V["V1"])
            ti = E.activation(self.ewA[:, 64:96], self.ewA[:, 0:32], AF.Sqrt,
                              bias=self.cst[:, CEPS5:CEPS5 + 1],
                              scale=self.cst[:, CONE:CONE + 1])
            t(ti, V, "SC", 1)
        else:
            t(None, V, "SC", 1)
        if eng == "vector":
            wt(E, "SC", V["SC"])
            ti = E.reciprocal(self.st1r[:, :], self.ewA[:, 64:96])
            t(ti, V, "V1", 1)
        else:
            t(None, V, "V1", 1)

    def ph_norm1(self, eng, E, V):
        t, wt = self.tick, self.wt
        b = dict(V)
        for i in range(NT):
            xin = self.xt[:, 1024 * (i % 4):1024 * (i % 4) + 1024]
            xnb = self.ewB[:, 1024 * (i % 4):1024 * (i % 4) + 1024]
            ptt = self.pt if i % 2 == 0 else self.pt2
            ks = KS4[i % 4]
            if eng == "sync":
                if i >= 4:
                    wt(E, "V1", b["V1"] + 2 * (i - 4) + 1)
                t(E.dma_start(out=xin, in_=self.x[128 * i:128 * i + 128, :]),
                  V, ks, 16)
            else:
                t(None, V, ks, 16)
            if eng == "vector":
                wt(E, ks, b[ks] + 16 * (i // 4 + 1))
                if i >= 4:
                    wt(E, "T1", b["T1"] + (i - 4) + 1)
                ti = E.tensor_scalar(xnb, xin, self.st1m[:, i:i + 1],
                                     self.st1r[:, i:i + 1], ALU.subtract, ALU.mult)
                t(ti, V, "V1", 1)
            else:
                t(None, V, "V1", 1)
            if eng == "tensor":
                wt(E, "V1", b["V1"] + 2 * i + 1)
                if i >= 2:
                    wt(E, "V1", b["V1"] + 2 * (i - 2) + 2)
                for kk in range(8):
                    ti = E.transpose(ptt[:, 128 * kk:128 * kk + 128],
                                     xnb[:, 128 * kk:128 * kk + 128], self.idb[:, :])
                t(ti, V, "T1", 1)
            else:
                t(None, V, "T1", 1)
            if eng == "vector":
                wt(E, "T1", b["T1"] + i + 1)
                ti = E.tensor_copy(
                    self.bigA[:, :].rearrange("p (k s) -> p k s", k=8)[
                        :, :, 128 * i:128 * i + 128],
                    ptt[:, :].rearrange("p (k s) -> p k s", k=8))
                t(ti, V, "V1", 1)
            else:
                t(None, V, "V1", 1)

    def ph_xndbg(self, eng, E, V):
        t = self.tick
        if eng == "gpsimd":
            t(E.dma_start(out=self.xn_dbg[:, :], in_=self.bigA[:, :]), V, "S1", 16)
            t(E.dma_start(out=self.st_dbg[:, 0:32], in_=self.st1m[:, :]), V, "S1", 16)
            ti = E.dma_start(out=self.st_dbg[:, 32:64], in_=self.st1r[:, :])
            t(ti, V, "S1", 16)
            E.wait_ge(self.sems["S1"], V["S1"])
        else:
            for _ in range(3):
                t(None, V, "S1", 16)

    def ph_qkv(self, eng, E, V):
        t, wt = self.tick, self.wt
        for wi, (wsrc, spill) in enumerate(((self.wqT, self.qT_sp),
                                            (self.wkT, self.kT_sp),
                                            (self.wvT, None))):
            b = dict(V)
            for kk in range(8):
                wstage = self.ewA[:, 1024 * (kk % 2):1024 * (kk % 2) + 1024]
                ks = KS4[kk % 2]
                if eng == "sync":
                    wt(E, "V1", max(0, b["V1"] + kk - 1))
                    t(E.dma_start(out=wstage, in_=wsrc[128 * kk:128 * kk + 128, :]),
                      V, ks, 16)
                else:
                    t(None, V, ks, 16)
                if eng == "vector":
                    wt(E, ks, b[ks] + 16 * (kk // 2 + 1))
                    if kk == 0:
                        wt(E, "T1", b["T1"])
                    ti = E.tensor_copy(self.strm[:, 1024 * kk:1024 * kk + 1024],
                                       wstage)
                    t(ti, V, "V1", 1)
                else:
                    t(None, V, "V1", 1)
            if spill is not None:
                n = 0
                for m in range(8):
                    for u in range(8):
                        pbank = self.pb[n % 2]
                        if eng == "tensor":
                            wt(E, "V1", b["V1"] + 8)
                            wt(E, "SC", max(0, b["SC"] + (n - 2) + 1))
                            for kk in range(8):
                                ti = E.matmul(
                                    pbank[:, :],
                                    self.strm[:, 1024 * kk + 128 * m:
                                              1024 * kk + 128 * m + 128],
                                    self.bigA[:, 4096 * kk + 512 * u:
                                              4096 * kk + 512 * u + 512],
                                    start=(kk == 0), stop=(kk == 7))
                            t(ti, V, "T1", 1)
                        else:
                            t(None, V, "T1", 1)
                        stgb = self.ewB[:, 512 * (n % 4):512 * (n % 4) + 512]
                        if eng == "scalar":
                            wt(E, "T1", b["T1"] + n + 1)
                            if n >= 4:
                                wt(E, "S1", b["S1"] + 16 * (n - 4 + 1))
                            ti = E.activation(stgb, pbank[:, :], AF.Copy, bias=0.0,
                                              
                                              scale=self.cst[:, CONE:CONE + 1])
                            t(ti, V, "SC", 1)
                        else:
                            t(None, V, "SC", 1)
                        if eng == "gpsimd":
                            wt(E, "SC", b["SC"] + n + 1)
                            ti = E.dma_start(out=spill[m, :, 512 * u:512 * u + 512],
                                             in_=stgb)
                            t(ti, V, "S1", 16)
                        else:
                            t(None, V, "S1", 16)
                        n += 1
            else:
                for st in range(NT):
                    pba = self.pb[2 * (st % 2)]
                    pbb = self.pb[2 * (st % 2) + 1]
                    if eng == "tensor":
                        wt(E, "V1", b["V1"] + 8)
                        if st < 2:
                            wt(E, "SC", b["SC"])
                        else:
                            wt(E, "V1", b["V1"] + 8 + (st - 2) + 1)
                        for kk in range(8):
                            lhs = self.bigA[:, 4096 * kk + 128 * st:
                                            4096 * kk + 128 * st + 128]
                            E.matmul(pba[:, :], lhs,
                                     self.strm[:, 1024 * kk:1024 * kk + 512],
                                     start=(kk == 0), stop=(kk == 7))
                            ti = E.matmul(pbb[:, :], lhs,
                                          self.strm[:, 1024 * kk + 512:
                                                    1024 * kk + 1024],
                                          start=(kk == 0), stop=(kk == 7))
                        t(ti, V, "T1", 1)
                    else:
                        t(None, V, "T1", 1)
                    if eng == "vector":
                        wt(E, "T1", b["T1"] + st + 1)
                        dst = self.vaug[:, 1040 * st:1040 * st + 1040].rearrange(
                            "p (h j) -> p h j", j=65)[:, :, 0:64]
                        E.tensor_copy(dst[:, 0:8, :],
                                      pba[:, :].rearrange("p (h j) -> p h j", j=64))
                        ti = E.tensor_copy(dst[:, 8:16, :],
                                           pbb[:, :].rearrange("p (h j) -> p h j", j=64))
                        t(ti, V, "V1", 1)
                    else:
                        t(None, V, "V1", 1)

    def ph_kv(self, eng, E, V):
        t, wt = self.tick, self.wt
        B = dict(V)
        for g in range(4):
            b = dict(V)
            if eng == "sync":
                if g == 0:
                    wt(E, "S1", B["S1"])
                wt(E, "T1", B["T1"] + 32 * g)
                for cc in range(2):
                    t(E.dma_start(out=self.strm[:, 4096 * cc:4096 * cc + 4096],
                                  in_=self.kT_sp[2 * g + cc, :, :]),
                      V, KS4[cc], 16)
            else:
                for cc in range(2):
                    t(None, V, KS4[cc], 16)
            for tt in range(NT):
                zsl = tt % 2
                pz = (self.pb[0], self.pb[1]) if zsl == 0 else (self.pb[2], self.pb[3])
                et = self.ewA[:, 1024 * zsl:1024 * zsl + 1024]
                kf = self.ewB[:, 1024 * zsl:1024 * zsl + 1024]
                if eng == "tensor":
                    wt(E, "KA", B["KA"] + 16 * (g + 1))
                    wt(E, "KB", B["KB"] + 16 * (g + 1))
                    if tt >= 2:
                        wt(E, "V1", b["V1"] + (tt - 2) + 1)
                    for hh in range(4):
                        head = 4 * g + hh
                        p = head % 2
                        cc = (head // 2) - 2 * g
                        bank = pz[hh // 2]
                        col = 256 * (hh % 2)
                        lhs = self.strm[64 * p:64 * p + 64,
                                        4096 * cc + 128 * tt:4096 * cc + 128 * tt + 128]
                        E.matmul(bank[:, col:col + 256], lhs,
                                 self.wfb[64 * p:64 * p + 64, :],
                                 start=(hh % 2 == 0), stop=False)
                        ti = E.matmul(bank[:, col:col + 256], self.onesb[0:1, 0:128],
                                      self.onesb[0:1, 0:256],
                                      start=False, stop=(hh % 2 == 1))
                    t(ti, V, "T1", 1)
                else:
                    t(None, V, "T1", 1)
                if eng == "scalar":
                    wt(E, "T1", b["T1"] + tt + 1)
                    E.activation(et[:, 0:512], pz[0][:, :], AF.Exp,
                                 bias=self.cst[:, CNEG1:CNEG1 + 1],
                                 scale=self.cst[:, CONE:CONE + 1])
                    ti = E.activation(et[:, 512:1024], pz[1][:, :], AF.Exp,
                                      bias=self.cst[:, CNEG1:CNEG1 + 1],
                                      scale=self.cst[:, CONE:CONE + 1])
                    t(ti, V, "SC", 1)
                else:
                    t(None, V, "SC", 1)
                if eng == "vector":
                    wt(E, "SC", b["SC"] + tt + 1)
                    if tt >= 2:
                        wt(E, "T2", b["T2"] + (tt - 2) + 1)
                    E.scalar_tensor_tensor(kf[:, 0:512], et[:, 0:512], 1.0,
                                           pz[0][:, :], ALU.min, ALU.max)
                    ti = E.scalar_tensor_tensor(kf[:, 512:1024], et[:, 512:1024], 1.0,
                                                pz[1][:, :], ALU.min, ALU.max)
                    t(ti, V, "V1", 1)
                else:
                    t(None, V, "V1", 1)
                if eng == "tensor":
                    wt(E, "V1", b["V1"] + tt + 1)
                    if tt == 0:
                        wt(E, "V2", B["V2"] + g)
                    for hh in range(4):
                        head = 4 * g + hh
                        bank = self.pb[4 + hh // 2]
                        col = 256 * (hh % 2)
                        lhs = self.vaug[:, 1040 * tt + 65 * head:
                                        1040 * tt + 65 * head + 65]
                        ti = E.matmul(bank[0:65, col:col + 256], lhs,
                                      kf[:, 256 * hh:256 * hh + 256],
                                      start=(tt == 0 and hh % 2 == 0),
                                      stop=(tt == NT - 1 and hh % 2 == 1))
                    t(ti, V, "T2", 1)
                else:
                    t(None, V, "T2", 1)
            if eng == "vector":
                wt(E, "T2", b["T2"] + NT)
                E.tensor_copy(self.kvstg[0:65, 0:512], self.pb[4][0:65, :])
                E.tensor_copy(self.kvstg[0:65, 512:1024], self.pb[5][0:65, :])
                E.tensor_copy(self.kvstg_b[0:65, 0:512], self.kvstg[0:65, 0:512])
                ti = E.tensor_copy(self.kvstg_b[0:65, 512:1024],
                                   self.kvstg[0:65, 512:1024])
                t(ti, V, "V2", 1)
            else:
                t(None, V, "V2", 1)
            if eng == "tensor":
                wt(E, "V2", B["V2"] + g + 1)
                if g >= 1:
                    wt(E, "V3", B["V3"] + g)
                for hh in range(4):
                    for j in range(2):
                        ti = E.transpose(
                            self.pt2[:, 68 * (2 * hh + j):68 * (2 * hh + j) + 65],
                            self.kvstg_b[0:65, 256 * hh + 128 * j:
                                         256 * hh + 128 * j + 128],
                            self.idb[0:65, 0:65])
                t(ti, V, "T3", 1)
            else:
                t(None, V, "T3", 1)
            if eng == "vector":
                wt(E, "T3", B["T3"] + g + 1)
                ti = E.tensor_copy(
                    self.kvh[:, 520 * g:520 * g + 520].rearrange(
                        "p (c j) -> p c j", j=65),
                    self.pt2[:, 0:544].rearrange("p (c j) -> p c j", j=68)[:, :, 0:65])
                t(ti, V, "V3", 1)
            else:
                t(None, V, "V3", 1)

    @property
    def sel(self):
        return self.bigA[:, 8192:10240].bitcast(F32)

    def ph_ksbd(self, eng, E, V):
        t = self.tick
        if eng == "sync":
            t(E.dma_start(out=self.sel[0:16, :], in_=self.selmat[:, :]), V, "L2", 16)
        else:
            t(None, V, "L2", 16)
        if eng == "vector":
            self.wt(E, "L2", V["L2"])
            E.memset(self.ksbd[:, :], 0.0)
            E.drain()
            for c in range(32):
                h = c // 2
                ti = E.tensor_copy(self.ksbd[:, 16 * c + h:16 * c + h + 1],
                                   self.kvh[:, 65 * c + 64:65 * c + 65])
            t(ti, V, "V1", 1)
        else:
            t(None, V, "V1", 1)

    def ph_attn(self, eng, E, V):
        t, wt = self.tick, self.wt
        B = dict(V)
        for kk in range(8):
            wstage = self.ewA[:, 1024 * (kk % 2):1024 * (kk % 2) + 1024]
            ks = KS4[2 + kk % 2]
            if eng == "sync":
                if kk >= 2:
                    wt(E, "V1", B["V1"] + (kk - 2) + 1)
                t(E.dma_start(out=wstage, in_=self.woT[128 * kk:128 * kk + 128, :]),
                  V, ks, 16)
            else:
                t(None, V, ks, 16)
            if eng == "vector":
                wt(E, ks, B[ks] + 16 * (kk // 2 + 1))
                ti = E.tensor_copy(self.bigA[:, 1024 * kk:1024 * kk + 1024], wstage)
                t(ti, V, "V1", 1)
            else:
                t(None, V, "V1", 1)
        for u in range(NU):
            lb = B["L1"] + 128 + 192 * u
            tb = B["T1"] + 36 * u
            t2b = B["T2"] + u
            t3b = B["T3"] + 16 * u
            scb = B["SC"] + 48 * u
            v1b = B["V1"] + 8 + 32 * u
            v2b = B["V2"] + 17 * u
            v3b = B["V3"] + 4 * u
            qsl = u % 2
            qb = self.strm[:, 4096 * qsl:4096 * qsl + 4096]
            kq = KS4[u % 2]
            if eng == "sync":
                if u == 0:
                    wt(E, "S1", B["S1"])
                if u >= 2:
                    wt(E, "T1", B["T1"] + 36 * (u - 2) + 32)
                for kk in range(8):
                    t(E.dma_start(out=qb[:, 512 * kk:512 * kk + 512],
                                  in_=self.qT_sp[kk, :, 512 * u:512 * u + 512]),
                      V, kq, 16)
                for r in range(4):
                    gi_ = 4 * u + r
                    kx = "KE" if gi_ % 2 == 0 else "KF"
                    if gi_ >= 2:
                        wt(E, "V3", B["V3"] + (gi_ - 2) + 1)
                    xin = self.xt[:, 1024 * (r % 2):1024 * (r % 2) + 1024]
                    t(E.dma_start(out=xin,
                                  in_=self.x[512 * u + 128 * r:
                                             512 * u + 128 * r + 128, :]),
                      V, kx, 16)
            else:
                for kk in range(8):
                    t(None, V, kq, 16)
                for r in range(4):
                    gi_ = 4 * u + r
                    kx = "KE" if gi_ % 2 == 0 else "KF"
                    t(None, V, kx, 16)
            for c in range(32):
                h, j = c // 2, c % 2
                bank = self.pb[c % 4]
                et = self.ewA[:, 512 * (c % 4):512 * (c % 4) + 512]
                qf = self.vaug[:, 512 * c:512 * c + 512]
                if eng == "tensor":
                    if c == 0:
                        wt(E, kq, B[kq] + 128 * (u // 2 + 1))
                        if u >= 1:
                            wt(E, "V3", B["V3"] + 4 * u)
                    if c >= 4:
                        wt(E, "V1", v1b + (c - 4) + 1)
                    m, p = h // 2, h % 2
                    E.matmul(bank[:, :],
                             self.wfb[64 * p:64 * p + 64, 128 * j:128 * j + 128],
                             qb[64 * p:64 * p + 64, 512 * m:512 * m + 512],
                             start=True, stop=False)
                    ti = E.matmul(bank[:, :], self.onesb[0:1, 0:128],
                                  self.onesb[0:1, 0:512], start=False, stop=True)
                    t(ti, V, "T1", 1)
                else:
                    t(None, V, "T1", 1)
                if eng == "scalar":
                    wt(E, "T1", tb + c + 1)
                    if c >= 4:
                        wt(E, "V1", v1b + (c - 4) + 1)
                    ti = E.activation(et, bank[:, :], AF.Exp,
                                      bias=self.cst[:, CNEG1:CNEG1 + 1],
                                      scale=self.cst[:, CONE:CONE + 1])
                    t(ti, V, "SC", 1)
                else:
                    t(None, V, "SC", 1)
                if eng == "vector":
                    wt(E, "SC", scb + c + 1)
                    if u >= 1 and c == 0:
                        wt(E, "T2", B["T2"] + u)
                        wt(E, "T3", B["T3"] + 16 * u)
                    ti = E.scalar_tensor_tensor(qf, et, 1.0, bank[:, :],
                                                ALU.min, ALU.max)
                    t(ti, V, "V1", 1)
                else:
                    t(None, V, "V1", 1)
            if eng == "tensor":
                wt(E, "V1", v1b + 32)
                if u >= 1:
                    wt(E, "V2", B["V2"] + 17 * u)
                for c in range(32):
                    ti = E.matmul(self.pb[4][0:16, :], self.ksbd[:, 16 * c:16 * c + 16],
                                  self.vaug[:, 512 * c:512 * c + 512],
                                  start=(c == 0), stop=(c == 31))
                t(ti, V, "T2", 1)
            else:
                t(None, V, "T2", 1)
            if eng == "vector":
                wt(E, "T2", t2b + 1)
                E.tensor_scalar(self.kvstg[0:16, 512:1024], self.pb[4][0:16, :],
                                self.cst[0:16, CEPS6:CEPS6 + 1], None, ALU.add)
                E.drain()
                ti = E.reciprocal(self.kvstg[0:16, 0:512], self.kvstg[0:16, 512:1024])
                t(ti, V, "V2", 1)
            else:
                t(None, V, "V2", 1)
            for h in range(H):
                rb = self.pb[5]
                ob = self.pb[4] if h % 2 == 0 else self.pb[3]
                osb = self.ewA[:, 2048 + 512 * (h % 2):2048 + 512 * (h % 2) + 512]
                if eng == "tensor":
                    wt(E, "V2", v2b + 1 + h)
                    E.matmul(rb[0:64, :], self.sel[0:16, 64 * h:64 * h + 64],
                             self.kvstg[0:16, 0:512], start=True, stop=True)
                    for j in range(2):
                        c = 2 * h + j
                        ti = E.matmul(ob[0:64, :], self.kvh[:, 65 * c:65 * c + 64],
                                      self.vaug[:, 512 * c:512 * c + 512],
                                      start=(j == 0), stop=(j == 1))
                    t(ti, V, "T3", 1)
                else:
                    t(None, V, "T3", 1)
                if eng == "scalar":
                    wt(E, "T3", t3b + h + 1)
                    if h >= 2:
                        wt(E, "V2", v2b + 1 + h - 1)
                    ti = E.activation(osb[0:64, :], ob[0:64, :], AF.Copy, bias=0.0,
                                      scale=self.cst[0:64, CONE:CONE + 1])
                    t(ti, V, "SC", 1)
                else:
                    t(None, V, "SC", 1)
                if eng == "vector":
                    wt(E, "SC", scb + 32 + h + 1)
                    if u >= 1 and h == 0:
                        wt(E, "T1", B["T1"] + 36 * u)
                    po = 64 * (h % 2)
                    dst = self.ewB[:, 512 * (h // 2):512 * (h // 2) + 512]
                    ti = E.tensor_tensor(dst[po:po + 64, :], osb[0:64, :],
                                         rb[0:64, :], ALU.mult)
                    t(ti, V, "V2", 1)
                else:
                    t(None, V, "V2", 1)
            for r in range(4):
                gi = 4 * u + r
                pa = self.pb[2 * (r % 2)]
                pc = self.pb[2 * (r % 2) + 1]
                xin = self.xt[:, 1024 * (r % 2):1024 * (r % 2) + 1024]
                xp = self.xt[:, 2048 + 1024 * (r % 2):2048 + 1024 * (r % 2) + 1024]
                if eng == "tensor":
                    wt(E, "V2", v2b + 17)
                    if r >= 2:
                        wt(E, "V3", v3b + (r - 2) + 1)
                    for kk in range(8):
                        lhs = self.ewB[:, 512 * kk + 128 * r:512 * kk + 128 * r + 128]
                        E.matmul(pa[:, :], lhs, self.bigA[:, 1024 * kk:1024 * kk + 512],
                                 start=(kk == 0), stop=(kk == 7))
                        ti = E.matmul(pc[:, :], lhs,
                                      self.bigA[:, 1024 * kk + 512:1024 * kk + 1024],
                                      start=(kk == 0), stop=(kk == 7))
                    t(ti, V, "T1", 1)
                else:
                    t(None, V, "T1", 1)
                if eng == "vector":
                    wt(E, "T1", tb + 32 + r + 1)
                    kx = "KE" if gi % 2 == 0 else "KF"
                    wt(E, kx, B[kx] + 16 * (gi // 2 + 1))
                    if gi >= 2:
                        wt(E, "S1", B["S1"] + 16 * (gi - 1))
                    E.tensor_tensor(xp[:, 0:512], pa[:, :], xin[:, 0:512], ALU.add)
                    E.tensor_tensor(xp[:, 512:1024], pc[:, :], xin[:, 512:1024],
                                    ALU.add)
                    E.tensor_tensor(self.ewA[:, 0:512], xp[:, 0:512],
                                    xp[:, 0:512], ALU.mult)
                    E.tensor_tensor(self.ewA[:, 512:1024], xp[:, 512:1024],
                                    xp[:, 512:1024], ALU.mult)
                    E.tensor_reduce(self.sum2[:, gi:gi + 1], xp, AX, ALU.add)
                    ti = E.tensor_reduce(self.sq2[:, gi:gi + 1],
                                         self.ewA[:, 0:1024], AX, ALU.add)
                    t(ti, V, "V3", 1)
                else:
                    t(None, V, "V3", 1)
                if eng == "gpsimd":
                    wt(E, "V3", v3b + r + 1)
                    ti = E.dma_start(out=self.xp_sp[gi, :, :], in_=xp)
                    t(ti, V, "S1", 16)
                else:
                    t(None, V, "S1", 16)

    def ph_ln2_derive(self, eng, E, V):
        t, wt = self.tick, self.wt
        B = dict(V)
        if eng == "vector":
            E.tensor_scalar(self.st2m[:, :], self.sum2[:, :], 1.0 / D, None, ALU.mult)
            E.tensor_scalar(self.ewA[:, 0:32], self.sq2[:, :], 1.0 / D, None, ALU.mult)
            E.tensor_tensor(self.ewA[:, 32:64], self.st2m[:, :], self.st2m[:, :],
                            ALU.mult)
            E.drain()
            ti = E.tensor_tensor(self.ewA[:, 0:32], self.ewA[:, 0:32],
                                 self.ewA[:, 32:64], ALU.subtract)
            t(ti, V, "V1", 1)
        else:
            t(None, V, "V1", 1)
        if eng == "scalar":
            wt(E, "V1", V["V1"])
            ti = E.activation(self.ewA[:, 64:96], self.ewA[:, 0:32], AF.Sqrt,
                              bias=self.cst[:, CEPS5:CEPS5 + 1],
                              scale=self.cst[:, CONE:CONE + 1])
            t(ti, V, "SC", 1)
        else:
            t(None, V, "SC", 1)
        if eng == "vector":
            wt(E, "SC", V["SC"])
            ti = E.reciprocal(self.st2r[:, :], self.ewA[:, 64:96])
            t(ti, V, "V1", 1)
        else:
            t(None, V, "V1", 1)
        bswp = dict(V)
        for i in range(NT):
            xp = self.xt[:, 1024 * (i % 4):1024 * (i % 4) + 1024]
            ks = KS4[i % 4]
            if eng == "sync":
                if i == 0:
                    wt(E, "S1", bswp["S1"])
                wt(E, "V1", bswp["V1"])
                if i >= 4:
                    wt(E, "V1", bswp["V1"] + 2 * (i - 4) + 2)
                t(E.dma_start(out=xp, in_=self.xp_sp[i, :, :]), V, ks, 16)
            else:
                t(None, V, ks, 16)
            trash = self.ewA[:, 1024 * (i % 2):1024 * (i % 2) + 1024]
            if eng == "vector":
                wt(E, ks, bswp[ks] + 16 * (i // 4 + 1))
                E.tensor_scalar(trash, xp, self.st2m[:, i:i + 1],
                                self.st2r[:, i:i + 1], ALU.subtract, ALU.mult)
                if i >= 1:
                    pt_ = self.ewA[:, 1024 * ((i - 1) % 2):1024 * ((i - 1) % 2) + 1024]
                    E.tensor_reduce(self.dqt[:, (i - 1) % 2:(i - 1) % 2 + 1],
                                    pt_, AX, ALU.max, apply_absolute_value=True)
                if i >= 2:
                    ti = E.tensor_tensor(self.amx[:, 0:1], self.amx[:, 0:1],
                                         self.dqt[:, (i - 2) % 2:(i - 2) % 2 + 1],
                                         ALU.max)
                else:
                    ti = E.engine_nop()
                t(ti, V, "V1", 2)
            else:
                t(None, V, "V1", 2)
        if eng == "vector":
            last = self.ewA[:, 1024 * ((NT - 1) % 2):1024 * ((NT - 1) % 2) + 1024]
            E.tensor_reduce(self.dqt[:, (NT - 1) % 2:(NT - 1) % 2 + 1], last,
                            AX, ALU.max, apply_absolute_value=True)
            E.tensor_tensor(self.amx[:, 0:1], self.amx[:, 0:1],
                            self.dqt[:, (NT - 2) % 2:(NT - 2) % 2 + 1], ALU.max)
            E.drain()
            ti = E.tensor_tensor(self.amx[:, 0:1], self.amx[:, 0:1],
                                 self.dqt[:, (NT - 1) % 2:(NT - 1) % 2 + 1],
                                 ALU.max)
            self.tick(ti, V, "V1", 1)
        else:
            self.tick(None, V, "V1", 1)
        self._barrier(eng, E, V, self.amx[:, 0:1], 1, T_G1, None, None)
        self._scal_chain(eng, E, V, [
            ("ts2", T_G1, T_S1, 1.0 / QMAX, 1e-12),
            ("recip", T_S1, T_IS1),
        ])
        self._bcast(eng, E, V, [(T_IS1, SC_INV_S1), (T_S1, SC_S1)])
        if eng == "vector":
            wt(E, "V2", V["V2"])
            E.tensor_scalar(self.abuf[:, :], self.st2r[:, :],
                            self.scal[:, SC_INV_S1:SC_INV_S1 + 1], None, ALU.mult)
            E.drain()
            E.tensor_tensor(self.ewA[:, 0:32], self.st2m[:, :], self.abuf[:, :],
                            ALU.mult)
            E.drain()
            ti = E.tensor_scalar(self.bcbuf[:, :], self.ewA[:, 0:32], -1.0, C_MAG,
                                 ALU.mult, ALU.add)
            t(ti, V, "V1", 1)
        else:
            t(None, V, "V1", 1)

    def _barrier(self, eng, E, V, col_ap, ar_key, dst_tcol, col2_ap, dst_tcol2):
        t, wt = self.tick, self.wt
        if eng == "tensor":
            wt(E, "V1", V["V1"])
            E.transpose(self.pb[0][0:1, 0:128], col_ap, self.idf[:, :])
            c2 = col2_ap if col2_ap is not None else col_ap
            ti = E.transpose(self.pb[1][0:1, 0:128], c2, self.idf[:, :])
            t(ti, V, "T1", 1)
        else:
            t(None, V, "T1", 1)
        if eng == "vector":
            wt(E, "T1", V["T1"])
            E.tensor_reduce(self.kvstg[0:1, 0:1], self.pb[0][0:1, 0:128], AX, ALU.max)
            ti = E.tensor_reduce(self.kvstg[0:1, 1:2], self.pb[1][0:1, 0:128],
                                 AX, ALU.max)
            t(ti, V, "V2", 1)
        else:
            t(None, V, "V2", 1)
        if eng == "sync":
            wt(E, "V2", V["V2"])
            t(E.dma_start(out=self.ar_in[ar_key][:, :], in_=self.kvstg[0:1, 0:2]),
              V, "L2", 16)
        else:
            t(None, V, "L2", 16)
        if eng == "gpsimd":
            wt(E, "L2", V["L2"])
            ti = E.collective_compute("AllReduce", ALU.max,
                                      replica_groups=[list(range(NCORES))],
                                      ins=[self.ar_in[ar_key].ap().opt()],
                                      outs=[self.ar_out[ar_key].ap().opt()])
            t(ti, V, "CC", 1)
        else:
            t(None, V, "CC", 1)
        if eng == "sync":
            wt(E, "CC", V["CC"])
            t(E.dma_start(out=self.sct[0:1, dst_tcol:dst_tcol + 1],
                          in_=self.ar_out[ar_key][0:1, 0:1]), V, "L2", 16)
        else:
            t(None, V, "L2", 16)
        if dst_tcol2 is not None:
            if eng == "sync":
                t(E.dma_start(out=self.sct[0:1, dst_tcol2:dst_tcol2 + 1],
                              in_=self.ar_out[ar_key][0:1, 1:2]), V, "L2", 16)
            else:
                t(None, V, "L2", 16)

    def _scal_chain(self, eng, E, V, ops):
        t, wt = self.tick, self.wt
        if eng == "vector":
            wt(E, "L2", V["L2"])
            ti = None
            first = True
            for op in ops:
                if not first:
                    E.drain()
                first = False
                if op[0] == "ts2":
                    _, src, dst, m, mx = op
                    ti = E.tensor_scalar(self.sct[0:1, dst:dst + 1],
                                         self.sct[0:1, src:src + 1], m, mx,
                                         ALU.mult, ALU.max)
                elif op[0] == "recip":
                    ti = E.reciprocal(self.sct[0:1, op[2]:op[2] + 1],
                                      self.sct[0:1, op[1]:op[1] + 1])
                elif op[0] == "mul2":
                    _, a, bb, dst = op
                    ti = E.tensor_tensor(self.sct[0:1, dst:dst + 1],
                                         self.sct[0:1, a:a + 1],
                                         self.sct[0:1, bb:bb + 1], ALU.mult)
                elif op[0] == "mulc":
                    _, src, dst, cc = op
                    ti = E.tensor_scalar(self.sct[0:1, dst:dst + 1],
                                         self.sct[0:1, src:src + 1], cc, None,
                                         ALU.mult)
            t(ti, V, "V1", 1)
        else:
            t(None, V, "V1", 1)

    def _bcast(self, eng, E, V, pairs):
        t, wt = self.tick, self.wt
        if eng == "tensor":
            wt(E, "V1", V["V1"])
            for i, (tcol, scol) in enumerate(pairs):
                ti = E.matmul(self.pb[1][:, i:i + 1], self.onesf[0:1, 0:128],
                              self.sct[0:1, tcol:tcol + 1], start=True, stop=True)
            t(ti, V, "T2", 1)
        else:
            t(None, V, "T2", 1)
        if eng == "vector":
            wt(E, "T2", V["T2"])
            for i, (tcol, scol) in enumerate(pairs):
                ti = E.tensor_copy(self.scal[:, scol:scol + 1], self.pb[1][:, i:i + 1])
            t(ti, V, "V2", 1)
        else:
            t(None, V, "V2", 1)

    def ph_xq(self, eng, E, V):
        t, wt = self.tick, self.wt
        b = dict(V)
        for i in range(NT):
            xp = self.xt[:, 1024 * (i % 4):1024 * (i % 4) + 1024]
            y = self.ewA[:, 1024 * (i % 2):1024 * (i % 2) + 1024]
            xqb = self.ewB[:, 1024 * (i % 4):1024 * (i % 4) + 1024].bitcast(F16)
            ptt = self.pt if i % 2 == 0 else self.pt2
            ks = KS4[i % 4]
            if eng == "sync":
                if i == 0:
                    wt(E, "S1", b["S1"])
                if i >= 4:
                    wt(E, "V1", b["V1"] + 2 * (i - 4) + 2)
                t(E.dma_start(out=xp, in_=self.xp_sp[i, :, :]), V, ks, 16)
            else:
                t(None, V, ks, 16)
            if eng == "vector":
                wt(E, ks, b[ks] + 16 * (i // 4 + 1))
                E.tensor_scalar(y, xp, self.abuf[:, i:i + 1], self.bcbuf[:, i:i + 1],
                                ALU.mult, ALU.add)
                E.drain()
                if i >= 4:
                    wt(E, "T1", b["T1"] + (i - 4) + 1)
                ti = E.tensor_scalar(xqb, y, -C_MAG, self.scal[:, SC_S1:SC_S1 + 1],
                                     ALU.add, ALU.mult)
                t(ti, V, "V1", 2)
            else:
                t(None, V, "V1", 2)
            if eng == "tensor":
                wt(E, "V1", b["V1"] + 2 * i + 2)
                if i >= 2:
                    wt(E, "V2", b["V2"] + (i - 2) + 1)
                for kk in range(8):
                    ti = E.transpose(ptt[:, 128 * kk:128 * kk + 128].bitcast(F16),
                                     xqb[:, 128 * kk:128 * kk + 128], self.idh[:, :])
                t(ti, V, "T1", 1)
            else:
                t(None, V, "T1", 1)
            if eng == "vector":
                wt(E, "T1", b["T1"] + i + 1)
                ti = E.tensor_copy(
                    self.bigA[:, :].bitcast(F16).rearrange("p (k s) -> p k s", k=8)[
                        :, :, 128 * i:128 * i + 128],
                    ptt[:, :].bitcast(F16).rearrange("p (k s) -> p k s", k=8))
                t(ti, V, "V2", 1)
            else:
                t(None, V, "V2", 1)

    def ph_upgate(self, eng, E, V, which):
        t, wt = self.tick, self.wt
        ag_out = self.agu_out if which == "up" else self.agg_out
        spill = self.up_sp if which == "up" else self.gt_sp
        amxcol = 1 if which == "up" else 2
        b = dict(V)
        if eng == "sync":
            wt(E, "CC", 3)
            for k in range(8):
                src = ag_out.rearrange("(c r) j -> c r j", c=NCORES)[
                    :, 128 * k:128 * k + 128, :].transpose([1, 0, 2])
                t(E.dma_start(out=self.vaug[:, 3072 * k:3072 * k + 3072].bitcast(
                    F16).rearrange("p (c j) -> p c j", c=NCORES), in_=src),
                  V, "L1", 16)
        else:
            for k in range(8):
                t(None, V, "L1", 16)
        groups = [(0, 1, 2), (3, 4, 5), (6, 7)]
        evc = []
        tot = 0
        for ul in groups:
            tot += len(ul)
            evc.append(tot)
        q = 0
        gidx = 0
        for m in range(IC):
            for ulist in groups:
                banks = self.pb[0:3] if gidx % 2 == 0 else self.pb[3:6]
                if eng == "tensor":
                    if gidx == 0:
                        wt(E, "L1", b["L1"] + 128)
                    if gidx >= 2:
                        done = evc[(gidx - 2) % 3] + 8 * ((gidx - 2) // 3)
                        wt(E, "SC", b["SC"] + 8 * ((gidx - 2) // 3) +
                           evc[(gidx - 2) % 3])
                    for k in range(8):
                        lhs = self.vaug[:, 3072 * k + 128 * m:
                                        3072 * k + 128 * m + 128].bitcast(F16)
                        for ui, u in enumerate(ulist):
                            ti = E.matmul(banks[ui][:, :], lhs,
                                          self.bigA[:, 4096 * k + 512 * u:
                                                    4096 * k + 512 * u + 512].bitcast(F16),
                                          start=(k == 0), stop=(k == 7))
                    t(ti, V, "T1", 1)
                else:
                    t(None, V, "T1", 1)
                for ui, u in enumerate(ulist):
                    sslot = q % 8
                    stgt = self.stg[:, 512 * sslot:512 * sslot + 512]
                    if eng == "scalar":
                        wt(E, "T1", b["T1"] + gidx + 1)
                        if q >= 8:
                            wt(E, "S1", b["S1"] + 16 * (q - 8 + 1))
                        ti = E.activation(stgt, banks[ui][:, :], AF.Copy, bias=0.0,
                                          scale=self.cst[:, CONE:CONE + 1])
                        t(ti, V, "SC", 1)
                    else:
                        t(None, V, "SC", 1)
                    if eng == "vector":
                        wt(E, "SC", b["SC"] + q + 1)
                        if q >= 2:
                            E.tensor_tensor(self.amx[:, amxcol:amxcol + 1],
                                            self.amx[:, amxcol:amxcol + 1],
                                            self.dqt[:, (q - 2) % 3:(q - 2) % 3 + 1],
                                            ALU.max)
                        ti = E.tensor_reduce(self.dqt[:, q % 3:q % 3 + 1], stgt,
                                             AX, ALU.max, apply_absolute_value=True)
                        t(ti, V, "V1", 2)
                    else:
                        t(None, V, "V1", 2)
                    if eng == "gpsimd":
                        wt(E, "SC", b["SC"] + q + 1)
                        ti = E.dma_start(out=spill[m, u, :, :], in_=stgt)
                        t(ti, V, "S1", 16)
                    else:
                        t(None, V, "S1", 16)
                    q += 1
                gidx += 1
        if eng == "vector":
            E.drain()
            E.tensor_tensor(self.amx[:, amxcol:amxcol + 1],
                            self.amx[:, amxcol:amxcol + 1],
                            self.dqt[:, (q - 2) % 3:(q - 2) % 3 + 1], ALU.max)
            E.drain()
            ti = E.tensor_tensor(self.amx[:, amxcol:amxcol + 1],
                                 self.amx[:, amxcol:amxcol + 1],
                                 self.dqt[:, (q - 1) % 3:(q - 1) % 3 + 1], ALU.max)
            t(ti, V, "V1", 1)
        else:
            t(None, V, "V1", 1)

    def ph_b2(self, eng, E, V):
        self._barrier(eng, E, V, self.amx[:, 1:2], 2, T_GU, self.amx[:, 2:3], T_GG)
        self._scal_chain(eng, E, V, [
            ("ts2", T_GU, T_S2U, 1.0 / QMAX, 1e-12),
            ("recip", T_S2U, T_IS2U),
            ("ts2", T_GG, T_S2G, 1.0 / QMAX, 1e-12),
            ("recip", T_S2G, T_IS2G),
            ("mulc", T_S2G, T_CG, 1.0 / 65536.0),
        ])
        self._bcast(eng, E, V, [(T_IS2G, SC_INV_S2G), (T_S2G, SC_S2G),
                                (T_CG, SC_CG), (T_IS2U, SC_INV_S2U)])

    def ph_p2(self, eng, E, V):
        t, wt = self.tick, self.wt
        b = dict(V)
        n = 0
        for m in range(IC):
            for u in range(NU):
                g16 = self.stg[:, 512 * (n % 4):512 * (n % 4) + 512]
                m16 = self.stg[:, 2048 + 512 * (n % 4):2048 + 512 * (n % 4) + 512]
                yg = self.xt[:, 512 * (n % 4):512 * (n % 4) + 512]
                tg = self.xt[:, 2048 + 512 * (n % 4):2048 + 512 * (n % 4) + 512]
                sg = self.ewA[:, 2048 + 512 * (n % 2):2048 + 512 * (n % 2) + 512]
                ys = self.ewA[:, 512 * (n % 2):512 * (n % 2) + 512]
                ts5 = self.ewA[:, 1024 + 512 * (n % 2):1024 + 512 * (n % 2) + 512]
                ks = KS4[n % 4]
                if eng == "sync":
                    if n == 0:
                        wt(E, "S1", b["S1"])
                    if n >= 4:
                        wt(E, "SC", b["SC"] + 3 * (n - 4) + 1)
                    t(E.dma_start(out=g16, in_=self.gt_sp[m, u, :, :]), V, ks, 16)
                else:
                    t(None, V, ks, 16)
                if eng == "scalar":
                    wt(E, ks, b[ks] + 16 * (n // 4 + 1))
                    if n >= 4:
                        wt(E, "V1", b["V1"] + 4 * (n - 4) + 1)
                    ti = E.activation(yg, g16, AF.Copy, bias=C_MAG,
                                      scale=self.scal[:, SC_INV_S2G:SC_INV_S2G + 1])
                    t(ti, V, "SC", 1)
                else:
                    t(None, V, "SC", 1)
                if eng == "vector":
                    wt(E, "SC", b["SC"] + 3 * n + 1)
                    ti = E.tensor_scalar(tg, yg, -C_MAG, None, ALU.add)
                    t(ti, V, "V1", 1)
                else:
                    t(None, V, "V1", 1)
                if eng == "scalar":
                    wt(E, "V1", b["V1"] + 4 * n + 1)
                    if n >= 2:
                        wt(E, "V1", b["V1"] + 4 * (n - 2) + 2)
                    E.activation(sg, tg, AF.Sigmoid,
                                 bias=self.cst[:, CZERO:CZERO + 1],
                                 scale=self.scal[:, SC_S2G:SC_S2G + 1])
                    ti = E.activation(ys, sg, AF.Copy, bias=C_MAG,
                                      scale=self.cst[:, C65536:C65536 + 1])
                    t(ti, V, "SC", 2)
                else:
                    t(None, V, "SC", 2)
                if eng == "vector":
                    wt(E, "SC", b["SC"] + 3 * n + 3)
                    if n >= 4:
                        wt(E, "S1", b["S1"] + 16 * (n - 4 + 1))
                    E.tensor_scalar(ts5, ys, -C_MAG, 65535.0, ALU.add, ALU.min)
                    if n >= 1:
                        pm16 = self.stg[:, 2048 + 512 * ((n - 1) % 4):
                                        2048 + 512 * ((n - 1) % 4) + 512]
                        E.tensor_reduce(self.dqt[:, (n - 1) % 3:(n - 1) % 3 + 1],
                                        pm16, AX, ALU.max,
                                        apply_absolute_value=True)
                    E.scalar_tensor_tensor(m16, tg, self.scal[:, SC_CG:SC_CG + 1],
                                           ts5, ALU.mult, ALU.mult)
                    if n >= 2:
                        ti = E.tensor_tensor(self.amx[:, 3:4], self.amx[:, 3:4],
                                             self.dqt[:, (n - 2) % 3:(n - 2) % 3 + 1],
                                             ALU.max)
                    else:
                        ti = E.engine_nop()
                    t(ti, V, "V1", 3)
                else:
                    t(None, V, "V1", 3)
                if eng == "gpsimd":
                    wt(E, "V1", b["V1"] + 4 * n + 4)
                    ti = E.dma_start(out=self.m1_sp[m, u, :, :], in_=m16)
                    t(ti, V, "S1", 16)
                else:
                    t(None, V, "S1", 16)
                n += 1
        if eng == "vector":
            lm16 = self.stg[:, 2048 + 512 * ((n - 1) % 4):
                            2048 + 512 * ((n - 1) % 4) + 512]
            E.tensor_reduce(self.dqt[:, (n - 1) % 3:(n - 1) % 3 + 1], lm16,
                            AX, ALU.max, apply_absolute_value=True)
            E.tensor_tensor(self.amx[:, 3:4], self.amx[:, 3:4],
                            self.dqt[:, (n - 2) % 3:(n - 2) % 3 + 1], ALU.max)
            E.drain()
            ti = E.tensor_tensor(self.amx[:, 3:4], self.amx[:, 3:4],
                                 self.dqt[:, (n - 1) % 3:(n - 1) % 3 + 1], ALU.max)
            t(ti, V, "V1", 1)
        else:
            t(None, V, "V1", 1)

    def ph_b4(self, eng, E, V):
        self._barrier(eng, E, V, self.amx[:, 3:4], 4, T_G4, None, None)
        self._scal_chain(eng, E, V, [
            ("ts2", T_G4, T_S4, 1.0 / QMAX, 1e-12),
            ("recip", T_S4, T_IS4),
            ("mul2", T_S4, T_S2U, T_C45),
        ])
        self._bcast(eng, E, V, [(T_IS4, SC_INV_S4), (T_C45, SC_C45)])

    def ph_p3(self, eng, E, V):
        t, wt = self.tick, self.wt
        b = dict(V)
        if eng == "sync":
            wt(E, "CC", 3)
            for i in range(IC):
                src = self.agd_out.rearrange("(c r) j -> c r j", c=NCORES)[
                    :, 128 * i:128 * i + 128, :].transpose([1, 0, 2])
                t(E.dma_start(out=self.bigA[:, 1024 * i:1024 * i + 1024].bitcast(
                    F16).rearrange("p (c j) -> p c j", c=NCORES), in_=src),
                  V, "L2", 16)
        else:
            for i in range(IC):
                t(None, V, "L2", 16)
        n = 0
        for m in range(IC):
            for u in range(NU):
                a16 = self.stg[:, 512 * (n % 4):512 * (n % 4) + 512]
                b16 = self.ewB[:, 512 * (n % 4):512 * (n % 4) + 512].bitcast(F16)
                m16 = self.stg[:, 2048 + 512 * (n % 4):2048 + 512 * (n % 4) + 512]
                y2 = self.xt[:, 512 * (n % 4):512 * (n % 4) + 512]
                yu = self.xt[:, 2048 + 512 * (n % 4):2048 + 512 * (n % 4) + 512]
                tu = self.ewA[:, 512 * (n % 4):512 * (n % 4) + 512]
                ks = KS4[n % 4]
                if eng == "sync":
                    if n == 0:
                        wt(E, "S1", b["S1"])
                    if n >= 4:
                        wt(E, "SC", b["SC"] + (n - 4) + 1)
                    t(E.dma_start(out=a16, in_=self.m1_sp[m, u, :, :]), V, ks, 16)
                    t(E.dma_start(out=b16, in_=self.up_sp[m, u, :, :]), V, ks, 16)
                else:
                    t(None, V, ks, 16)
                    t(None, V, ks, 16)
                if eng == "scalar":
                    wt(E, ks, b[ks] + 32 * (n // 4 + 1))
                    if n >= 4:
                        wt(E, "V1", b["V1"] + 3 * (n - 4) + 2)
                    E.activation(y2, a16, AF.Copy, bias=C_MAG,
                                 scale=self.scal[:, SC_INV_S4:SC_INV_S4 + 1])
                    ti = E.activation(yu, b16, AF.Copy, bias=C_MAG,
                                      scale=self.scal[:, SC_INV_S2U:SC_INV_S2U + 1])
                    t(ti, V, "SC", 1)
                else:
                    t(None, V, "SC", 1)
                if eng == "vector":
                    wt(E, "SC", b["SC"] + n + 1)
                    if n >= 4:
                        wt(E, "S1", b["S1"] + 16 * (n - 4 + 1))
                    E.tensor_scalar(tu, yu, -C_MAG, self.scal[:, SC_C45:SC_C45 + 1],
                                    ALU.add, ALU.mult)
                    if n >= 1:
                        pm16 = self.stg[:, 2048 + 512 * ((n - 1) % 4):
                                        2048 + 512 * ((n - 1) % 4) + 512]
                        E.tensor_reduce(self.dqt[:, (n - 1) % 3:(n - 1) % 3 + 1],
                                        pm16, AX, ALU.max,
                                        apply_absolute_value=True)
                    E.scalar_tensor_tensor(m16, y2, -C_MAG, tu, ALU.add, ALU.mult)
                    if n >= 2:
                        ti = E.tensor_tensor(self.amx[:, 4:5], self.amx[:, 4:5],
                                             self.dqt[:, (n - 2) % 3:(n - 2) % 3 + 1],
                                             ALU.max)
                    else:
                        ti = E.engine_nop()
                    t(ti, V, "V1", 3)
                else:
                    t(None, V, "V1", 3)
                if eng == "gpsimd":
                    wt(E, "V1", b["V1"] + 3 * n + 3)
                    ti = E.dma_start(out=self.m2_sp[m, u, :, :], in_=m16)
                    t(ti, V, "S1", 16)
                else:
                    t(None, V, "S1", 16)
                n += 1
        if eng == "vector":
            lm16 = self.stg[:, 2048 + 512 * ((n - 1) % 4):
                            2048 + 512 * ((n - 1) % 4) + 512]
            E.tensor_reduce(self.dqt[:, (n - 1) % 3:(n - 1) % 3 + 1], lm16,
                            AX, ALU.max, apply_absolute_value=True)
            E.tensor_tensor(self.amx[:, 4:5], self.amx[:, 4:5],
                            self.dqt[:, (n - 2) % 3:(n - 2) % 3 + 1], ALU.max)
            E.drain()
            ti = E.tensor_tensor(self.amx[:, 4:5], self.amx[:, 4:5],
                                 self.dqt[:, (n - 1) % 3:(n - 1) % 3 + 1], ALU.max)
            t(ti, V, "V1", 1)
        else:
            t(None, V, "V1", 1)

    def ph_b5(self, eng, E, V):
        self._barrier(eng, E, V, self.amx[:, 4:5], 5, T_G5, None, None)
        self._scal_chain(eng, E, V, [
            ("ts2", T_G5, T_S5V, 1.0 / QMAX, 1e-12),
            ("recip", T_S5V, T_IS5),
        ])
        self._bcast(eng, E, V, [(T_IS5, SC_INV_S5), (T_S5V, SC_S5)])

    def ph_down(self, eng, E, V):
        t, wt = self.tick, self.wt
        b = dict(V)
        for u in range(NU):
            lb = b["L1"] + 28 * 16 * u
            osl = u % 2
            for i2 in range(IC):
                q = IC * u + i2
                m16 = self.stg[:, 512 * (i2 % 4):512 * (i2 % 4) + 512]
                yo = self.ewA[:, 512 * (i2 % 4):512 * (i2 % 4) + 512]
                oq = self.vaug[:, 12288 * osl + 512 * i2:
                               12288 * osl + 512 * i2 + 512].bitcast(F16)
                ks = KS4[q % 4]
                if eng == "sync":
                    if q == 0:
                        wt(E, "S1", b["S1"])
                    if q >= 4:
                        wt(E, "SC", b["SC"] + (q - 4) + 1)
                    if i2 == 0 and u >= 2:
                        wt(E, "T1", b["T1"] + 4 * (u - 1))
                    t(E.dma_start(out=m16, in_=self.m2_sp[i2, u, :, :]), V, ks, 16)
                else:
                    t(None, V, ks, 16)
                if eng == "scalar":
                    wt(E, ks, b[ks] + 16 * (q // 4 + 1))
                    if q >= 4:
                        wt(E, "V1", b["V1"] + (q - 4) + 1)
                    ti = E.activation(yo, m16, AF.Copy, bias=C_MAG,
                                      scale=self.scal[:, SC_INV_S5:SC_INV_S5 + 1])
                    t(ti, V, "SC", 1)
                else:
                    t(None, V, "SC", 1)
                if eng == "vector":
                    wt(E, "SC", b["SC"] + q + 1)
                    ti = E.tensor_scalar(oq, yo, -C_MAG, None, ALU.add)
                    t(ti, V, "V1", 1)
                else:
                    t(None, V, "V1", 1)
            if eng == "sync":
                for r in range(4):
                    gi_ = 4 * u + r
                    kx = "KE" if gi_ % 2 == 0 else "KF"
                    if gi_ >= 2:
                        wt(E, "V2", b["V2"] + (gi_ - 2) + 1)
                    xin = self.xt[:, 1024 * (r % 2):1024 * (r % 2) + 1024]
                    t(E.dma_start(out=xin, in_=self.xp_sp[4 * u + r, :, :]),
                      V, kx, 16)
            else:
                for r in range(4):
                    gi_ = 4 * u + r
                    kx = "KE" if gi_ % 2 == 0 else "KF"
                    t(None, V, kx, 16)
            for r in range(4):
                gi = 4 * u + r
                pa = self.pb[2 * (r % 2)]
                pc = self.pb[2 * (r % 2) + 1]
                xin = self.xt[:, 1024 * (r % 2):1024 * (r % 2) + 1024]
                ot = self.xt[:, 2048 + 1024 * (r % 2):2048 + 1024 * (r % 2) + 1024]
                if eng == "tensor":
                    wt(E, "V1", b["V1"] + IC * (u + 1))
                    wt(E, "L2", b["L2"])
                    if gi >= 2:
                        wt(E, "V2", b["V2"] + (gi - 2) + 1)
                    for i2 in range(IC):
                        lhs = self.vaug[:, 12288 * osl + 512 * i2 + 128 * r:
                                        12288 * osl + 512 * i2 + 128 * r + 128]
                        E.matmul(pa[:, :], lhs.bitcast(F16),
                                 self.bigA[:, 1024 * i2:1024 * i2 + 512].bitcast(F16),
                                 start=(i2 == 0), stop=(i2 == IC - 1))
                        ti = E.matmul(pc[:, :], lhs.bitcast(F16),
                                      self.bigA[:, 1024 * i2 + 512:
                                                1024 * i2 + 1024].bitcast(F16),
                                      start=(i2 == 0), stop=(i2 == IC - 1))
                    t(ti, V, "T1", 1)
                else:
                    t(None, V, "T1", 1)
                if eng == "vector":
                    wt(E, "T1", b["T1"] + 4 * u + r + 1)
                    kx = "KE" if gi % 2 == 0 else "KF"
                    wt(E, kx, b[kx] + 16 * (gi // 2 + 1))
                    if gi >= 2:
                        wt(E, "S1", b["S1"] + 16 * (gi - 1))
                    E.scalar_tensor_tensor(ot[:, 0:512], pa[:, :],
                                           self.scal[:, SC_S5:SC_S5 + 1],
                                           xin[:, 0:512], ALU.mult, ALU.add)
                    ti = E.scalar_tensor_tensor(ot[:, 512:1024], pc[:, :],
                                                self.scal[:, SC_S5:SC_S5 + 1],
                                                xin[:, 512:1024], ALU.mult, ALU.add)
                    t(ti, V, "V2", 1)
                else:
                    t(None, V, "V2", 1)
                if eng == "gpsimd":
                    wt(E, "V2", b["V2"] + gi + 1)
                    ti = E.dma_start(
                        out=self.out[512 * u + 128 * r:512 * u + 128 * r + 128, :],
                        in_=ot)
                    t(ti, V, "S1", 16)
                else:
                    t(None, V, "S1", 16)

    def ph_fin(self, eng, E, V):
        t = self.tick
        if "oq_dbg" in self.dbg:
            if eng == "gpsimd":
                ti = E.dma_start(out=self.oq_dbg[:, :],
                                 in_=self.vaug[:, 12288:24576].bitcast(F16))
                t(ti, V, "S1", 16)
            else:
                t(None, V, "S1", 16)
        if "sc_dbg" in self.dbg:
            if eng == "gpsimd":
                t(E.dma_start(out=self.sc_dbg[:, 0:16], in_=self.scal[:, :]), V, "S1", 16)
                t(E.dma_start(out=self.sc_dbg[0:1, 16:40], in_=self.sct[:, :]), V, "S1", 16)
                ti = E.dma_start(out=self.sc_dbg[:, 40:48], in_=self.amx[:, :])
                t(ti, V, "S1", 16)
            else:
                for _ in range(3):
                    t(None, V, "S1", 16)
        if "agu_out" in self.dbg:
            if eng == "gpsimd":
                self.wt(E, "CC", 3)
                t(E.dma_start(out=self.agu_dbg[:, :], in_=self.agu_out[:, :]),
                  V, "S1", 16)
                t(E.dma_start(out=self.agi_dbg[:, :], in_=self.agu_in[:, :]),
                  V, "S1", 16)
            else:
                t(None, V, "S1", 16)
                t(None, V, "S1", 16)
        if eng == "gpsimd":
            E.wait_ge(self.sems["S1"], V["S1"])

    PHASES = ["init", "dequant", "stats1", "norm1", "qkv", "kv", "ksbd", "attn",
              "ln2", "xq", "up", "gt", "b2", "p2", "b4", "p3", "b5", "down"]

    def build(self, upto=None):
        seq = [
            ("init", self.ph_init, ()),
            ("dequant", self.ph_dequant, ()),
            ("stats1", self.ph_stats1, ()),
            ("norm1", self.ph_norm1, ()),
            ("qkv", self.ph_qkv, ()),
            ("kv", self.ph_kv, ()),
            ("ksbd", self.ph_ksbd, ()),
            ("attn", self.ph_attn, ()),
            ("ln2", self.ph_ln2_derive, ()),
            ("xq", self.ph_xq, ()),
            ("up", self.ph_upgate, ("up",)),
            ("gt", self.ph_upgate, ("gt",)),
            ("b2", self.ph_b2, ()),
            ("p2", self.ph_p2, ()),
            ("b4", self.ph_b4, ()),
            ("p3", self.ph_p3, ()),
            ("b5", self.ph_b5, ()),
            ("down", self.ph_down, ()),
        ]
        if "xn_dbg" in self.dbg:
            i = [n for n, _, _ in seq].index("norm1") + 1
            seq.insert(i, ("xndbg", self.ph_xndbg, ()))
        with self.es:
            for name, fn, args in seq:
                self.phase(fn, *args)
                if upto is not None and name == upto:
                    break
            self.phase(self.ph_fin)
        return self.nc


def make_in_maps(L0_attn_hidden_states, L0_attn_Wq, L0_attn_Wk, L0_attn_Wv,
                 L0_attn_Wo, L0_attn_Wf, L0_ffn_w_gate, L0_ffn_w_up,
                 L0_ffn_w_down):
    x = np.ascontiguousarray(np.asarray(L0_attn_hidden_states, dtype=np.float32))
    wqT = np.ascontiguousarray(np.asarray(L0_attn_Wq, np.float32).T)
    wkT = np.ascontiguousarray(np.asarray(L0_attn_Wk, np.float32).T)
    wvT = np.ascontiguousarray(np.asarray(L0_attn_Wv, np.float32).T)
    woT = np.ascontiguousarray(np.asarray(L0_attn_Wo, np.float32).T)
    wfT = np.ascontiguousarray(np.asarray(L0_attn_Wf, np.float32).T)
    wup = np.ascontiguousarray(np.asarray(L0_ffn_w_up, np.float32))
    wgt = np.ascontiguousarray(np.asarray(L0_ffn_w_gate, np.float32))
    wdn = np.ascontiguousarray(np.asarray(L0_ffn_w_down, np.float32))
    identbf = np.eye(128, dtype=ml_dtypes.bfloat16)
    identf32 = np.eye(128, dtype=np.float32)
    selmat = np.kron(np.eye(16, dtype=np.float32),
                     np.ones((1, 64), dtype=np.float32))
    identf16 = np.eye(128, dtype=np.float16)
    in_maps = []
    for c in range(NCORES):
        in_maps.append({
            "x": np.ascontiguousarray(x[c]),
            "wqT": wqT, "wkT": wkT, "wvT": wvT, "woT": woT, "wfT": wfT,
            "wup_nat": np.ascontiguousarray(wup[384 * c:384 * c + 384, :]),
            "wgt_nat": np.ascontiguousarray(wgt[384 * c:384 * c + 384, :]),
            "wdn_nat": np.ascontiguousarray(wdn[128 * c:128 * c + 128, :]),
            "identbf": identbf, "identf32": identf32, "selmat": selmat,
            "identf16": identf16,
        })
    return in_maps


def kernel(**inputs):
    in_maps = make_in_maps(**inputs)
    nc = Build().build()
    res = run_bass_kernel_spmd(nc, in_maps, core_ids=list(range(NCORES)))
    out = np.stack([res.results[c]["out"] for c in range(NCORES)], axis=0)
    return out.astype(np.float32)


# version 02
# speedup vs baseline: 1.1940x; 1.1940x over previous
import numpy as np
import ml_dtypes
from contextlib import ExitStack

import concourse.bass as bass
import concourse.mybir as mybir
from concourse.bass_utils import run_bass_kernel_spmd

F32 = mybir.dt.float32
BF16 = mybir.dt.bfloat16
F16 = mybir.dt.float16
AF = mybir.ActivationFunctionType
ALU = mybir.AluOpType
AX = mybir.AxisListType.X

NCORES = 8
S, D, H, HD, FD, I = 4096, 1024, 16, 64, 256, 3072
NT = S // 128
NU = S // 512
IC = I // 128
C_MAG = 12582912.0
QMAX = 32767.0
ENGS = ("sync", "tensor", "scalar", "vector", "gpsimd")

CZERO, CONE, CNEG1, CC_, CNC, CEPS6, CEPS5, C65536 = range(8)
SC_INV_S1, SC_S1, SC_INV_S2G, SC_S2G, SC_CG, SC_INV_S2U, SC_C45, \
    SC_INV_S4, SC_INV_S5, SC_S5 = range(10)
T_G1, T_S1, T_IS1, T_GU, T_GG, T_S2U, T_IS2U, T_S2G, T_IS2G, T_CG, \
    T_G4, T_S4, T_IS4, T_C45, T_G5, T_S5V, T_IS5 = range(17)

SEM_NAMES = ("L1", "L2", "S1", "T1", "T2", "T3", "SC", "V1", "V2", "V3", "CC",
             "KA", "KB", "KC", "KD", "KE", "KF")
KS4 = ("KA", "KB", "KC", "KD")


class Build:
    def __init__(self, dbg=()):
        nc = bass.Bass("TRN2", target_bir_lowering=False)
        self.nc = nc
        self.es = ExitStack()
        self.dbg = set(dbg)
        es = self.es

        dt = nc.dram_tensor
        self.x = dt("x", [S, D], F32, kind="ExternalInput")
        self.wqT = dt("wqT", [D, D], F32, kind="ExternalInput")
        self.wkT = dt("wkT", [D, D], F32, kind="ExternalInput")
        self.wvT = dt("wvT", [D, D], F32, kind="ExternalInput")
        self.woT = dt("woT", [D, D], F32, kind="ExternalInput")
        self.wfT = dt("wfT", [HD, FD], F32, kind="ExternalInput")
        self.wup_nat = dt("wup_nat", [I // 8, D], F32, kind="ExternalInput")
        self.wgt_nat = dt("wgt_nat", [I // 8, D], F32, kind="ExternalInput")
        self.wdn_nat = dt("wdn_nat", [D // 8, I], F32, kind="ExternalInput")
        self.identbf = dt("identbf", [128, 128], BF16, kind="ExternalInput")
        self.identf32 = dt("identf32", [128, 128], F32, kind="ExternalInput")
        self.selmat = dt("selmat", [16, 1024], F32, kind="ExternalInput")
        self.identf16 = dt("identf16", [128, 128], F16, kind="ExternalInput")
        self.out = dt("out", [S, D], F32, kind="ExternalOutput")

        def scratch(name, shape, dtp):
            kind = "ExternalOutput" if name in self.dbg else "Internal"
            return dt(name, shape, dtp, kind=kind)
        self.qT_sp = scratch("qT_sp", [8, 128, S], BF16)
        self.kT_sp = scratch("kT_sp", [8, 128, S], BF16)
        self.xp_sp = scratch("xp_sp", [NT, 128, D], F32)
        self.up_sp = scratch("up_sp", [IC, NU, 128, 512], F16)
        self.gt_sp = scratch("gt_sp", [IC, NU, 128, 512], F16)
        self.m1_sp = scratch("m1_sp", [IC, NU, 128, 512], F16)
        self.m2_sp = scratch("m2_sp", [IC, NU, 128, 512], F16)
        self.agu_in = dt("agu_in", [D, I // 8], F16)
        self.agu_out = dt("agu_out", [NCORES * D, I // 8], F16)
        if "agu_out" in self.dbg:
            self.agu_dbg = dt("agu_dbg", [NCORES * D, I // 8], F16,
                              kind="ExternalOutput")
            self.agi_dbg = dt("agi_dbg", [D, I // 8], BF16,
                              kind="ExternalOutput")
        self.agg_in = dt("agg_in", [D, I // 8], F16)
        self.agg_out = dt("agg_out", [NCORES * D, I // 8], F16)
        self.agd_in = dt("agd_in", [I, D // 8], F16)
        self.agd_out = dt("agd_out", [NCORES * I, D // 8], F16)
        self.ar_in = {k: dt(f"ar{k}_in", [1, 2], F32) for k in (1, 2, 4, 5)}
        if "xn_dbg" in self.dbg:
            self.xn_dbg = dt("xn_dbg", [128, 32768], BF16, kind="ExternalOutput")
            self.st_dbg = dt("st_dbg", [128, 64], F32, kind="ExternalOutput")
        if "sc_dbg" in self.dbg:
            self.sc_dbg = dt("sc_dbg", [128, 48], F32, kind="ExternalOutput")
        if "oq_dbg" in self.dbg:
            self.oq_dbg = dt("oq_dbg", [128, 12288], F16, kind="ExternalOutput")
        self.ar_out = {k: dt(f"ar{k}_out", [1, 2], F32) for k in (1, 2, 4, 5)}

        def sb(name, shape, dtp):
            return es.enter_context(nc.sbuf_tensor(name, shape, dtp))
        self.bigA = sb("bigA", [128, 32768], BF16)
        self.vaug = sb("vaug", [128, 33280], BF16)
        self.strm = sb("strm", [128, 8192], BF16)
        self.xt = sb("xt", [128, 4096], F32)
        self.ewA = sb("ewA", [128, 3072], F32)
        self.ewB = sb("ewB", [128, 4096], BF16)
        self.stg = sb("stg", [128, 4096], F16)
        self.kvh = sb("kvh", [128, 2080], BF16)
        self.ksbd = sb("ksbd", [128, 512], BF16)
        self.kvstg = sb("kvstg", [128, 1024], F32)
        self.kvstg_b = sb("kvstg_b", [128, 1024], BF16)
        self.wfb = sb("wfb", [128, 256], BF16)
        self.cst = sb("cst", [128, 8], F32)
        self.scal = sb("scal", [128, 16], F32)
        self.sct = sb("sct", [1, 24], F32)
        self.onesb = sb("onesb", [1, 512], BF16)
        self.onesf = sb("onesf", [1, 128], F32)
        self.idb = sb("idb", [128, 128], BF16)
        self.idf = sb("idf", [128, 128], F32)
        self.idh = sb("idh", [128, 128], F16)
        self.st1m = sb("st1m", [128, 32], F32)
        self.st1r = sb("st1r", [128, 32], F32)
        self.sum1 = sb("sum1", [128, 32], F32)
        self.sq1 = sb("sq1", [128, 32], F32)
        self.st2m = self.st1m
        self.st2r = self.st1r
        self.sum2 = self.sum1
        self.sq2 = self.sq1
        self.abuf = sb("abuf", [128, 32], F32)
        self.bcbuf = sb("bcbuf", [128, 32], F32)
        self.amx = sb("amx", [128, 8], F32)
        self.dqt = sb("dqt", [128, 64], F32)

        self.pb = [es.enter_context(nc.psum_tensor(f"pb{i}", [128, 512], F32))
                   for i in range(6)]
        self.pt = es.enter_context(nc.psum_tensor("pt", [128, 1024], BF16))
        self.pt2 = es.enter_context(nc.psum_tensor("pt2", [128, 1024], BF16))

        nc.const_aps.register(F32, 0.0, self.cst[:, CZERO:CZERO + 1])
        nc.const_aps.register(F32, 1.0, self.cst[:, CONE:CONE + 1])
        nc.const_aps.register(F32, -1.0, self.cst[:, CNEG1:CNEG1 + 1])
        nc.const_aps.register(F32, C_MAG, self.cst[:, CC_:CC_ + 1])
        nc.const_aps.register(F32, -C_MAG, self.cst[:, CNC:CNC + 1])
        nc.const_aps.register(F32, 1e-5, self.cst[:, CEPS5:CEPS5 + 1])
        nc.const_aps.register(F32, 65536.0, self.cst[:, C65536:C65536 + 1])

        self.sems = {}
        self.semv = {}
        for name in SEM_NAMES:
            self.sems[name] = es.enter_context(nc.semaphore(name))
            self.semv[name] = 0

    def phase(self, fn, *args):
        base = dict(self.semv)
        with self.nc.Block() as blk:
            for eng in ENGS:
                V = dict(base)
                def body(E, eng=eng, V=V):
                    fn(eng, E, V, *args)
                getattr(blk, eng)(body)
        V = dict(base)
        fn(None, None, V, *args)
        self.semv = V

    def tick(self, instr, V, s, n=1):
        V[s] += n
        if instr is not None:
            instr.then_inc(self.sems[s], n)

    def wt(self, E, s, val):
        if E is not None:
            E.wait_ge(self.sems[s], val)


    def ph_init(self, eng, E, V):
        t, wt = self.tick, self.wt
        if eng == "sync":
            t(E.dma_start(out=self.idb[:, :], in_=self.identbf[:, :]), V, "L1", 16)
            t(E.dma_start(out=self.idf[:, :], in_=self.identf32[:, :]), V, "L1", 16)
            t(E.dma_start(out=self.idh[:, :], in_=self.identf16[:, :]), V, "L1", 16)
            t(E.dma_start(out=self.ewA[0:64, 0:256], in_=self.wfT[:, :]), V, "L1", 16)
        else:
            for _ in range(4):
                t(None, V, "L1", 16)
        if eng == "vector":
            for col, val in ((CZERO, 0.0), (CONE, 1.0), (CNEG1, -1.0), (CC_, C_MAG),
                             (CNC, -C_MAG), (CEPS6, 1e-6), (CEPS5, 1e-5),
                             (C65536, 65536.0)):
                E.memset(self.cst[:, col:col + 1], float(val))
            E.memset(self.onesb[:, :], 1.0)
            E.memset(self.onesf[:, :], 1.0)
            E.memset(self.amx[:, :], 0.0)
            E.memset(self.vaug[:, 0:NT * 1040].rearrange(
                "p (t j) -> p t j", j=65)[:, :, 64:65], 1.0)
            wt(E, "L1", V["L1"])
            E.tensor_copy(self.wfb[0:64, :], self.ewA[0:64, 0:256])
            ti = E.tensor_copy(self.wfb[64:128, :], self.ewA[0:64, 0:256])
            t(ti, V, "V1", 1)
        else:
            t(None, V, "V1", 1)

    def ph_dequant(self, eng, E, V):
        t, wt = self.tick, self.wt
        jobs = [("up", self.wup_nat, 128 * r, 0, r) for r in range(3)] + \
               [("gt", self.wgt_nat, 128 * r, 0, r) for r in range(3)] + \
               [("dn", self.wdn_nat, 0, 1024 * c3, c3) for c3 in range(3)]
        import os
        if os.environ.get("DQ_ONE"):
            jobs = jobs[:1]
        b = dict(V)
        for j, (kind, src, ro, co, idx) in enumerate(jobs):
            wsl = self.ewA[:, 0:1024]
            q1 = self.ewA[:, 1024:2048]
            q2 = self.ewA[:, 2048:3072]
            dq = self.ewB[:, 0:1024].bitcast(F16)
            stage = self.ewB[:, 1024:2048].bitcast(F16)
            if eng == "sync":
                wt(E, "V1", b["V1"] + j)
                t(E.dma_start(out=wsl, in_=src[ro:ro + 128, co:co + 1024]), V, "L1", 16)
            else:
                t(None, V, "L1", 16)
            if eng == "vector":
                wt(E, "L1", b["L1"] + 16 * (j + 1))
                wt(E, "T1", b["T1"] + j)
                bm = self.dqt[:, 0:32]
                sc = self.dqt[:, 32:64]
                E.tensor_reduce(bm.rearrange("p (b o) -> p b o", o=1),
                                wsl.rearrange("p (b u) -> p b u", u=32),
                                AX, ALU.max, apply_absolute_value=True)
                E.drain()
                E.tensor_scalar(sc, bm, 1.0 / 7.0, 1e-12, ALU.mult, ALU.max)
                E.drain()
                E.reciprocal(bm, sc)
                E.drain()
                E.tensor_tensor(q1.rearrange("p (b u) -> p b u", u=32),
                                wsl.rearrange("p (b u) -> p b u", u=32),
                                bm.unsqueeze(2).broadcast_to([128, 32, 32]),
                                ALU.mult)
                E.drain()
                E.tensor_scalar(q2, q1, C_MAG, -C_MAG, ALU.add, ALU.add)
                E.drain()
                ti = E.tensor_tensor(dq.rearrange("p (b u) -> p b u", u=32),
                                     q2.rearrange("p (b u) -> p b u", u=32),
                                     sc.unsqueeze(2).broadcast_to([128, 32, 32]),
                                     ALU.mult)
                t(ti, V, "V1", 1)
            else:
                t(None, V, "V1", 1)
            if eng == "tensor":
                wt(E, "V1", b["V1"] + j + 1)
                wt(E, "SC", b["SC"] + j)
                for kk in range(8):
                    ti = E.transpose(self.pt[:, 128 * kk:128 * kk + 128].bitcast(F16),
                                     dq[:, 128 * kk:128 * kk + 128], self.idh[:, :])
                t(ti, V, "T1", 1)
            else:
                t(None, V, "T1", 1)
            if eng == "scalar":
                wt(E, "T1", b["T1"] + j + 1)
                wt(E, "S1", b["S1"] + 128 * j)
                ti = E.activation(stage, self.pt[:, :].bitcast(F16), AF.Copy,
                                  bias=0.0, scale=self.cst[:, CONE:CONE + 1])
                t(ti, V, "SC", 1)
            else:
                t(None, V, "SC", 1)
            if eng == "gpsimd":
                wt(E, "SC", b["SC"] + j + 1)
                dst = {"up": self.agu_in, "gt": self.agg_in, "dn": self.agd_in}[kind]
                for kk in range(8):
                    blkv = stage[:, 128 * kk:128 * kk + 128]
                    if kind in ("up", "gt"):
                        ti = E.dma_start(out=dst[128 * kk:128 * kk + 128,
                                                 128 * idx:128 * idx + 128], in_=blkv)
                    else:
                        ti = E.dma_start(out=dst[1024 * idx + 128 * kk:
                                                 1024 * idx + 128 * kk + 128, 0:128],
                                         in_=blkv)
                    t(ti, V, "S1", 16)
            else:
                for kk in range(8):
                    t(None, V, "S1", 16)
        import os
        if os.environ.get("NO_AG"):
            return
        if eng == "gpsimd":
            wt(E, "S1", b["S1"] + 128 * 9)
            for (i_, o_) in ((self.agu_in, self.agu_out), (self.agg_in, self.agg_out),
                            (self.agd_in, self.agd_out)):
                ti = E.collective_compute("AllGather", ALU.bypass,
                                          replica_groups=[list(range(NCORES))],
                                          ins=[i_.ap().opt()], outs=[o_.ap().opt()])
                t(ti, V, "CC", 1)
            wt(E, "CC", V["CC"])
        else:
            for _ in range(3):
                t(None, V, "CC", 1)

    def ph_stats1(self, eng, E, V):
        t, wt = self.tick, self.wt
        b = dict(V)
        for i in range(NT):
            xin = self.xt[:, 1024 * (i % 4):1024 * (i % 4) + 1024]
            ks = KS4[i % 4]
            if eng == "sync":
                if i >= 4:
                    wt(E, "V1", b["V1"] + (i - 4) + 1)
                t(E.dma_start(out=xin, in_=self.x[128 * i:128 * i + 128, :]),
                  V, ks, 16)
            else:
                t(None, V, ks, 16)
            if eng == "vector":
                wt(E, ks, b[ks] + 16 * (i // 4 + 1))
                E.tensor_tensor(self.ewA[:, 0:1024], xin, xin, ALU.mult)
                E.tensor_reduce(self.sum1[:, i:i + 1], xin, AX, ALU.add)
                ti = E.tensor_reduce(self.sq1[:, i:i + 1], self.ewA[:, 0:1024],
                                     AX, ALU.add)
                t(ti, V, "V1", 1)
            else:
                t(None, V, "V1", 1)
        if eng == "vector":
            E.tensor_scalar(self.st1m[:, :], self.sum1[:, :], 1.0 / D, None, ALU.mult)
            E.tensor_scalar(self.ewA[:, 0:32], self.sq1[:, :], 1.0 / D, None, ALU.mult)
            E.tensor_tensor(self.ewA[:, 32:64], self.st1m[:, :], self.st1m[:, :],
                            ALU.mult)
            E.drain()
            ti = E.tensor_tensor(self.ewA[:, 0:32], self.ewA[:, 0:32],
                                 self.ewA[:, 32:64], ALU.subtract)
            t(ti, V, "V1", 1)
        else:
            t(None, V, "V1", 1)
        if eng == "scalar":
            wt(E, "V1", V["V1"])
            ti = E.activation(self.ewA[:, 64:96], self.ewA[:, 0:32], AF.Sqrt,
                              bias=self.cst[:, CEPS5:CEPS5 + 1],
                              scale=self.cst[:, CONE:CONE + 1])
            t(ti, V, "SC", 1)
        else:
            t(None, V, "SC", 1)
        if eng == "vector":
            wt(E, "SC", V["SC"])
            ti = E.reciprocal(self.st1r[:, :], self.ewA[:, 64:96])
            t(ti, V, "V1", 1)
        else:
            t(None, V, "V1", 1)

    def ph_norm1(self, eng, E, V):
        t, wt = self.tick, self.wt
        b = dict(V)
        for i in range(NT):
            xin = self.xt[:, 1024 * (i % 4):1024 * (i % 4) + 1024]
            xnb = self.ewB[:, 1024 * (i % 4):1024 * (i % 4) + 1024]
            ptt = self.pt if i % 2 == 0 else self.pt2
            ks = KS4[i % 4]
            if eng == "sync":
                if i >= 4:
                    wt(E, "V1", b["V1"] + 2 * (i - 4) + 1)
                t(E.dma_start(out=xin, in_=self.x[128 * i:128 * i + 128, :]),
                  V, ks, 16)
            else:
                t(None, V, ks, 16)
            if eng == "vector":
                wt(E, ks, b[ks] + 16 * (i // 4 + 1))
                if i >= 4:
                    wt(E, "T1", b["T1"] + (i - 4) + 1)
                ti = E.tensor_scalar(xnb, xin, self.st1m[:, i:i + 1],
                                     self.st1r[:, i:i + 1], ALU.subtract, ALU.mult)
                t(ti, V, "V1", 1)
            else:
                t(None, V, "V1", 1)
            if eng == "tensor":
                wt(E, "V1", b["V1"] + 2 * i + 1)
                if i >= 2:
                    wt(E, "V1", b["V1"] + 2 * (i - 2) + 2)
                for kk in range(8):
                    ti = E.transpose(ptt[:, 128 * kk:128 * kk + 128],
                                     xnb[:, 128 * kk:128 * kk + 128], self.idb[:, :])
                t(ti, V, "T1", 1)
            else:
                t(None, V, "T1", 1)
            if eng == "vector":
                wt(E, "T1", b["T1"] + i + 1)
                ti = E.tensor_copy(
                    self.bigA[:, :].rearrange("p (k s) -> p k s", k=8)[
                        :, :, 128 * i:128 * i + 128],
                    ptt[:, :].rearrange("p (k s) -> p k s", k=8))
                t(ti, V, "V1", 1)
            else:
                t(None, V, "V1", 1)

    def ph_xndbg(self, eng, E, V):
        t = self.tick
        if eng == "gpsimd":
            t(E.dma_start(out=self.xn_dbg[:, :], in_=self.bigA[:, :]), V, "S1", 16)
            t(E.dma_start(out=self.st_dbg[:, 0:32], in_=self.st1m[:, :]), V, "S1", 16)
            ti = E.dma_start(out=self.st_dbg[:, 32:64], in_=self.st1r[:, :])
            t(ti, V, "S1", 16)
            E.wait_ge(self.sems["S1"], V["S1"])
        else:
            for _ in range(3):
                t(None, V, "S1", 16)

    def ph_qkv(self, eng, E, V):
        t, wt = self.tick, self.wt
        for wi, (wsrc, spill) in enumerate(((self.wqT, self.qT_sp),
                                            (self.wkT, self.kT_sp),
                                            (self.wvT, None))):
            b = dict(V)
            for kk in range(8):
                wstage = self.ewA[:, 1024 * (kk % 2):1024 * (kk % 2) + 1024]
                ks = KS4[kk % 2]
                if eng == "sync":
                    wt(E, "V1", max(0, b["V1"] + kk - 1))
                    t(E.dma_start(out=wstage, in_=wsrc[128 * kk:128 * kk + 128, :]),
                      V, ks, 16)
                else:
                    t(None, V, ks, 16)
                if eng == "vector":
                    wt(E, ks, b[ks] + 16 * (kk // 2 + 1))
                    if kk == 0:
                        wt(E, "T1", b["T1"])
                    ti = E.tensor_copy(self.strm[:, 1024 * kk:1024 * kk + 1024],
                                       wstage)
                    t(ti, V, "V1", 1)
                else:
                    t(None, V, "V1", 1)
            if spill is not None:
                n = 0
                for m in range(8):
                    for u in range(8):
                        pbank = self.pb[n % 2]
                        if eng == "tensor":
                            wt(E, "V1", b["V1"] + 8)
                            wt(E, "SC", max(0, b["SC"] + (n - 2) + 1))
                            for kk in range(8):
                                ti = E.matmul(
                                    pbank[:, :],
                                    self.strm[:, 1024 * kk + 128 * m:
                                              1024 * kk + 128 * m + 128],
                                    self.bigA[:, 4096 * kk + 512 * u:
                                              4096 * kk + 512 * u + 512],
                                    start=(kk == 0), stop=(kk == 7))
                            t(ti, V, "T1", 1)
                        else:
                            t(None, V, "T1", 1)
                        stgb = self.ewB[:, 512 * (n % 4):512 * (n % 4) + 512]
                        if eng == "scalar":
                            wt(E, "T1", b["T1"] + n + 1)
                            if n >= 4:
                                wt(E, "S1", b["S1"] + 16 * (n - 4 + 1))
                            ti = E.activation(stgb, pbank[:, :], AF.Copy, bias=0.0,
                                              
                                              scale=self.cst[:, CONE:CONE + 1])
                            t(ti, V, "SC", 1)
                        else:
                            t(None, V, "SC", 1)
                        if eng == "gpsimd":
                            wt(E, "SC", b["SC"] + n + 1)
                            ti = E.dma_start(out=spill[m, :, 512 * u:512 * u + 512],
                                             in_=stgb)
                            t(ti, V, "S1", 16)
                        else:
                            t(None, V, "S1", 16)
                        n += 1
            else:
                for st in range(NT):
                    pba = self.pb[2 * (st % 2)]
                    pbb = self.pb[2 * (st % 2) + 1]
                    if eng == "tensor":
                        wt(E, "V1", b["V1"] + 8)
                        if st < 2:
                            wt(E, "SC", b["SC"])
                        else:
                            wt(E, "V1", b["V1"] + 8 + (st - 2) + 1)
                        for kk in range(8):
                            lhs = self.bigA[:, 4096 * kk + 128 * st:
                                            4096 * kk + 128 * st + 128]
                            E.matmul(pba[:, :], lhs,
                                     self.strm[:, 1024 * kk:1024 * kk + 512],
                                     start=(kk == 0), stop=(kk == 7))
                            ti = E.matmul(pbb[:, :], lhs,
                                          self.strm[:, 1024 * kk + 512:
                                                    1024 * kk + 1024],
                                          start=(kk == 0), stop=(kk == 7))
                        t(ti, V, "T1", 1)
                    else:
                        t(None, V, "T1", 1)
                    if eng == "vector":
                        wt(E, "T1", b["T1"] + st + 1)
                        dst = self.vaug[:, 1040 * st:1040 * st + 1040].rearrange(
                            "p (h j) -> p h j", j=65)[:, :, 0:64]
                        E.tensor_copy(dst[:, 0:8, :],
                                      pba[:, :].rearrange("p (h j) -> p h j", j=64))
                        ti = E.tensor_copy(dst[:, 8:16, :],
                                           pbb[:, :].rearrange("p (h j) -> p h j", j=64))
                        t(ti, V, "V1", 1)
                    else:
                        t(None, V, "V1", 1)

    def ph_kv(self, eng, E, V):
        t, wt = self.tick, self.wt
        B = dict(V)
        for g in range(4):
            b = dict(V)
            if eng == "sync":
                if g == 0:
                    wt(E, "S1", B["S1"])
                wt(E, "T1", B["T1"] + 32 * g)
                for cc in range(2):
                    t(E.dma_start(out=self.strm[:, 4096 * cc:4096 * cc + 4096],
                                  in_=self.kT_sp[2 * g + cc, :, :]),
                      V, KS4[cc], 16)
            else:
                for cc in range(2):
                    t(None, V, KS4[cc], 16)
            def emit_z1(tt):
                zsl = tt % 2
                pz = (self.pb[0], self.pb[1]) if zsl == 0 else (self.pb[2], self.pb[3])
                for hh in range(4):
                    head = 4 * g + hh
                    p = head % 2
                    cc = (head // 2) - 2 * g
                    bank = pz[hh // 2]
                    col = 256 * (hh % 2)
                    lhs = self.strm[64 * p:64 * p + 64,
                                    4096 * cc + 128 * tt:4096 * cc + 128 * tt + 128]
                    E.matmul(bank[:, col:col + 256], lhs,
                             self.wfb[64 * p:64 * p + 64, :],
                             start=(hh % 2 == 0), stop=False)
                    ti = E.matmul(bank[:, col:col + 256], self.onesb[0:1, 0:128],
                                  self.onesb[0:1, 0:256],
                                  start=False, stop=(hh % 2 == 1))
                return ti

            def emit_kv(tt):
                zsl = tt % 2
                kf = self.ewB[:, 1024 * zsl:1024 * zsl + 1024]
                for hh in range(4):
                    head = 4 * g + hh
                    bank = self.pb[4 + hh // 2]
                    col = 256 * (hh % 2)
                    lhs = self.vaug[:, 1040 * tt + 65 * head:
                                    1040 * tt + 65 * head + 65]
                    ti = E.matmul(bank[0:65, col:col + 256], lhs,
                                  kf[:, 256 * hh:256 * hh + 256],
                                  start=(tt == 0 and hh % 2 == 0),
                                  stop=(tt == NT - 1 and hh % 2 == 1))
                return ti

            for tt in range(NT):
                zsl = tt % 2
                pz = (self.pb[0], self.pb[1]) if zsl == 0 else (self.pb[2], self.pb[3])
                et = self.ewA[:, 1024 * zsl:1024 * zsl + 1024]
                kf = self.ewB[:, 1024 * zsl:1024 * zsl + 1024]
                if eng == "tensor":
                    if tt == 0:
                        wt(E, "KA", B["KA"] + 16 * (g + 1))
                        wt(E, "KB", B["KB"] + 16 * (g + 1))
                        wt(E, "V2", B["V2"] + g)
                    if tt >= 2:
                        wt(E, "V1", b["V1"] + (tt - 2) + 1)
                    ti = emit_z1(tt)
                    t(ti, V, "T1", 1)
                    if tt >= 1:
                        wt(E, "V1", b["V1"] + tt)
                        ti = emit_kv(tt - 1)
                        t(ti, V, "T2", 1)
                    if tt == NT - 1:
                        wt(E, "V1", b["V1"] + tt + 1)
                        ti = emit_kv(tt)
                        t(ti, V, "T2", 1)
                else:
                    t(None, V, "T1", 1)
                    if tt >= 1:
                        t(None, V, "T2", 1)
                    if tt == NT - 1:
                        t(None, V, "T2", 1)
                if eng == "scalar":
                    wt(E, "T1", b["T1"] + tt + 1)
                    E.activation(et[:, 0:512], pz[0][:, :], AF.Exp,
                                 bias=self.cst[:, CNEG1:CNEG1 + 1],
                                 scale=self.cst[:, CONE:CONE + 1])
                    ti = E.activation(et[:, 512:1024], pz[1][:, :], AF.Exp,
                                      bias=self.cst[:, CNEG1:CNEG1 + 1],
                                      scale=self.cst[:, CONE:CONE + 1])
                    t(ti, V, "SC", 1)
                else:
                    t(None, V, "SC", 1)
                if eng == "vector":
                    wt(E, "SC", b["SC"] + tt + 1)
                    if tt >= 2:
                        wt(E, "T2", b["T2"] + (tt - 2) + 1)
                    E.scalar_tensor_tensor(kf[:, 0:512], et[:, 0:512], 1.0,
                                           pz[0][:, :], ALU.min, ALU.max)
                    ti = E.scalar_tensor_tensor(kf[:, 512:1024], et[:, 512:1024], 1.0,
                                                pz[1][:, :], ALU.min, ALU.max)
                    t(ti, V, "V1", 1)
                else:
                    t(None, V, "V1", 1)
            if eng == "vector":
                wt(E, "T2", b["T2"] + NT)
                E.tensor_copy(self.kvstg[0:65, 0:512], self.pb[4][0:65, :])
                E.tensor_copy(self.kvstg[0:65, 512:1024], self.pb[5][0:65, :])
                E.tensor_copy(self.kvstg_b[0:65, 0:512], self.kvstg[0:65, 0:512])
                ti = E.tensor_copy(self.kvstg_b[0:65, 512:1024],
                                   self.kvstg[0:65, 512:1024])
                t(ti, V, "V2", 1)
            else:
                t(None, V, "V2", 1)
            if eng == "tensor":
                wt(E, "V2", B["V2"] + g + 1)
                if g >= 1:
                    wt(E, "V3", B["V3"] + g)
                for hh in range(4):
                    for j in range(2):
                        ti = E.transpose(
                            self.pt2[:, 68 * (2 * hh + j):68 * (2 * hh + j) + 65],
                            self.kvstg_b[0:65, 256 * hh + 128 * j:
                                         256 * hh + 128 * j + 128],
                            self.idb[0:65, 0:65])
                t(ti, V, "T3", 1)
            else:
                t(None, V, "T3", 1)
            if eng == "vector":
                wt(E, "T3", B["T3"] + g + 1)
                ti = E.tensor_copy(
                    self.kvh[:, 520 * g:520 * g + 520].rearrange(
                        "p (c j) -> p c j", j=65),
                    self.pt2[:, 0:544].rearrange("p (c j) -> p c j", j=68)[:, :, 0:65])
                t(ti, V, "V3", 1)
            else:
                t(None, V, "V3", 1)

    @property
    def sel(self):
        return self.bigA[:, 8192:10240].bitcast(F32)

    def ph_ksbd(self, eng, E, V):
        t = self.tick
        if eng == "sync":
            t(E.dma_start(out=self.sel[0:16, :], in_=self.selmat[:, :]), V, "L2", 16)
        else:
            t(None, V, "L2", 16)
        if eng == "vector":
            self.wt(E, "L2", V["L2"])
            E.memset(self.ksbd[:, :], 0.0)
            E.drain()
            for c in range(32):
                h = c // 2
                ti = E.tensor_copy(self.ksbd[:, 16 * c + h:16 * c + h + 1],
                                   self.kvh[:, 65 * c + 64:65 * c + 65])
            t(ti, V, "V1", 1)
        else:
            t(None, V, "V1", 1)

    def ph_attn(self, eng, E, V):
        t, wt = self.tick, self.wt
        B = dict(V)
        for kk in range(8):
            wstage = self.ewA[:, 1024 * (kk % 2):1024 * (kk % 2) + 1024]
            ks = KS4[2 + kk % 2]
            if eng == "sync":
                if kk >= 2:
                    wt(E, "V1", B["V1"] + (kk - 2) + 1)
                t(E.dma_start(out=wstage, in_=self.woT[128 * kk:128 * kk + 128, :]),
                  V, ks, 16)
            else:
                t(None, V, ks, 16)
            if eng == "vector":
                wt(E, ks, B[ks] + 16 * (kk // 2 + 1))
                ti = E.tensor_copy(self.bigA[:, 1024 * kk:1024 * kk + 1024], wstage)
                t(ti, V, "V1", 1)
            else:
                t(None, V, "V1", 1)
        for u in range(NU):
            lb = B["L1"] + 128 + 192 * u
            tb = B["T1"] + 36 * u
            t2b = B["T2"] + u
            t3b = B["T3"] + 16 * u
            scb = B["SC"] + 48 * u
            v1b = B["V1"] + 8 + 32 * u
            v2b = B["V2"] + 17 * u
            v3b = B["V3"] + 4 * u
            qsl = u % 2
            qb = self.strm[:, 4096 * qsl:4096 * qsl + 4096]
            kq = KS4[u % 2]
            if eng == "sync":
                if u == 0:
                    wt(E, "S1", B["S1"])
                if u >= 2:
                    wt(E, "T1", B["T1"] + 36 * (u - 2) + 32)
                for kk in range(8):
                    t(E.dma_start(out=qb[:, 512 * kk:512 * kk + 512],
                                  in_=self.qT_sp[kk, :, 512 * u:512 * u + 512]),
                      V, kq, 16)
                for r in range(4):
                    gi_ = 4 * u + r
                    kx = "KE" if gi_ % 2 == 0 else "KF"
                    if gi_ >= 2:
                        wt(E, "V3", B["V3"] + (gi_ - 2) + 1)
                    xin = self.xt[:, 1024 * (r % 2):1024 * (r % 2) + 1024]
                    t(E.dma_start(out=xin,
                                  in_=self.x[512 * u + 128 * r:
                                             512 * u + 128 * r + 128, :]),
                      V, kx, 16)
            else:
                for kk in range(8):
                    t(None, V, kq, 16)
                for r in range(4):
                    gi_ = 4 * u + r
                    kx = "KE" if gi_ % 2 == 0 else "KF"
                    t(None, V, kx, 16)
            for c in range(32):
                h, j = c // 2, c % 2
                bank = self.pb[c % 4]
                et = self.ewA[:, 512 * (c % 4):512 * (c % 4) + 512]
                qf = self.vaug[:, 512 * c:512 * c + 512]
                if eng == "tensor":
                    if c == 0:
                        wt(E, kq, B[kq] + 128 * (u // 2 + 1))
                        if u >= 1:
                            wt(E, "V3", B["V3"] + 4 * u)
                    if c >= 4:
                        wt(E, "V1", v1b + (c - 4) + 1)
                    m, p = h // 2, h % 2
                    E.matmul(bank[:, :],
                             self.wfb[64 * p:64 * p + 64, 128 * j:128 * j + 128],
                             qb[64 * p:64 * p + 64, 512 * m:512 * m + 512],
                             start=True, stop=False)
                    ti = E.matmul(bank[:, :], self.onesb[0:1, 0:128],
                                  self.onesb[0:1, 0:512], start=False, stop=True)
                    t(ti, V, "T1", 1)
                else:
                    t(None, V, "T1", 1)
                if eng == "scalar":
                    wt(E, "T1", tb + c + 1)
                    if c >= 4:
                        wt(E, "V1", v1b + (c - 4) + 1)
                    ti = E.activation(et, bank[:, :], AF.Exp,
                                      bias=self.cst[:, CNEG1:CNEG1 + 1],
                                      scale=self.cst[:, CONE:CONE + 1])
                    t(ti, V, "SC", 1)
                else:
                    t(None, V, "SC", 1)
                if eng == "vector":
                    wt(E, "SC", scb + c + 1)
                    if u >= 1 and c == 0:
                        wt(E, "T2", B["T2"] + u)
                        wt(E, "T3", B["T3"] + 16 * u)
                    ti = E.scalar_tensor_tensor(qf, et, 1.0, bank[:, :],
                                                ALU.min, ALU.max)
                    t(ti, V, "V1", 1)
                else:
                    t(None, V, "V1", 1)
            if eng == "tensor":
                wt(E, "V1", v1b + 32)
                if u >= 1:
                    wt(E, "V2", B["V2"] + 17 * u)
                for c in range(32):
                    ti = E.matmul(self.pb[4][0:16, :], self.ksbd[:, 16 * c:16 * c + 16],
                                  self.vaug[:, 512 * c:512 * c + 512],
                                  start=(c == 0), stop=(c == 31))
                t(ti, V, "T2", 1)
            else:
                t(None, V, "T2", 1)
            if eng == "vector":
                wt(E, "T2", t2b + 1)
                E.tensor_scalar(self.kvstg[0:16, 512:1024], self.pb[4][0:16, :],
                                self.cst[0:16, CEPS6:CEPS6 + 1], None, ALU.add)
                E.drain()
                ti = E.reciprocal(self.kvstg[0:16, 0:512], self.kvstg[0:16, 512:1024])
                t(ti, V, "V2", 1)
            else:
                t(None, V, "V2", 1)
            for h in range(H):
                rb = self.pb[5] if h % 2 == 0 else self.pt[:, :].bitcast(F32)
                ob = self.pb[4] if h % 2 == 0 else self.pb[3]
                osb = self.ewA[:, 2048 + 512 * (h % 2):2048 + 512 * (h % 2) + 512]
                if eng == "tensor":
                    if h < 2:
                        wt(E, "V2", v2b + 1)
                    else:
                        wt(E, "V2", v2b + h)
                        wt(E, "SC", scb + 31 + h)
                    E.matmul(rb[0:64, :], self.sel[0:16, 64 * h:64 * h + 64],
                             self.kvstg[0:16, 0:512], start=True, stop=True)
                    for j in range(2):
                        c = 2 * h + j
                        ti = E.matmul(ob[0:64, :], self.kvh[:, 65 * c:65 * c + 64],
                                      self.vaug[:, 512 * c:512 * c + 512],
                                      start=(j == 0), stop=(j == 1))
                    t(ti, V, "T3", 1)
                else:
                    t(None, V, "T3", 1)
                if eng == "scalar":
                    wt(E, "T3", t3b + h + 1)
                    if h >= 2:
                        wt(E, "V2", v2b + 1 + h - 1)
                    ti = E.activation(osb[0:64, :], ob[0:64, :], AF.Copy, bias=0.0,
                                      scale=self.cst[0:64, CONE:CONE + 1])
                    t(ti, V, "SC", 1)
                else:
                    t(None, V, "SC", 1)
                if eng == "vector":
                    wt(E, "SC", scb + 32 + h + 1)
                    if u >= 1 and h == 0:
                        wt(E, "T1", B["T1"] + 36 * u)
                    po = 64 * (h % 2)
                    dst = self.ewB[:, 512 * (h // 2):512 * (h // 2) + 512]
                    ti = E.tensor_tensor(dst[po:po + 64, :], osb[0:64, :],
                                         rb[0:64, :], ALU.mult)
                    t(ti, V, "V2", 1)
                else:
                    t(None, V, "V2", 1)
            for r in range(4):
                gi = 4 * u + r
                pa = self.pb[2 * (r % 2)]
                pc = self.pb[2 * (r % 2) + 1]
                xin = self.xt[:, 1024 * (r % 2):1024 * (r % 2) + 1024]
                xp = self.xt[:, 2048 + 1024 * (r % 2):2048 + 1024 * (r % 2) + 1024]
                if eng == "tensor":
                    wt(E, "V2", v2b + 17)
                    if r >= 2:
                        wt(E, "V3", v3b + (r - 2) + 1)
                    for kk in range(8):
                        lhs = self.ewB[:, 512 * kk + 128 * r:512 * kk + 128 * r + 128]
                        E.matmul(pa[:, :], lhs, self.bigA[:, 1024 * kk:1024 * kk + 512],
                                 start=(kk == 0), stop=(kk == 7))
                        ti = E.matmul(pc[:, :], lhs,
                                      self.bigA[:, 1024 * kk + 512:1024 * kk + 1024],
                                      start=(kk == 0), stop=(kk == 7))
                    t(ti, V, "T1", 1)
                else:
                    t(None, V, "T1", 1)
                if eng == "vector":
                    wt(E, "T1", tb + 32 + r + 1)
                    kx = "KE" if gi % 2 == 0 else "KF"
                    wt(E, kx, B[kx] + 16 * (gi // 2 + 1))
                    if gi >= 2:
                        wt(E, "S1", B["S1"] + 16 * (gi - 1))
                    E.tensor_tensor(xp[:, 0:512], pa[:, :], xin[:, 0:512], ALU.add)
                    E.tensor_tensor(xp[:, 512:1024], pc[:, :], xin[:, 512:1024],
                                    ALU.add)
                    E.tensor_tensor(self.ewA[:, 0:512], xp[:, 0:512],
                                    xp[:, 0:512], ALU.mult)
                    E.tensor_tensor(self.ewA[:, 512:1024], xp[:, 512:1024],
                                    xp[:, 512:1024], ALU.mult)
                    E.tensor_reduce(self.sum2[:, gi:gi + 1], xp, AX, ALU.add)
                    ti = E.tensor_reduce(self.sq2[:, gi:gi + 1],
                                         self.ewA[:, 0:1024], AX, ALU.add)
                    t(ti, V, "V3", 1)
                else:
                    t(None, V, "V3", 1)
                if eng == "gpsimd":
                    wt(E, "V3", v3b + r + 1)
                    ti = E.dma_start(out=self.xp_sp[gi, :, :], in_=xp)
                    t(ti, V, "S1", 16)
                else:
                    t(None, V, "S1", 16)

    def ph_ln2_derive(self, eng, E, V):
        t, wt = self.tick, self.wt
        B = dict(V)
        if eng == "vector":
            E.tensor_scalar(self.st2m[:, :], self.sum2[:, :], 1.0 / D, None, ALU.mult)
            E.tensor_scalar(self.ewA[:, 0:32], self.sq2[:, :], 1.0 / D, None, ALU.mult)
            E.tensor_tensor(self.ewA[:, 32:64], self.st2m[:, :], self.st2m[:, :],
                            ALU.mult)
            E.drain()
            ti = E.tensor_tensor(self.ewA[:, 0:32], self.ewA[:, 0:32],
                                 self.ewA[:, 32:64], ALU.subtract)
            t(ti, V, "V1", 1)
        else:
            t(None, V, "V1", 1)
        if eng == "scalar":
            wt(E, "V1", V["V1"])
            ti = E.activation(self.ewA[:, 64:96], self.ewA[:, 0:32], AF.Sqrt,
                              bias=self.cst[:, CEPS5:CEPS5 + 1],
                              scale=self.cst[:, CONE:CONE + 1])
            t(ti, V, "SC", 1)
        else:
            t(None, V, "SC", 1)
        if eng == "vector":
            wt(E, "SC", V["SC"])
            ti = E.reciprocal(self.st2r[:, :], self.ewA[:, 64:96])
            t(ti, V, "V1", 1)
        else:
            t(None, V, "V1", 1)
        bswp = dict(V)
        for i in range(NT):
            xp = self.xt[:, 1024 * (i % 4):1024 * (i % 4) + 1024]
            ks = KS4[i % 4]
            if eng == "sync":
                if i == 0:
                    wt(E, "S1", bswp["S1"])
                wt(E, "V1", bswp["V1"])
                if i >= 4:
                    wt(E, "V1", bswp["V1"] + 2 * (i - 4) + 2)
                t(E.dma_start(out=xp, in_=self.xp_sp[i, :, :]), V, ks, 16)
            else:
                t(None, V, ks, 16)
            trash = self.ewA[:, 1024 * (i % 2):1024 * (i % 2) + 1024]
            if eng == "vector":
                wt(E, ks, bswp[ks] + 16 * (i // 4 + 1))
                E.tensor_scalar(trash, xp, self.st2m[:, i:i + 1],
                                self.st2r[:, i:i + 1], ALU.subtract, ALU.mult)
                if i >= 1:
                    pt_ = self.ewA[:, 1024 * ((i - 1) % 2):1024 * ((i - 1) % 2) + 1024]
                    E.tensor_reduce(self.dqt[:, (i - 1) % 2:(i - 1) % 2 + 1],
                                    pt_, AX, ALU.max, apply_absolute_value=True)
                if i >= 2:
                    ti = E.tensor_tensor(self.amx[:, 0:1], self.amx[:, 0:1],
                                         self.dqt[:, (i - 2) % 2:(i - 2) % 2 + 1],
                                         ALU.max)
                else:
                    ti = E.engine_nop()
                t(ti, V, "V1", 2)
            else:
                t(None, V, "V1", 2)
        if eng == "vector":
            last = self.ewA[:, 1024 * ((NT - 1) % 2):1024 * ((NT - 1) % 2) + 1024]
            E.tensor_reduce(self.dqt[:, (NT - 1) % 2:(NT - 1) % 2 + 1], last,
                            AX, ALU.max, apply_absolute_value=True)
            E.tensor_tensor(self.amx[:, 0:1], self.amx[:, 0:1],
                            self.dqt[:, (NT - 2) % 2:(NT - 2) % 2 + 1], ALU.max)
            E.drain()
            ti = E.tensor_tensor(self.amx[:, 0:1], self.amx[:, 0:1],
                                 self.dqt[:, (NT - 1) % 2:(NT - 1) % 2 + 1],
                                 ALU.max)
            self.tick(ti, V, "V1", 1)
        else:
            self.tick(None, V, "V1", 1)
        self._barrier(eng, E, V, self.amx[:, 0:1], 1, T_G1, None, None)
        self._scal_chain(eng, E, V, [
            ("ts2", T_G1, T_S1, 1.0 / QMAX, 1e-12),
            ("recip", T_S1, T_IS1),
        ])
        self._bcast(eng, E, V, [(T_IS1, SC_INV_S1), (T_S1, SC_S1)])
        if eng == "vector":
            wt(E, "V2", V["V2"])
            E.tensor_scalar(self.abuf[:, :], self.st2r[:, :],
                            self.scal[:, SC_INV_S1:SC_INV_S1 + 1], None, ALU.mult)
            E.drain()
            E.tensor_tensor(self.ewA[:, 0:32], self.st2m[:, :], self.abuf[:, :],
                            ALU.mult)
            E.drain()
            ti = E.tensor_scalar(self.bcbuf[:, :], self.ewA[:, 0:32], -1.0, C_MAG,
                                 ALU.mult, ALU.add)
            t(ti, V, "V1", 1)
        else:
            t(None, V, "V1", 1)

    def _barrier(self, eng, E, V, col_ap, ar_key, dst_tcol, col2_ap, dst_tcol2):
        t, wt = self.tick, self.wt
        if eng == "tensor":
            wt(E, "V1", V["V1"])
            E.transpose(self.pb[0][0:1, 0:128], col_ap, self.idf[:, :])
            c2 = col2_ap if col2_ap is not None else col_ap
            ti = E.transpose(self.pb[1][0:1, 0:128], c2, self.idf[:, :])
            t(ti, V, "T1", 1)
        else:
            t(None, V, "T1", 1)
        if eng == "vector":
            wt(E, "T1", V["T1"])
            E.tensor_reduce(self.kvstg[0:1, 0:1], self.pb[0][0:1, 0:128], AX, ALU.max)
            ti = E.tensor_reduce(self.kvstg[0:1, 1:2], self.pb[1][0:1, 0:128],
                                 AX, ALU.max)
            t(ti, V, "V2", 1)
        else:
            t(None, V, "V2", 1)
        if eng == "sync":
            wt(E, "V2", V["V2"])
            t(E.dma_start(out=self.ar_in[ar_key][:, :], in_=self.kvstg[0:1, 0:2]),
              V, "L2", 16)
        else:
            t(None, V, "L2", 16)
        if eng == "gpsimd":
            wt(E, "L2", V["L2"])
            ti = E.collective_compute("AllReduce", ALU.max,
                                      replica_groups=[list(range(NCORES))],
                                      ins=[self.ar_in[ar_key].ap().opt()],
                                      outs=[self.ar_out[ar_key].ap().opt()])
            t(ti, V, "CC", 1)
        else:
            t(None, V, "CC", 1)
        if eng == "sync":
            wt(E, "CC", V["CC"])
            t(E.dma_start(out=self.sct[0:1, dst_tcol:dst_tcol + 1],
                          in_=self.ar_out[ar_key][0:1, 0:1]), V, "L2", 16)
        else:
            t(None, V, "L2", 16)
        if dst_tcol2 is not None:
            if eng == "sync":
                t(E.dma_start(out=self.sct[0:1, dst_tcol2:dst_tcol2 + 1],
                              in_=self.ar_out[ar_key][0:1, 1:2]), V, "L2", 16)
            else:
                t(None, V, "L2", 16)

    def _scal_chain(self, eng, E, V, ops):
        t, wt = self.tick, self.wt
        if eng == "vector":
            wt(E, "L2", V["L2"])
            ti = None
            first = True
            for op in ops:
                if not first:
                    E.drain()
                first = False
                if op[0] == "ts2":
                    _, src, dst, m, mx = op
                    ti = E.tensor_scalar(self.sct[0:1, dst:dst + 1],
                                         self.sct[0:1, src:src + 1], m, mx,
                                         ALU.mult, ALU.max)
                elif op[0] == "recip":
                    ti = E.reciprocal(self.sct[0:1, op[2]:op[2] + 1],
                                      self.sct[0:1, op[1]:op[1] + 1])
                elif op[0] == "mul2":
                    _, a, bb, dst = op
                    ti = E.tensor_tensor(self.sct[0:1, dst:dst + 1],
                                         self.sct[0:1, a:a + 1],
                                         self.sct[0:1, bb:bb + 1], ALU.mult)
                elif op[0] == "mulc":
                    _, src, dst, cc = op
                    ti = E.tensor_scalar(self.sct[0:1, dst:dst + 1],
                                         self.sct[0:1, src:src + 1], cc, None,
                                         ALU.mult)
            t(ti, V, "V1", 1)
        else:
            t(None, V, "V1", 1)

    def _bcast(self, eng, E, V, pairs):
        t, wt = self.tick, self.wt
        if eng == "tensor":
            wt(E, "V1", V["V1"])
            for i, (tcol, scol) in enumerate(pairs):
                ti = E.matmul(self.pb[1][:, i:i + 1], self.onesf[0:1, 0:128],
                              self.sct[0:1, tcol:tcol + 1], start=True, stop=True)
            t(ti, V, "T2", 1)
        else:
            t(None, V, "T2", 1)
        if eng == "vector":
            wt(E, "T2", V["T2"])
            for i, (tcol, scol) in enumerate(pairs):
                ti = E.tensor_copy(self.scal[:, scol:scol + 1], self.pb[1][:, i:i + 1])
            t(ti, V, "V2", 1)
        else:
            t(None, V, "V2", 1)

    def ph_xq(self, eng, E, V):
        t, wt = self.tick, self.wt
        b = dict(V)
        for i in range(NT):
            xp = self.xt[:, 1024 * (i % 4):1024 * (i % 4) + 1024]
            y = self.ewA[:, 1024 * (i % 2):1024 * (i % 2) + 1024]
            xqb = self.ewB[:, 1024 * (i % 4):1024 * (i % 4) + 1024].bitcast(F16)
            ptt = self.pt if i % 2 == 0 else self.pt2
            ks = KS4[i % 4]
            if eng == "sync":
                if i == 0:
                    wt(E, "S1", b["S1"])
                if i >= 4:
                    wt(E, "V1", b["V1"] + 2 * (i - 4) + 2)
                t(E.dma_start(out=xp, in_=self.xp_sp[i, :, :]), V, ks, 16)
            else:
                t(None, V, ks, 16)
            if eng == "vector":
                wt(E, ks, b[ks] + 16 * (i // 4 + 1))
                E.tensor_scalar(y, xp, self.abuf[:, i:i + 1], self.bcbuf[:, i:i + 1],
                                ALU.mult, ALU.add)
                E.drain()
                if i >= 4:
                    wt(E, "T1", b["T1"] + (i - 4) + 1)
                ti = E.tensor_scalar(xqb, y, -C_MAG, self.scal[:, SC_S1:SC_S1 + 1],
                                     ALU.add, ALU.mult)
                t(ti, V, "V1", 2)
            else:
                t(None, V, "V1", 2)
            if eng == "tensor":
                wt(E, "V1", b["V1"] + 2 * i + 2)
                if i >= 2:
                    wt(E, "V2", b["V2"] + (i - 2) + 1)
                for kk in range(8):
                    ti = E.transpose(ptt[:, 128 * kk:128 * kk + 128].bitcast(F16),
                                     xqb[:, 128 * kk:128 * kk + 128], self.idh[:, :])
                t(ti, V, "T1", 1)
            else:
                t(None, V, "T1", 1)
            if eng == "vector":
                wt(E, "T1", b["T1"] + i + 1)
                ti = E.tensor_copy(
                    self.bigA[:, :].bitcast(F16).rearrange("p (k s) -> p k s", k=8)[
                        :, :, 128 * i:128 * i + 128],
                    ptt[:, :].bitcast(F16).rearrange("p (k s) -> p k s", k=8))
                t(ti, V, "V2", 1)
            else:
                t(None, V, "V2", 1)

    def ph_upgate(self, eng, E, V, which):
        t, wt = self.tick, self.wt
        ag_out = self.agu_out if which == "up" else self.agg_out
        spill = self.up_sp if which == "up" else self.gt_sp
        amxcol = 1 if which == "up" else 2
        b = dict(V)
        if eng == "sync":
            wt(E, "CC", 3)
            for k in range(8):
                src = ag_out.rearrange("(c r) j -> c r j", c=NCORES)[
                    :, 128 * k:128 * k + 128, :].transpose([1, 0, 2])
                t(E.dma_start(out=self.vaug[:, 3072 * k:3072 * k + 3072].bitcast(
                    F16).rearrange("p (c j) -> p c j", c=NCORES), in_=src),
                  V, "L1", 16)
        else:
            for k in range(8):
                t(None, V, "L1", 16)
        groups = [(0, 1, 2), (3, 4, 5), (6, 7)]
        evc = []
        tot = 0
        for ul in groups:
            tot += len(ul)
            evc.append(tot)
        q = 0
        gidx = 0
        for m in range(IC):
            for ulist in groups:
                banks = self.pb[0:3] if gidx % 2 == 0 else self.pb[3:6]
                if eng == "tensor":
                    if gidx == 0:
                        wt(E, "L1", b["L1"] + 128)
                    if gidx >= 2:
                        done = evc[(gidx - 2) % 3] + 8 * ((gidx - 2) // 3)
                        wt(E, "SC", b["SC"] + 8 * ((gidx - 2) // 3) +
                           evc[(gidx - 2) % 3])
                    for k in range(8):
                        lhs = self.vaug[:, 3072 * k + 128 * m:
                                        3072 * k + 128 * m + 128].bitcast(F16)
                        for ui, u in enumerate(ulist):
                            ti = E.matmul(banks[ui][:, :], lhs,
                                          self.bigA[:, 4096 * k + 512 * u:
                                                    4096 * k + 512 * u + 512].bitcast(F16),
                                          start=(k == 0), stop=(k == 7))
                    t(ti, V, "T1", 1)
                else:
                    t(None, V, "T1", 1)
                for ui, u in enumerate(ulist):
                    sslot = q % 8
                    stgt = self.stg[:, 512 * sslot:512 * sslot + 512]
                    if eng == "scalar":
                        wt(E, "T1", b["T1"] + gidx + 1)
                        if q >= 8:
                            wt(E, "S1", b["S1"] + 16 * (q - 8 + 1))
                        ti = E.activation(stgt, banks[ui][:, :], AF.Copy, bias=0.0,
                                          scale=self.cst[:, CONE:CONE + 1])
                        t(ti, V, "SC", 1)
                    else:
                        t(None, V, "SC", 1)
                    if eng == "vector":
                        wt(E, "SC", b["SC"] + q + 1)
                        if q >= 2:
                            E.tensor_tensor(self.amx[:, amxcol:amxcol + 1],
                                            self.amx[:, amxcol:amxcol + 1],
                                            self.dqt[:, (q - 2) % 3:(q - 2) % 3 + 1],
                                            ALU.max)
                        ti = E.tensor_reduce(self.dqt[:, q % 3:q % 3 + 1], stgt,
                                             AX, ALU.max, apply_absolute_value=True)
                        t(ti, V, "V1", 2)
                    else:
                        t(None, V, "V1", 2)
                    if eng == "gpsimd":
                        wt(E, "SC", b["SC"] + q + 1)
                        ti = E.dma_start(out=spill[m, u, :, :], in_=stgt)
                        t(ti, V, "S1", 16)
                    else:
                        t(None, V, "S1", 16)
                    q += 1
                gidx += 1
        if eng == "vector":
            E.drain()
            E.tensor_tensor(self.amx[:, amxcol:amxcol + 1],
                            self.amx[:, amxcol:amxcol + 1],
                            self.dqt[:, (q - 2) % 3:(q - 2) % 3 + 1], ALU.max)
            E.drain()
            ti = E.tensor_tensor(self.amx[:, amxcol:amxcol + 1],
                                 self.amx[:, amxcol:amxcol + 1],
                                 self.dqt[:, (q - 1) % 3:(q - 1) % 3 + 1], ALU.max)
            t(ti, V, "V1", 1)
        else:
            t(None, V, "V1", 1)

    def ph_b2(self, eng, E, V):
        self._barrier(eng, E, V, self.amx[:, 1:2], 2, T_GU, self.amx[:, 2:3], T_GG)
        self._scal_chain(eng, E, V, [
            ("ts2", T_GU, T_S2U, 1.0 / QMAX, 1e-12),
            ("recip", T_S2U, T_IS2U),
            ("ts2", T_GG, T_S2G, 1.0 / QMAX, 1e-12),
            ("recip", T_S2G, T_IS2G),
            ("mulc", T_S2G, T_CG, 1.0 / 65536.0),
        ])
        self._bcast(eng, E, V, [(T_IS2G, SC_INV_S2G), (T_S2G, SC_S2G),
                                (T_CG, SC_CG), (T_IS2U, SC_INV_S2U)])

    def ph_p2(self, eng, E, V):
        t, wt = self.tick, self.wt
        b = dict(V)
        NTT = IC * NU
        SCy, SCs, V1t, V1b, S1s = {}, {}, {}, {}, {}
        sc = v1 = s1 = 0
        for n in range(NTT + 2):
            if n < NTT:
                sc += 1
                SCy[n] = sc
            if 1 <= n <= NTT:
                sc += 2
                SCs[n - 1] = sc
            if n < NTT:
                v1 += 1
                V1t[n] = v1
            if n >= 2:
                v1 += 3
                V1b[n - 2] = v1
                s1 += 16
                S1s[n - 2] = s1
        for n in range(NTT + 2):
            if n < NTT:
                g16 = self.stg[:, 512 * (n % 4):512 * (n % 4) + 512]
                yg = self.xt[:, 512 * (n % 4):512 * (n % 4) + 512]
                tg = self.xt[:, 2048 + 512 * (n % 4):2048 + 512 * (n % 4) + 512]
                ks = KS4[n % 4]
                if eng == "sync":
                    if n == 0:
                        wt(E, "S1", b["S1"])
                    if n >= 4:
                        wt(E, "SC", b["SC"] + SCy[n - 4])
                    t(E.dma_start(out=g16, in_=self.gt_sp[n // NU, n % NU, :, :]),
                      V, ks, 16)
                else:
                    t(None, V, ks, 16)
                if eng == "scalar":
                    wt(E, ks, b[ks] + 16 * (n // 4 + 1))
                    if n >= 4:
                        wt(E, "V1", b["V1"] + V1t[n - 4])
                    ti = E.activation(yg, g16, AF.Copy, bias=C_MAG,
                                      scale=self.scal[:, SC_INV_S2G:SC_INV_S2G + 1])
                    t(ti, V, "SC", 1)
                else:
                    t(None, V, "SC", 1)
                if eng == "vector":
                    wt(E, "SC", b["SC"] + SCy[n])
                    ti = E.tensor_scalar(tg, yg, -C_MAG, None, ALU.add)
                    t(ti, V, "V1", 1)
                else:
                    t(None, V, "V1", 1)
            if 1 <= n <= NTT:
                k = n - 1
                sg = self.ewA[:, 2560:3072]
                ys = self.ewA[:, 512 * (k % 4):512 * (k % 4) + 512]
                tgk = self.xt[:, 2048 + 512 * (k % 4):2048 + 512 * (k % 4) + 512]
                if eng == "scalar":
                    wt(E, "V1", b["V1"] + V1t[k])
                    if k >= 4:
                        wt(E, "V1", b["V1"] + V1b[k - 4])
                    E.activation(sg, tgk, AF.Sigmoid,
                                 bias=self.cst[:, CZERO:CZERO + 1],
                                 scale=self.scal[:, SC_S2G:SC_S2G + 1])
                    ti = E.activation(ys, sg, AF.Copy, bias=C_MAG,
                                      scale=self.cst[:, C65536:C65536 + 1])
                    t(ti, V, "SC", 2)
                else:
                    t(None, V, "SC", 2)
            if n >= 2:
                k = n - 2
                ys = self.ewA[:, 512 * (k % 4):512 * (k % 4) + 512]
                ts5 = self.ewA[:, 2048:2560]
                tgk = self.xt[:, 2048 + 512 * (k % 4):2048 + 512 * (k % 4) + 512]
                m16 = self.stg[:, 2048 + 512 * (k % 4):2048 + 512 * (k % 4) + 512]
                if eng == "vector":
                    wt(E, "SC", b["SC"] + SCs[k])
                    if k >= 4:
                        wt(E, "S1", b["S1"] + S1s[k - 4])
                    E.tensor_scalar(ts5, ys, -C_MAG, 65535.0, ALU.add, ALU.min)
                    if k >= 1:
                        pm16 = self.stg[:, 2048 + 512 * ((k - 1) % 4):
                                        2048 + 512 * ((k - 1) % 4) + 512]
                        E.tensor_reduce(self.dqt[:, (k - 1) % 3:(k - 1) % 3 + 1],
                                        pm16, AX, ALU.max,
                                        apply_absolute_value=True)
                    else:
                        E.engine_nop()
                    E.scalar_tensor_tensor(m16, tgk, self.scal[:, SC_CG:SC_CG + 1],
                                           ts5, ALU.mult, ALU.mult)
                    if k >= 2:
                        ti = E.tensor_tensor(self.amx[:, 3:4], self.amx[:, 3:4],
                                             self.dqt[:, (k - 2) % 3:(k - 2) % 3 + 1],
                                             ALU.max)
                    else:
                        ti = E.engine_nop()
                    t(ti, V, "V1", 3)
                else:
                    t(None, V, "V1", 3)
                if eng == "gpsimd":
                    wt(E, "V1", b["V1"] + V1b[k])
                    ti = E.dma_start(out=self.m1_sp[k // NU, k % NU, :, :], in_=m16)
                    t(ti, V, "S1", 16)
                else:
                    t(None, V, "S1", 16)
        k = NTT - 1
        if eng == "vector":
            lm16 = self.stg[:, 2048 + 512 * (k % 4):2048 + 512 * (k % 4) + 512]
            E.tensor_reduce(self.dqt[:, k % 3:k % 3 + 1], lm16,
                            AX, ALU.max, apply_absolute_value=True)
            E.tensor_tensor(self.amx[:, 3:4], self.amx[:, 3:4],
                            self.dqt[:, (k - 1) % 3:(k - 1) % 3 + 1], ALU.max)
            E.drain()
            ti = E.tensor_tensor(self.amx[:, 3:4], self.amx[:, 3:4],
                                 self.dqt[:, k % 3:k % 3 + 1], ALU.max)
            t(ti, V, "V1", 1)
        else:
            t(None, V, "V1", 1)

    def ph_b4(self, eng, E, V):
        self._barrier(eng, E, V, self.amx[:, 3:4], 4, T_G4, None, None)
        self._scal_chain(eng, E, V, [
            ("ts2", T_G4, T_S4, 1.0 / QMAX, 1e-12),
            ("recip", T_S4, T_IS4),
            ("mul2", T_S4, T_S2U, T_C45),
        ])
        self._bcast(eng, E, V, [(T_IS4, SC_INV_S4), (T_C45, SC_C45)])

    def ph_p3(self, eng, E, V):
        t, wt = self.tick, self.wt
        b = dict(V)
        if eng == "sync":
            wt(E, "CC", 3)
            for i in range(IC):
                src = self.agd_out.rearrange("(c r) j -> c r j", c=NCORES)[
                    :, 128 * i:128 * i + 128, :].transpose([1, 0, 2])
                t(E.dma_start(out=self.bigA[:, 1024 * i:1024 * i + 1024].bitcast(
                    F16).rearrange("p (c j) -> p c j", c=NCORES), in_=src),
                  V, "L2", 16)
        else:
            for i in range(IC):
                t(None, V, "L2", 16)
        n = 0
        for m in range(IC):
            for u in range(NU):
                a16 = self.stg[:, 512 * (n % 4):512 * (n % 4) + 512]
                b16 = self.ewB[:, 512 * (n % 4):512 * (n % 4) + 512].bitcast(F16)
                m16 = self.stg[:, 2048 + 512 * (n % 4):2048 + 512 * (n % 4) + 512]
                y2 = self.xt[:, 512 * (n % 4):512 * (n % 4) + 512]
                yu = self.xt[:, 2048 + 512 * (n % 4):2048 + 512 * (n % 4) + 512]
                tu = self.ewA[:, 512 * (n % 4):512 * (n % 4) + 512]
                ks = KS4[n % 4]
                if eng == "sync":
                    if n == 0:
                        wt(E, "S1", b["S1"])
                    if n >= 4:
                        wt(E, "SC", b["SC"] + (n - 4) + 1)
                    t(E.dma_start(out=a16, in_=self.m1_sp[m, u, :, :]), V, ks, 16)
                    t(E.dma_start(out=b16, in_=self.up_sp[m, u, :, :]), V, ks, 16)
                else:
                    t(None, V, ks, 16)
                    t(None, V, ks, 16)
                if eng == "scalar":
                    wt(E, ks, b[ks] + 32 * (n // 4 + 1))
                    if n >= 4:
                        wt(E, "V1", b["V1"] + 3 * (n - 4) + 2)
                    E.activation(y2, a16, AF.Copy, bias=C_MAG,
                                 scale=self.scal[:, SC_INV_S4:SC_INV_S4 + 1])
                    ti = E.activation(yu, b16, AF.Copy, bias=C_MAG,
                                      scale=self.scal[:, SC_INV_S2U:SC_INV_S2U + 1])
                    t(ti, V, "SC", 1)
                else:
                    t(None, V, "SC", 1)
                if eng == "vector":
                    wt(E, "SC", b["SC"] + n + 1)
                    if n >= 4:
                        wt(E, "S1", b["S1"] + 16 * (n - 4 + 1))
                    E.tensor_scalar(tu, yu, -C_MAG, self.scal[:, SC_C45:SC_C45 + 1],
                                    ALU.add, ALU.mult)
                    if n >= 1:
                        pm16 = self.stg[:, 2048 + 512 * ((n - 1) % 4):
                                        2048 + 512 * ((n - 1) % 4) + 512]
                        E.tensor_reduce(self.dqt[:, (n - 1) % 3:(n - 1) % 3 + 1],
                                        pm16, AX, ALU.max,
                                        apply_absolute_value=True)
                    E.scalar_tensor_tensor(m16, y2, -C_MAG, tu, ALU.add, ALU.mult)
                    if n >= 2:
                        ti = E.tensor_tensor(self.amx[:, 4:5], self.amx[:, 4:5],
                                             self.dqt[:, (n - 2) % 3:(n - 2) % 3 + 1],
                                             ALU.max)
                    else:
                        ti = E.engine_nop()
                    t(ti, V, "V1", 3)
                else:
                    t(None, V, "V1", 3)
                if eng == "gpsimd":
                    wt(E, "V1", b["V1"] + 3 * n + 3)
                    ti = E.dma_start(out=self.m2_sp[m, u, :, :], in_=m16)
                    t(ti, V, "S1", 16)
                else:
                    t(None, V, "S1", 16)
                n += 1
        if eng == "vector":
            lm16 = self.stg[:, 2048 + 512 * ((n - 1) % 4):
                            2048 + 512 * ((n - 1) % 4) + 512]
            E.tensor_reduce(self.dqt[:, (n - 1) % 3:(n - 1) % 3 + 1], lm16,
                            AX, ALU.max, apply_absolute_value=True)
            E.tensor_tensor(self.amx[:, 4:5], self.amx[:, 4:5],
                            self.dqt[:, (n - 2) % 3:(n - 2) % 3 + 1], ALU.max)
            E.drain()
            ti = E.tensor_tensor(self.amx[:, 4:5], self.amx[:, 4:5],
                                 self.dqt[:, (n - 1) % 3:(n - 1) % 3 + 1], ALU.max)
            t(ti, V, "V1", 1)
        else:
            t(None, V, "V1", 1)

    def ph_b5(self, eng, E, V):
        self._barrier(eng, E, V, self.amx[:, 4:5], 5, T_G5, None, None)
        self._scal_chain(eng, E, V, [
            ("ts2", T_G5, T_S5V, 1.0 / QMAX, 1e-12),
            ("recip", T_S5V, T_IS5),
        ])
        self._bcast(eng, E, V, [(T_IS5, SC_INV_S5), (T_S5V, SC_S5)])

    def ph_down(self, eng, E, V):
        t, wt = self.tick, self.wt
        b = dict(V)
        for u in range(NU):
            lb = b["L1"] + 28 * 16 * u
            osl = u % 2
            for i2 in range(IC):
                q = IC * u + i2
                m16 = self.stg[:, 512 * (i2 % 4):512 * (i2 % 4) + 512]
                yo = self.ewA[:, 512 * (i2 % 4):512 * (i2 % 4) + 512]
                oq = self.vaug[:, 12288 * osl + 512 * i2:
                               12288 * osl + 512 * i2 + 512].bitcast(F16)
                ks = KS4[q % 4]
                if eng == "sync":
                    if q == 0:
                        wt(E, "S1", b["S1"])
                    if q >= 4:
                        wt(E, "SC", b["SC"] + (q - 4) + 1)
                    if i2 == 0 and u >= 2:
                        wt(E, "T1", b["T1"] + 4 * (u - 1))
                    t(E.dma_start(out=m16, in_=self.m2_sp[i2, u, :, :]), V, ks, 16)
                else:
                    t(None, V, ks, 16)
                if eng == "scalar":
                    wt(E, ks, b[ks] + 16 * (q // 4 + 1))
                    if q >= 4:
                        wt(E, "V1", b["V1"] + (q - 4) + 1)
                    ti = E.activation(yo, m16, AF.Copy, bias=C_MAG,
                                      scale=self.scal[:, SC_INV_S5:SC_INV_S5 + 1])
                    t(ti, V, "SC", 1)
                else:
                    t(None, V, "SC", 1)
                if eng == "vector":
                    wt(E, "SC", b["SC"] + q + 1)
                    ti = E.tensor_scalar(oq, yo, -C_MAG, None, ALU.add)
                    t(ti, V, "V1", 1)
                else:
                    t(None, V, "V1", 1)
            if eng == "sync":
                for r in range(4):
                    gi_ = 4 * u + r
                    kx = "KE" if gi_ % 2 == 0 else "KF"
                    if gi_ >= 2:
                        wt(E, "V2", b["V2"] + (gi_ - 2) + 1)
                    xin = self.xt[:, 1024 * (r % 2):1024 * (r % 2) + 1024]
                    t(E.dma_start(out=xin, in_=self.xp_sp[4 * u + r, :, :]),
                      V, kx, 16)
            else:
                for r in range(4):
                    gi_ = 4 * u + r
                    kx = "KE" if gi_ % 2 == 0 else "KF"
                    t(None, V, kx, 16)
            for r in range(4):
                gi = 4 * u + r
                pa = self.pb[2 * (r % 2)]
                pc = self.pb[2 * (r % 2) + 1]
                xin = self.xt[:, 1024 * (r % 2):1024 * (r % 2) + 1024]
                ot = self.xt[:, 2048 + 1024 * (r % 2):2048 + 1024 * (r % 2) + 1024]
                if eng == "tensor":
                    wt(E, "V1", b["V1"] + IC * (u + 1))
                    wt(E, "L2", b["L2"])
                    if gi >= 2:
                        wt(E, "V2", b["V2"] + (gi - 2) + 1)
                    for i2 in range(IC):
                        lhs = self.vaug[:, 12288 * osl + 512 * i2 + 128 * r:
                                        12288 * osl + 512 * i2 + 128 * r + 128]
                        E.matmul(pa[:, :], lhs.bitcast(F16),
                                 self.bigA[:, 1024 * i2:1024 * i2 + 512].bitcast(F16),
                                 start=(i2 == 0), stop=(i2 == IC - 1))
                        ti = E.matmul(pc[:, :], lhs.bitcast(F16),
                                      self.bigA[:, 1024 * i2 + 512:
                                                1024 * i2 + 1024].bitcast(F16),
                                      start=(i2 == 0), stop=(i2 == IC - 1))
                    t(ti, V, "T1", 1)
                else:
                    t(None, V, "T1", 1)
                if eng == "vector":
                    wt(E, "T1", b["T1"] + 4 * u + r + 1)
                    kx = "KE" if gi % 2 == 0 else "KF"
                    wt(E, kx, b[kx] + 16 * (gi // 2 + 1))
                    if gi >= 2:
                        wt(E, "S1", b["S1"] + 16 * (gi - 1))
                    E.scalar_tensor_tensor(ot[:, 0:512], pa[:, :],
                                           self.scal[:, SC_S5:SC_S5 + 1],
                                           xin[:, 0:512], ALU.mult, ALU.add)
                    ti = E.scalar_tensor_tensor(ot[:, 512:1024], pc[:, :],
                                                self.scal[:, SC_S5:SC_S5 + 1],
                                                xin[:, 512:1024], ALU.mult, ALU.add)
                    t(ti, V, "V2", 1)
                else:
                    t(None, V, "V2", 1)
                if eng == "gpsimd":
                    wt(E, "V2", b["V2"] + gi + 1)
                    ti = E.dma_start(
                        out=self.out[512 * u + 128 * r:512 * u + 128 * r + 128, :],
                        in_=ot)
                    t(ti, V, "S1", 16)
                else:
                    t(None, V, "S1", 16)

    def ph_fin(self, eng, E, V):
        t = self.tick
        if "oq_dbg" in self.dbg:
            if eng == "gpsimd":
                ti = E.dma_start(out=self.oq_dbg[:, :],
                                 in_=self.vaug[:, 12288:24576].bitcast(F16))
                t(ti, V, "S1", 16)
            else:
                t(None, V, "S1", 16)
        if "sc_dbg" in self.dbg:
            if eng == "gpsimd":
                t(E.dma_start(out=self.sc_dbg[:, 0:16], in_=self.scal[:, :]), V, "S1", 16)
                t(E.dma_start(out=self.sc_dbg[0:1, 16:40], in_=self.sct[:, :]), V, "S1", 16)
                ti = E.dma_start(out=self.sc_dbg[:, 40:48], in_=self.amx[:, :])
                t(ti, V, "S1", 16)
            else:
                for _ in range(3):
                    t(None, V, "S1", 16)
        if "agu_out" in self.dbg:
            if eng == "gpsimd":
                self.wt(E, "CC", 3)
                t(E.dma_start(out=self.agu_dbg[:, :], in_=self.agu_out[:, :]),
                  V, "S1", 16)
                t(E.dma_start(out=self.agi_dbg[:, :], in_=self.agu_in[:, :]),
                  V, "S1", 16)
            else:
                t(None, V, "S1", 16)
                t(None, V, "S1", 16)
        if eng == "gpsimd":
            E.wait_ge(self.sems["S1"], V["S1"])

    PHASES = ["init", "dequant", "stats1", "norm1", "qkv", "kv", "ksbd", "attn",
              "ln2", "xq", "up", "gt", "b2", "p2", "b4", "p3", "b5", "down"]

    def build(self, upto=None):
        seq = [
            ("init", self.ph_init, ()),
            ("dequant", self.ph_dequant, ()),
            ("stats1", self.ph_stats1, ()),
            ("norm1", self.ph_norm1, ()),
            ("qkv", self.ph_qkv, ()),
            ("kv", self.ph_kv, ()),
            ("ksbd", self.ph_ksbd, ()),
            ("attn", self.ph_attn, ()),
            ("ln2", self.ph_ln2_derive, ()),
            ("xq", self.ph_xq, ()),
            ("up", self.ph_upgate, ("up",)),
            ("gt", self.ph_upgate, ("gt",)),
            ("b2", self.ph_b2, ()),
            ("p2", self.ph_p2, ()),
            ("b4", self.ph_b4, ()),
            ("p3", self.ph_p3, ()),
            ("b5", self.ph_b5, ()),
            ("down", self.ph_down, ()),
        ]
        if "xn_dbg" in self.dbg:
            i = [n for n, _, _ in seq].index("norm1") + 1
            seq.insert(i, ("xndbg", self.ph_xndbg, ()))
        with self.es:
            for name, fn, args in seq:
                self.phase(fn, *args)
                if upto is not None and name == upto:
                    break
            self.phase(self.ph_fin)
        return self.nc


def make_in_maps(L0_attn_hidden_states, L0_attn_Wq, L0_attn_Wk, L0_attn_Wv,
                 L0_attn_Wo, L0_attn_Wf, L0_ffn_w_gate, L0_ffn_w_up,
                 L0_ffn_w_down):
    x = np.ascontiguousarray(np.asarray(L0_attn_hidden_states, dtype=np.float32))
    wqT = np.ascontiguousarray(np.asarray(L0_attn_Wq, np.float32).T)
    wkT = np.ascontiguousarray(np.asarray(L0_attn_Wk, np.float32).T)
    wvT = np.ascontiguousarray(np.asarray(L0_attn_Wv, np.float32).T)
    woT = np.ascontiguousarray(np.asarray(L0_attn_Wo, np.float32).T)
    wfT = np.ascontiguousarray(np.asarray(L0_attn_Wf, np.float32).T)
    wup = np.ascontiguousarray(np.asarray(L0_ffn_w_up, np.float32))
    wgt = np.ascontiguousarray(np.asarray(L0_ffn_w_gate, np.float32))
    wdn = np.ascontiguousarray(np.asarray(L0_ffn_w_down, np.float32))
    identbf = np.eye(128, dtype=ml_dtypes.bfloat16)
    identf32 = np.eye(128, dtype=np.float32)
    selmat = np.kron(np.eye(16, dtype=np.float32),
                     np.ones((1, 64), dtype=np.float32))
    identf16 = np.eye(128, dtype=np.float16)
    in_maps = []
    for c in range(NCORES):
        in_maps.append({
            "x": np.ascontiguousarray(x[c]),
            "wqT": wqT, "wkT": wkT, "wvT": wvT, "woT": woT, "wfT": wfT,
            "wup_nat": np.ascontiguousarray(wup[384 * c:384 * c + 384, :]),
            "wgt_nat": np.ascontiguousarray(wgt[384 * c:384 * c + 384, :]),
            "wdn_nat": np.ascontiguousarray(wdn[128 * c:128 * c + 128, :]),
            "identbf": identbf, "identf32": identf32, "selmat": selmat,
            "identf16": identf16,
        })
    return in_maps


def kernel(**inputs):
    in_maps = make_in_maps(**inputs)
    nc = Build().build()
    res = run_bass_kernel_spmd(nc, in_maps, core_ids=list(range(NCORES)))
    out = np.stack([res.results[c]["out"] for c in range(NCORES)], axis=0)
    return out.astype(np.float32)


# version 03
# speedup vs baseline: 1.2696x; 1.0633x over previous
import numpy as np
import ml_dtypes
from contextlib import ExitStack

import concourse.bass as bass
import concourse.mybir as mybir
from concourse.bass_utils import run_bass_kernel_spmd

F32 = mybir.dt.float32
BF16 = mybir.dt.bfloat16
F16 = mybir.dt.float16
AF = mybir.ActivationFunctionType
ALU = mybir.AluOpType
AX = mybir.AxisListType.X

NCORES = 8
S, D, H, HD, FD, I = 4096, 1024, 16, 64, 256, 3072
NT = S // 128
NU = S // 512
IC = I // 128
C_MAG = 12582912.0
QMAX = 32767.0
ENGS = ("sync", "tensor", "scalar", "vector", "gpsimd")

CZERO, CONE, CNEG1, CC_, CNC, CEPS6, CEPS5, C65536 = range(8)
SC_INV_S1, SC_S1, SC_INV_S2G, SC_S2G, SC_CG, SC_INV_S2U, SC_C45, \
    SC_INV_S4, SC_INV_S5, SC_S5 = range(10)
T_G1, T_S1, T_IS1, T_GU, T_GG, T_S2U, T_IS2U, T_S2G, T_IS2G, T_CG, \
    T_G4, T_S4, T_IS4, T_C45, T_G5, T_S5V, T_IS5 = range(17)

SEM_NAMES = ("L1", "L2", "S1", "T1", "T2", "T3", "SC", "V1", "V2", "V3", "CC",
             "KA", "KB", "KC", "KD", "KE", "KF")
KS4 = ("KA", "KB", "KC", "KD")


class Build:
    def __init__(self, dbg=()):
        nc = bass.Bass("TRN2", target_bir_lowering=False)
        self.nc = nc
        self.es = ExitStack()
        self.dbg = set(dbg)
        es = self.es

        dt = nc.dram_tensor
        self.x = dt("x", [S, D], F32, kind="ExternalInput")
        self.wqT = dt("wqT", [D, D], F32, kind="ExternalInput")
        self.wkT = dt("wkT", [D, D], F32, kind="ExternalInput")
        self.wvT = dt("wvT", [D, D], F32, kind="ExternalInput")
        self.woT = dt("woT", [D, D], F32, kind="ExternalInput")
        self.wfT = dt("wfT", [HD, FD], F32, kind="ExternalInput")
        self.wup_nat = dt("wup_nat", [I // 8, D], F32, kind="ExternalInput")
        self.wgt_nat = dt("wgt_nat", [I // 8, D], F32, kind="ExternalInput")
        self.wdn_nat = dt("wdn_nat", [D // 8, I], F32, kind="ExternalInput")
        self.identbf = dt("identbf", [128, 128], BF16, kind="ExternalInput")
        self.identf32 = dt("identf32", [128, 128], F32, kind="ExternalInput")
        self.selmat = dt("selmat", [16, 1024], F32, kind="ExternalInput")
        self.identf16 = dt("identf16", [128, 128], F16, kind="ExternalInput")
        self.out = dt("out", [S, D], F32, kind="ExternalOutput")

        def scratch(name, shape, dtp):
            kind = "ExternalOutput" if name in self.dbg else "Internal"
            return dt(name, shape, dtp, kind=kind)
        self.qT_sp = scratch("qT_sp", [8, 128, S], BF16)
        self.kT_sp = scratch("kT_sp", [8, 128, S], BF16)
        self.xp_sp = scratch("xp_sp", [NT, 128, D], F32)
        self.up_sp = scratch("up_sp", [IC, NU, 128, 512], F16)
        self.gt_sp = scratch("gt_sp", [IC, NU, 128, 512], F16)
        self.m1_sp = scratch("m1_sp", [IC, NU, 128, 512], F16)
        self.m2_sp = scratch("m2_sp", [IC, NU, 128, 512], F16)
        self.agu_in = dt("agu_in", [D, I // 8], F16)
        self.agu_out = dt("agu_out", [NCORES * D, I // 8], F16)
        if "agu_out" in self.dbg:
            self.agu_dbg = dt("agu_dbg", [NCORES * D, I // 8], F16,
                              kind="ExternalOutput")
            self.agi_dbg = dt("agi_dbg", [D, I // 8], BF16,
                              kind="ExternalOutput")
        self.agg_in = dt("agg_in", [D, I // 8], F16)
        self.agg_out = dt("agg_out", [NCORES * D, I // 8], F16)
        self.agd_in = dt("agd_in", [I, D // 8], F16)
        self.agd_out = dt("agd_out", [NCORES * I, D // 8], F16)
        self.ar_in = {k: dt(f"ar{k}_in", [1, 2], F32) for k in (1, 2, 4, 5)}
        if "xn_dbg" in self.dbg:
            self.xn_dbg = dt("xn_dbg", [128, 32768], BF16, kind="ExternalOutput")
            self.st_dbg = dt("st_dbg", [128, 64], F32, kind="ExternalOutput")
        if "sc_dbg" in self.dbg:
            self.sc_dbg = dt("sc_dbg", [128, 48], F32, kind="ExternalOutput")
        if "oq_dbg" in self.dbg:
            self.oq_dbg = dt("oq_dbg", [128, 12288], F16, kind="ExternalOutput")
        self.ar_out = {k: dt(f"ar{k}_out", [1, 2], F32) for k in (1, 2, 4, 5)}

        def sb(name, shape, dtp):
            return es.enter_context(nc.sbuf_tensor(name, shape, dtp))
        self.bigA = sb("bigA", [128, 32768], BF16)
        self.vaug = sb("vaug", [128, 33280], BF16)
        self.strm = sb("strm", [128, 8192], BF16)
        self.xt = sb("xt", [128, 4096], F32)
        self.ewA = sb("ewA", [128, 3072], F32)
        self.ewB = sb("ewB", [128, 4096], BF16)
        self.stg = sb("stg", [128, 4096], F16)
        self.kvh = sb("kvh", [128, 2080], BF16)
        self.ksbd = sb("ksbd", [128, 512], BF16)
        self.kvstg = sb("kvstg", [128, 1024], F32)
        self.kvstg_b = sb("kvstg_b", [128, 1024], BF16)
        self.wfb = sb("wfb", [128, 256], BF16)
        self.cst = sb("cst", [128, 8], F32)
        self.scal = sb("scal", [128, 16], F32)
        self.sct = sb("sct", [1, 24], F32)
        self.onesb = sb("onesb", [1, 512], BF16)
        self.onesf = sb("onesf", [1, 128], F32)
        self.idb = sb("idb", [128, 128], BF16)
        self.idf = sb("idf", [128, 128], F32)
        self.idh = sb("idh", [128, 128], F16)
        self.st1m = sb("st1m", [128, 32], F32)
        self.st1r = sb("st1r", [128, 32], F32)
        self.sum1 = sb("sum1", [128, 32], F32)
        self.sq1 = sb("sq1", [128, 32], F32)
        self.st2m = self.st1m
        self.st2r = self.st1r
        self.sum2 = self.sum1
        self.sq2 = self.sq1
        self.abuf = sb("abuf", [128, 32], F32)
        self.bcbuf = sb("bcbuf", [128, 32], F32)
        self.amx = sb("amx", [128, 8], F32)
        self.dqt = sb("dqt", [128, 64], F32)

        self.pb = [es.enter_context(nc.psum_tensor(f"pb{i}", [128, 512], F32))
                   for i in range(6)]
        self.pt = es.enter_context(nc.psum_tensor("pt", [128, 1024], BF16))
        self.pt2 = es.enter_context(nc.psum_tensor("pt2", [128, 1024], BF16))

        nc.const_aps.register(F32, 0.0, self.cst[:, CZERO:CZERO + 1])
        nc.const_aps.register(F32, 1.0, self.cst[:, CONE:CONE + 1])
        nc.const_aps.register(F32, -1.0, self.cst[:, CNEG1:CNEG1 + 1])
        nc.const_aps.register(F32, C_MAG, self.cst[:, CC_:CC_ + 1])
        nc.const_aps.register(F32, -C_MAG, self.cst[:, CNC:CNC + 1])
        nc.const_aps.register(F32, 1e-5, self.cst[:, CEPS5:CEPS5 + 1])
        nc.const_aps.register(F32, 65536.0, self.cst[:, C65536:C65536 + 1])

        self.sems = {}
        self.semv = {}
        for name in SEM_NAMES:
            self.sems[name] = es.enter_context(nc.semaphore(name))
            self.semv[name] = 0

    def phase(self, fn, *args):
        base = dict(self.semv)
        with self.nc.Block() as blk:
            for eng in ENGS:
                V = dict(base)
                def body(E, eng=eng, V=V):
                    fn(eng, E, V, *args)
                getattr(blk, eng)(body)
        V = dict(base)
        fn(None, None, V, *args)
        self.semv = V

    def tick(self, instr, V, s, n=1):
        V[s] += n
        if instr is not None:
            instr.then_inc(self.sems[s], n)

    def wt(self, E, s, val):
        if E is not None:
            E.wait_ge(self.sems[s], val)


    def ph_init(self, eng, E, V):
        t, wt = self.tick, self.wt
        if eng == "sync":
            t(E.dma_start(out=self.idb[:, :], in_=self.identbf[:, :]), V, "L1", 16)
            t(E.dma_start(out=self.idf[:, :], in_=self.identf32[:, :]), V, "L1", 16)
            t(E.dma_start(out=self.idh[:, :], in_=self.identf16[:, :]), V, "L1", 16)
            t(E.dma_start(out=self.ewA[0:64, 0:256], in_=self.wfT[:, :]), V, "L1", 16)
        else:
            for _ in range(4):
                t(None, V, "L1", 16)
        if eng == "vector":
            for col, val in ((CZERO, 0.0), (CONE, 1.0), (CNEG1, -1.0), (CC_, C_MAG),
                             (CNC, -C_MAG), (CEPS6, 1e-6), (CEPS5, 1e-5),
                             (C65536, 65536.0)):
                E.memset(self.cst[:, col:col + 1], float(val))
            E.memset(self.onesb[:, :], 1.0)
            E.memset(self.onesf[:, :], 1.0)
            E.memset(self.amx[:, :], 0.0)
            E.memset(self.vaug[:, 0:NT * 1040].rearrange(
                "p (t j) -> p t j", j=65)[:, :, 64:65], 1.0)
            wt(E, "L1", V["L1"])
            E.tensor_copy(self.wfb[0:64, :], self.ewA[0:64, 0:256])
            ti = E.tensor_copy(self.wfb[64:128, :], self.ewA[0:64, 0:256])
            t(ti, V, "V1", 1)
        else:
            t(None, V, "V1", 1)

    def ph_dequant(self, eng, E, V):
        t, wt = self.tick, self.wt
        jobs = [("up", self.wup_nat, 128 * r, 0, r) for r in range(3)] + \
               [("gt", self.wgt_nat, 128 * r, 0, r) for r in range(3)] + \
               [("dn", self.wdn_nat, 0, 1024 * c3, c3) for c3 in range(3)]
        import os
        if os.environ.get("DQ_ONE"):
            jobs = jobs[:1]
        b = dict(V)
        for j, (kind, src, ro, co, idx) in enumerate(jobs):
            wsl = self.ewA[:, 0:1024]
            q1 = self.ewA[:, 1024:2048]
            q2 = self.ewA[:, 2048:3072]
            dq = self.ewB[:, 0:1024].bitcast(F16)
            stage = self.ewB[:, 1024:2048].bitcast(F16)
            if eng == "sync":
                wt(E, "V1", b["V1"] + j)
                t(E.dma_start(out=wsl, in_=src[ro:ro + 128, co:co + 1024]), V, "L1", 16)
            else:
                t(None, V, "L1", 16)
            if eng == "vector":
                wt(E, "L1", b["L1"] + 16 * (j + 1))
                wt(E, "T1", b["T1"] + j)
                bm = self.dqt[:, 0:32]
                sc = self.dqt[:, 32:64]
                E.tensor_reduce(bm.rearrange("p (b o) -> p b o", o=1),
                                wsl.rearrange("p (b u) -> p b u", u=32),
                                AX, ALU.max, apply_absolute_value=True)
                E.drain()
                E.tensor_scalar(sc, bm, 1.0 / 7.0, 1e-12, ALU.mult, ALU.max)
                E.drain()
                E.reciprocal(bm, sc)
                E.drain()
                E.tensor_tensor(q1.rearrange("p (b u) -> p b u", u=32),
                                wsl.rearrange("p (b u) -> p b u", u=32),
                                bm.unsqueeze(2).broadcast_to([128, 32, 32]),
                                ALU.mult)
                E.drain()
                E.tensor_scalar(q2, q1, C_MAG, -C_MAG, ALU.add, ALU.add)
                E.drain()
                ti = E.tensor_tensor(dq.rearrange("p (b u) -> p b u", u=32),
                                     q2.rearrange("p (b u) -> p b u", u=32),
                                     sc.unsqueeze(2).broadcast_to([128, 32, 32]),
                                     ALU.mult)
                t(ti, V, "V1", 1)
            else:
                t(None, V, "V1", 1)
            if eng == "tensor":
                wt(E, "V1", b["V1"] + j + 1)
                wt(E, "SC", b["SC"] + j)
                for kk in range(8):
                    ti = E.transpose(self.pt[:, 128 * kk:128 * kk + 128].bitcast(F16),
                                     dq[:, 128 * kk:128 * kk + 128], self.idh[:, :])
                t(ti, V, "T1", 1)
            else:
                t(None, V, "T1", 1)
            if eng == "scalar":
                wt(E, "T1", b["T1"] + j + 1)
                wt(E, "S1", b["S1"] + 128 * j)
                ti = E.activation(stage, self.pt[:, :].bitcast(F16), AF.Copy,
                                  bias=0.0, scale=self.cst[:, CONE:CONE + 1])
                t(ti, V, "SC", 1)
            else:
                t(None, V, "SC", 1)
            if eng == "gpsimd":
                wt(E, "SC", b["SC"] + j + 1)
                dst = {"up": self.agu_in, "gt": self.agg_in, "dn": self.agd_in}[kind]
                for kk in range(8):
                    blkv = stage[:, 128 * kk:128 * kk + 128]
                    if kind in ("up", "gt"):
                        ti = E.dma_start(out=dst[128 * kk:128 * kk + 128,
                                                 128 * idx:128 * idx + 128], in_=blkv)
                    else:
                        ti = E.dma_start(out=dst[1024 * idx + 128 * kk:
                                                 1024 * idx + 128 * kk + 128, 0:128],
                                         in_=blkv)
                    t(ti, V, "S1", 16)
            else:
                for kk in range(8):
                    t(None, V, "S1", 16)
        import os
        if os.environ.get("NO_AG"):
            return
        if eng == "gpsimd":
            wt(E, "S1", b["S1"] + 128 * 9)
            for (i_, o_) in ((self.agu_in, self.agu_out), (self.agg_in, self.agg_out),
                            (self.agd_in, self.agd_out)):
                ti = E.collective_compute("AllGather", ALU.bypass,
                                          replica_groups=[list(range(NCORES))],
                                          ins=[i_.ap().opt()], outs=[o_.ap().opt()])
                t(ti, V, "CC", 1)
            wt(E, "CC", V["CC"])
        else:
            for _ in range(3):
                t(None, V, "CC", 1)

    def ph_stats1(self, eng, E, V):
        t, wt = self.tick, self.wt
        b = dict(V)
        for i in range(NT):
            xin = self.xt[:, 1024 * (i % 4):1024 * (i % 4) + 1024]
            ks = KS4[i % 4]
            if eng == "sync":
                if i >= 4:
                    wt(E, "V1", b["V1"] + (i - 4) + 1)
                t(E.dma_start(out=xin, in_=self.x[128 * i:128 * i + 128, :]),
                  V, ks, 16)
            else:
                t(None, V, ks, 16)
            if eng == "vector":
                wt(E, ks, b[ks] + 16 * (i // 4 + 1))
                E.tensor_tensor(self.ewA[:, 0:1024], xin, xin, ALU.mult)
                E.tensor_reduce(self.sum1[:, i:i + 1], xin, AX, ALU.add)
                ti = E.tensor_reduce(self.sq1[:, i:i + 1], self.ewA[:, 0:1024],
                                     AX, ALU.add)
                t(ti, V, "V1", 1)
            else:
                t(None, V, "V1", 1)
        if eng == "vector":
            E.tensor_scalar(self.st1m[:, :], self.sum1[:, :], 1.0 / D, None, ALU.mult)
            E.tensor_scalar(self.ewA[:, 0:32], self.sq1[:, :], 1.0 / D, None, ALU.mult)
            E.tensor_tensor(self.ewA[:, 32:64], self.st1m[:, :], self.st1m[:, :],
                            ALU.mult)
            E.drain()
            ti = E.tensor_tensor(self.ewA[:, 0:32], self.ewA[:, 0:32],
                                 self.ewA[:, 32:64], ALU.subtract)
            t(ti, V, "V1", 1)
        else:
            t(None, V, "V1", 1)
        if eng == "scalar":
            wt(E, "V1", V["V1"])
            ti = E.activation(self.ewA[:, 64:96], self.ewA[:, 0:32], AF.Sqrt,
                              bias=self.cst[:, CEPS5:CEPS5 + 1],
                              scale=self.cst[:, CONE:CONE + 1])
            t(ti, V, "SC", 1)
        else:
            t(None, V, "SC", 1)
        if eng == "vector":
            wt(E, "SC", V["SC"])
            ti = E.reciprocal(self.st1r[:, :], self.ewA[:, 64:96])
            t(ti, V, "V1", 1)
        else:
            t(None, V, "V1", 1)

    def ph_norm1(self, eng, E, V):
        t, wt = self.tick, self.wt
        b = dict(V)
        for i in range(NT):
            xin = self.xt[:, 1024 * (i % 4):1024 * (i % 4) + 1024]
            xnb = self.ewB[:, 1024 * (i % 4):1024 * (i % 4) + 1024]
            ptt = self.pt if i % 2 == 0 else self.pt2
            ks = KS4[i % 4]
            if eng == "sync":
                if i >= 4:
                    wt(E, "V1", b["V1"] + 2 * (i - 4) + 1)
                t(E.dma_start(out=xin, in_=self.x[128 * i:128 * i + 128, :]),
                  V, ks, 16)
            else:
                t(None, V, ks, 16)
            if eng == "vector":
                wt(E, ks, b[ks] + 16 * (i // 4 + 1))
                if i >= 4:
                    wt(E, "T1", b["T1"] + (i - 4) + 1)
                ti = E.tensor_scalar(xnb, xin, self.st1m[:, i:i + 1],
                                     self.st1r[:, i:i + 1], ALU.subtract, ALU.mult)
                t(ti, V, "V1", 1)
            else:
                t(None, V, "V1", 1)
            if eng == "tensor":
                wt(E, "V1", b["V1"] + 2 * i + 1)
                if i >= 2:
                    wt(E, "V1", b["V1"] + 2 * (i - 2) + 2)
                for kk in range(8):
                    ti = E.transpose(ptt[:, 128 * kk:128 * kk + 128],
                                     xnb[:, 128 * kk:128 * kk + 128], self.idb[:, :])
                t(ti, V, "T1", 1)
            else:
                t(None, V, "T1", 1)
            if eng == "vector":
                wt(E, "T1", b["T1"] + i + 1)
                ti = E.tensor_copy(
                    self.bigA[:, :].rearrange("p (k s) -> p k s", k=8)[
                        :, :, 128 * i:128 * i + 128],
                    ptt[:, :].rearrange("p (k s) -> p k s", k=8))
                t(ti, V, "V1", 1)
            else:
                t(None, V, "V1", 1)

    def ph_xndbg(self, eng, E, V):
        t = self.tick
        if eng == "gpsimd":
            t(E.dma_start(out=self.xn_dbg[:, :], in_=self.bigA[:, :]), V, "S1", 16)
            t(E.dma_start(out=self.st_dbg[:, 0:32], in_=self.st1m[:, :]), V, "S1", 16)
            ti = E.dma_start(out=self.st_dbg[:, 32:64], in_=self.st1r[:, :])
            t(ti, V, "S1", 16)
            E.wait_ge(self.sems["S1"], V["S1"])
        else:
            for _ in range(3):
                t(None, V, "S1", 16)

    def ph_qkv(self, eng, E, V):
        t, wt = self.tick, self.wt
        for wi, (wsrc, spill) in enumerate(((self.wqT, self.qT_sp),
                                            (self.wkT, self.kT_sp),
                                            (self.wvT, None))):
            b = dict(V)
            for kk in range(8):
                wstage = self.ewA[:, 1024 * (kk % 2):1024 * (kk % 2) + 1024]
                ks = KS4[kk % 2]
                if eng == "sync":
                    wt(E, "V1", max(0, b["V1"] + kk - 1))
                    t(E.dma_start(out=wstage, in_=wsrc[128 * kk:128 * kk + 128, :]),
                      V, ks, 16)
                else:
                    t(None, V, ks, 16)
                if eng == "vector":
                    wt(E, ks, b[ks] + 16 * (kk // 2 + 1))
                    if kk == 0:
                        wt(E, "T1", b["T1"])
                    ti = E.tensor_copy(self.strm[:, 1024 * kk:1024 * kk + 1024],
                                       wstage)
                    t(ti, V, "V1", 1)
                else:
                    t(None, V, "V1", 1)
            if spill is not None:
                n = 0
                for m in range(8):
                    for u in range(8):
                        pbank = self.pb[n % 2]
                        if eng == "tensor":
                            wt(E, "V1", b["V1"] + 8)
                            wt(E, "SC", max(0, b["SC"] + (n - 2) + 1))
                            for kk in range(8):
                                ti = E.matmul(
                                    pbank[:, :],
                                    self.strm[:, 1024 * kk + 128 * m:
                                              1024 * kk + 128 * m + 128],
                                    self.bigA[:, 4096 * kk + 512 * u:
                                              4096 * kk + 512 * u + 512],
                                    start=(kk == 0), stop=(kk == 7))
                            t(ti, V, "T1", 1)
                        else:
                            t(None, V, "T1", 1)
                        stgb = self.ewB[:, 512 * (n % 4):512 * (n % 4) + 512]
                        if eng == "scalar":
                            wt(E, "T1", b["T1"] + n + 1)
                            if n >= 4:
                                wt(E, "S1", b["S1"] + 16 * (n - 4 + 1))
                            ti = E.activation(stgb, pbank[:, :], AF.Copy, bias=0.0,
                                              
                                              scale=self.cst[:, CONE:CONE + 1])
                            t(ti, V, "SC", 1)
                        else:
                            t(None, V, "SC", 1)
                        if eng == "gpsimd":
                            wt(E, "SC", b["SC"] + n + 1)
                            ti = E.dma_start(out=spill[m, :, 512 * u:512 * u + 512],
                                             in_=stgb)
                            t(ti, V, "S1", 16)
                        else:
                            t(None, V, "S1", 16)
                        n += 1
            else:
                for st in range(NT):
                    pba = self.pb[2 * (st % 2)]
                    pbb = self.pb[2 * (st % 2) + 1]
                    if eng == "tensor":
                        wt(E, "V1", b["V1"] + 8)
                        if st < 2:
                            wt(E, "SC", b["SC"])
                        else:
                            wt(E, "V1", b["V1"] + 8 + (st - 2) + 1)
                        for kk in range(8):
                            lhs = self.bigA[:, 4096 * kk + 128 * st:
                                            4096 * kk + 128 * st + 128]
                            E.matmul(pba[:, :], lhs,
                                     self.strm[:, 1024 * kk:1024 * kk + 512],
                                     start=(kk == 0), stop=(kk == 7))
                            ti = E.matmul(pbb[:, :], lhs,
                                          self.strm[:, 1024 * kk + 512:
                                                    1024 * kk + 1024],
                                          start=(kk == 0), stop=(kk == 7))
                        t(ti, V, "T1", 1)
                    else:
                        t(None, V, "T1", 1)
                    if eng == "vector":
                        wt(E, "T1", b["T1"] + st + 1)
                        dst = self.vaug[:, 1040 * st:1040 * st + 1040].rearrange(
                            "p (h j) -> p h j", j=65)[:, :, 0:64]
                        E.tensor_copy(dst[:, 0:8, :],
                                      pba[:, :].rearrange("p (h j) -> p h j", j=64))
                        ti = E.tensor_copy(dst[:, 8:16, :],
                                           pbb[:, :].rearrange("p (h j) -> p h j", j=64))
                        t(ti, V, "V1", 1)
                    else:
                        t(None, V, "V1", 1)

    def ph_kv(self, eng, E, V):
        t, wt = self.tick, self.wt
        B = dict(V)
        for g in range(4):
            b = dict(V)
            if eng == "sync":
                if g == 0:
                    wt(E, "S1", B["S1"])
                wt(E, "T1", B["T1"] + 32 * g)
                for cc in range(2):
                    t(E.dma_start(out=self.strm[:, 4096 * cc:4096 * cc + 4096],
                                  in_=self.kT_sp[2 * g + cc, :, :]),
                      V, KS4[cc], 16)
            else:
                for cc in range(2):
                    t(None, V, KS4[cc], 16)
            def emit_z1(tt):
                zsl = tt % 2
                pz = (self.pb[0], self.pb[1]) if zsl == 0 else (self.pb[2], self.pb[3])
                for hh in range(4):
                    head = 4 * g + hh
                    p = head % 2
                    cc = (head // 2) - 2 * g
                    bank = pz[hh // 2]
                    col = 256 * (hh % 2)
                    lhs = self.strm[64 * p:64 * p + 64,
                                    4096 * cc + 128 * tt:4096 * cc + 128 * tt + 128]
                    E.matmul(bank[:, col:col + 256], lhs,
                             self.wfb[64 * p:64 * p + 64, :],
                             start=(hh % 2 == 0), stop=False)
                    ti = E.matmul(bank[:, col:col + 256], self.onesb[0:1, 0:128],
                                  self.onesb[0:1, 0:256],
                                  start=False, stop=(hh % 2 == 1))
                return ti

            def emit_kv(tt):
                zsl = tt % 2
                kf = self.ewB[:, 1024 * zsl:1024 * zsl + 1024]
                for hh in range(4):
                    head = 4 * g + hh
                    bank = self.pb[4 + hh // 2]
                    col = 256 * (hh % 2)
                    lhs = self.vaug[:, 1040 * tt + 65 * head:
                                    1040 * tt + 65 * head + 65]
                    ti = E.matmul(bank[0:65, col:col + 256], lhs,
                                  kf[:, 256 * hh:256 * hh + 256],
                                  start=(tt == 0 and hh % 2 == 0),
                                  stop=(tt == NT - 1 and hh % 2 == 1))
                return ti

            for tt in range(NT):
                zsl = tt % 2
                pz = (self.pb[0], self.pb[1]) if zsl == 0 else (self.pb[2], self.pb[3])
                et = self.ewA[:, 1024 * zsl:1024 * zsl + 1024]
                kf = self.ewB[:, 1024 * zsl:1024 * zsl + 1024]
                if eng == "tensor":
                    if tt == 0:
                        wt(E, "KA", B["KA"] + 16 * (g + 1))
                        wt(E, "KB", B["KB"] + 16 * (g + 1))
                        wt(E, "V2", B["V2"] + g)
                    if tt >= 2:
                        wt(E, "V1", b["V1"] + (tt - 2) + 1)
                    ti = emit_z1(tt)
                    t(ti, V, "T1", 1)
                    if tt >= 1:
                        wt(E, "V1", b["V1"] + tt)
                        ti = emit_kv(tt - 1)
                        t(ti, V, "T2", 1)
                    if tt == NT - 1:
                        wt(E, "V1", b["V1"] + tt + 1)
                        ti = emit_kv(tt)
                        t(ti, V, "T2", 1)
                else:
                    t(None, V, "T1", 1)
                    if tt >= 1:
                        t(None, V, "T2", 1)
                    if tt == NT - 1:
                        t(None, V, "T2", 1)
                if eng == "scalar":
                    wt(E, "T1", b["T1"] + tt + 1)
                    E.activation(et[:, 0:512], pz[0][:, :], AF.Exp,
                                 bias=self.cst[:, CNEG1:CNEG1 + 1],
                                 scale=self.cst[:, CONE:CONE + 1])
                    ti = E.activation(et[:, 512:1024], pz[1][:, :], AF.Exp,
                                      bias=self.cst[:, CNEG1:CNEG1 + 1],
                                      scale=self.cst[:, CONE:CONE + 1])
                    t(ti, V, "SC", 1)
                else:
                    t(None, V, "SC", 1)
                if eng == "vector":
                    wt(E, "SC", b["SC"] + tt + 1)
                    if tt >= 2:
                        wt(E, "T2", b["T2"] + (tt - 2) + 1)
                    E.scalar_tensor_tensor(kf[:, 0:512], et[:, 0:512], 1.0,
                                           pz[0][:, :], ALU.min, ALU.max)
                    ti = E.scalar_tensor_tensor(kf[:, 512:1024], et[:, 512:1024], 1.0,
                                                pz[1][:, :], ALU.min, ALU.max)
                    t(ti, V, "V1", 1)
                else:
                    t(None, V, "V1", 1)
            if eng == "vector":
                wt(E, "T2", b["T2"] + NT)
                E.tensor_copy(self.kvstg[0:65, 0:512], self.pb[4][0:65, :])
                E.tensor_copy(self.kvstg[0:65, 512:1024], self.pb[5][0:65, :])
                E.tensor_copy(self.kvstg_b[0:65, 0:512], self.kvstg[0:65, 0:512])
                ti = E.tensor_copy(self.kvstg_b[0:65, 512:1024],
                                   self.kvstg[0:65, 512:1024])
                t(ti, V, "V2", 1)
            else:
                t(None, V, "V2", 1)
            if eng == "tensor":
                wt(E, "V2", B["V2"] + g + 1)
                if g >= 1:
                    wt(E, "V3", B["V3"] + g)
                for hh in range(4):
                    for j in range(2):
                        ti = E.transpose(
                            self.pt2[:, 68 * (2 * hh + j):68 * (2 * hh + j) + 65],
                            self.kvstg_b[0:65, 256 * hh + 128 * j:
                                         256 * hh + 128 * j + 128],
                            self.idb[0:65, 0:65])
                t(ti, V, "T3", 1)
            else:
                t(None, V, "T3", 1)
            if eng == "vector":
                wt(E, "T3", B["T3"] + g + 1)
                ti = E.tensor_copy(
                    self.kvh[:, 520 * g:520 * g + 520].rearrange(
                        "p (c j) -> p c j", j=65),
                    self.pt2[:, 0:544].rearrange("p (c j) -> p c j", j=68)[:, :, 0:65])
                t(ti, V, "V3", 1)
            else:
                t(None, V, "V3", 1)

    @property
    def sel(self):
        return self.bigA[:, 8192:10240].bitcast(F32)

    def ph_ksbd(self, eng, E, V):
        t = self.tick
        if eng == "sync":
            t(E.dma_start(out=self.sel[0:16, :], in_=self.selmat[:, :]), V, "L2", 16)
        else:
            t(None, V, "L2", 16)
        if eng == "vector":
            self.wt(E, "L2", V["L2"])
            E.memset(self.ksbd[:, :], 0.0)
            E.drain()
            for c in range(32):
                h = c // 2
                ti = E.tensor_copy(self.ksbd[:, 16 * c + h:16 * c + h + 1],
                                   self.kvh[:, 65 * c + 64:65 * c + 65])
            t(ti, V, "V1", 1)
        else:
            t(None, V, "V1", 1)

    def ph_attn(self, eng, E, V):
        t, wt = self.tick, self.wt
        B = dict(V)
        for kk in range(8):
            wstage = self.ewA[:, 1024 * (kk % 2):1024 * (kk % 2) + 1024]
            ks = KS4[2 + kk % 2]
            if eng == "sync":
                if kk >= 2:
                    wt(E, "V1", B["V1"] + (kk - 2) + 1)
                t(E.dma_start(out=wstage, in_=self.woT[128 * kk:128 * kk + 128, :]),
                  V, ks, 16)
            else:
                t(None, V, ks, 16)
            if eng == "vector":
                wt(E, ks, B[ks] + 16 * (kk // 2 + 1))
                ti = E.tensor_copy(self.bigA[:, 1024 * kk:1024 * kk + 1024], wstage)
                t(ti, V, "V1", 1)
            else:
                t(None, V, "V1", 1)
        for u in range(NU):
            lb = B["L1"] + 128 + 192 * u
            tb = B["T1"] + 36 * u
            t2b = B["T2"] + u
            t3b = B["T3"] + 16 * u
            scb = B["SC"] + 48 * u
            v1b = B["V1"] + 8 + 32 * u
            v2b = B["V2"] + 17 * u
            v3b = B["V3"] + 4 * u
            qsl = u % 2
            qb = self.strm[:, 4096 * qsl:4096 * qsl + 4096]
            kq = KS4[u % 2]
            if eng == "sync":
                if u == 0:
                    wt(E, "S1", B["S1"])
                if u >= 2:
                    wt(E, "T1", B["T1"] + 36 * (u - 2) + 32)
                for kk in range(8):
                    t(E.dma_start(out=qb[:, 512 * kk:512 * kk + 512],
                                  in_=self.qT_sp[kk, :, 512 * u:512 * u + 512]),
                      V, kq, 16)
                for r in range(4):
                    gi_ = 4 * u + r
                    kx = "KE" if gi_ % 2 == 0 else "KF"
                    if gi_ >= 2:
                        wt(E, "V3", B["V3"] + (gi_ - 2) + 1)
                    xin = self.xt[:, 1024 * (r % 2):1024 * (r % 2) + 1024]
                    t(E.dma_start(out=xin,
                                  in_=self.x[512 * u + 128 * r:
                                             512 * u + 128 * r + 128, :]),
                      V, kx, 16)
            else:
                for kk in range(8):
                    t(None, V, kq, 16)
                for r in range(4):
                    gi_ = 4 * u + r
                    kx = "KE" if gi_ % 2 == 0 else "KF"
                    t(None, V, kx, 16)
            for c in range(32):
                h, j = c // 2, c % 2
                bank = self.pb[c % 4]
                et = self.ewA[:, 512 * (c % 4):512 * (c % 4) + 512]
                qf = self.vaug[:, 512 * c:512 * c + 512]
                if eng == "tensor":
                    if c == 0:
                        wt(E, kq, B[kq] + 128 * (u // 2 + 1))
                        if u >= 1:
                            wt(E, "V3", B["V3"] + 4 * u)
                    if c >= 4:
                        wt(E, "V1", v1b + (c - 4) + 1)
                    m, p = h // 2, h % 2
                    E.matmul(bank[:, :],
                             self.wfb[64 * p:64 * p + 64, 128 * j:128 * j + 128],
                             qb[64 * p:64 * p + 64, 512 * m:512 * m + 512],
                             start=True, stop=False)
                    ti = E.matmul(bank[:, :], self.onesb[0:1, 0:128],
                                  self.onesb[0:1, 0:512], start=False, stop=True)
                    t(ti, V, "T1", 1)
                else:
                    t(None, V, "T1", 1)
                if eng == "scalar":
                    wt(E, "T1", tb + c + 1)
                    if c >= 4:
                        wt(E, "V1", v1b + (c - 4) + 1)
                    ti = E.activation(et, bank[:, :], AF.Exp,
                                      bias=self.cst[:, CNEG1:CNEG1 + 1],
                                      scale=self.cst[:, CONE:CONE + 1])
                    t(ti, V, "SC", 1)
                else:
                    t(None, V, "SC", 1)
                if eng == "vector":
                    wt(E, "SC", scb + c + 1)
                    if u >= 1 and c == 0:
                        wt(E, "T2", B["T2"] + u)
                        wt(E, "T3", B["T3"] + 16 * u)
                    ti = E.scalar_tensor_tensor(qf, et, 1.0, bank[:, :],
                                                ALU.min, ALU.max)
                    t(ti, V, "V1", 1)
                else:
                    t(None, V, "V1", 1)
            if eng == "tensor":
                wt(E, "V1", v1b + 32)
                if u >= 1:
                    wt(E, "V2", B["V2"] + 17 * u)
                for c in range(32):
                    ti = E.matmul(self.pb[4][0:16, :], self.ksbd[:, 16 * c:16 * c + 16],
                                  self.vaug[:, 512 * c:512 * c + 512],
                                  start=(c == 0), stop=(c == 31))
                t(ti, V, "T2", 1)
            else:
                t(None, V, "T2", 1)
            if eng == "vector":
                wt(E, "T2", t2b + 1)
                E.tensor_scalar(self.kvstg[0:16, 512:1024], self.pb[4][0:16, :],
                                self.cst[0:16, CEPS6:CEPS6 + 1], None, ALU.add)
                E.drain()
                ti = E.reciprocal(self.kvstg[0:16, 0:512], self.kvstg[0:16, 512:1024])
                t(ti, V, "V2", 1)
            else:
                t(None, V, "V2", 1)
            for h in range(H):
                rb = self.pb[5] if h % 2 == 0 else self.pt[:, :].bitcast(F32)
                ob = self.pb[4] if h % 2 == 0 else self.pb[3]
                osb = self.ewA[:, 2048 + 512 * (h % 2):2048 + 512 * (h % 2) + 512]
                if eng == "tensor":
                    if h < 2:
                        wt(E, "V2", v2b + 1)
                    else:
                        wt(E, "V2", v2b + h)
                        wt(E, "SC", scb + 31 + h)
                    E.matmul(rb[0:64, :], self.sel[0:16, 64 * h:64 * h + 64],
                             self.kvstg[0:16, 0:512], start=True, stop=True)
                    for j in range(2):
                        c = 2 * h + j
                        ti = E.matmul(ob[0:64, :], self.kvh[:, 65 * c:65 * c + 64],
                                      self.vaug[:, 512 * c:512 * c + 512],
                                      start=(j == 0), stop=(j == 1))
                    t(ti, V, "T3", 1)
                else:
                    t(None, V, "T3", 1)
                if eng == "scalar":
                    wt(E, "T3", t3b + h + 1)
                    if h >= 2:
                        wt(E, "V2", v2b + 1 + h - 1)
                    ti = E.activation(osb[0:64, :], ob[0:64, :], AF.Copy, bias=0.0,
                                      scale=self.cst[0:64, CONE:CONE + 1])
                    t(ti, V, "SC", 1)
                else:
                    t(None, V, "SC", 1)
                if eng == "vector":
                    wt(E, "SC", scb + 32 + h + 1)
                    if u >= 1 and h == 0:
                        wt(E, "T1", B["T1"] + 36 * u)
                    po = 64 * (h % 2)
                    dst = self.ewB[:, 512 * (h // 2):512 * (h // 2) + 512]
                    ti = E.tensor_tensor(dst[po:po + 64, :], osb[0:64, :],
                                         rb[0:64, :], ALU.mult)
                    t(ti, V, "V2", 1)
                else:
                    t(None, V, "V2", 1)
            for r in range(4):
                gi = 4 * u + r
                pa = self.pb[2 * (r % 2)]
                pc = self.pb[2 * (r % 2) + 1]
                xin = self.xt[:, 1024 * (r % 2):1024 * (r % 2) + 1024]
                xp = self.xt[:, 2048 + 1024 * (r % 2):2048 + 1024 * (r % 2) + 1024]
                if eng == "tensor":
                    wt(E, "V2", v2b + 17)
                    if r >= 2:
                        wt(E, "V3", v3b + (r - 2) + 1)
                    for kk in range(8):
                        lhs = self.ewB[:, 512 * kk + 128 * r:512 * kk + 128 * r + 128]
                        E.matmul(pa[:, :], lhs, self.bigA[:, 1024 * kk:1024 * kk + 512],
                                 start=(kk == 0), stop=(kk == 7))
                        ti = E.matmul(pc[:, :], lhs,
                                      self.bigA[:, 1024 * kk + 512:1024 * kk + 1024],
                                      start=(kk == 0), stop=(kk == 7))
                    t(ti, V, "T1", 1)
                else:
                    t(None, V, "T1", 1)
                if eng == "vector":
                    wt(E, "T1", tb + 32 + r + 1)
                    kx = "KE" if gi % 2 == 0 else "KF"
                    wt(E, kx, B[kx] + 16 * (gi // 2 + 1))
                    if gi >= 2:
                        wt(E, "S1", B["S1"] + 16 * (gi - 1))
                    E.tensor_tensor(xp[:, 0:512], pa[:, :], xin[:, 0:512], ALU.add)
                    E.tensor_tensor(xp[:, 512:1024], pc[:, :], xin[:, 512:1024],
                                    ALU.add)
                    E.tensor_tensor(self.ewA[:, 0:512], xp[:, 0:512],
                                    xp[:, 0:512], ALU.mult)
                    E.tensor_tensor(self.ewA[:, 512:1024], xp[:, 512:1024],
                                    xp[:, 512:1024], ALU.mult)
                    E.tensor_reduce(self.sum2[:, gi:gi + 1], xp, AX, ALU.add)
                    ti = E.tensor_reduce(self.sq2[:, gi:gi + 1],
                                         self.ewA[:, 0:1024], AX, ALU.add)
                    t(ti, V, "V3", 1)
                else:
                    t(None, V, "V3", 1)
                if eng == "gpsimd":
                    wt(E, "V3", v3b + r + 1)
                    ti = E.dma_start(out=self.xp_sp[gi, :, :], in_=xp)
                    t(ti, V, "S1", 16)
                else:
                    t(None, V, "S1", 16)

    def ph_ln2_derive(self, eng, E, V):
        t, wt = self.tick, self.wt
        B = dict(V)
        if eng == "vector":
            E.tensor_scalar(self.st2m[:, :], self.sum2[:, :], 1.0 / D, None, ALU.mult)
            E.tensor_scalar(self.ewA[:, 0:32], self.sq2[:, :], 1.0 / D, None, ALU.mult)
            E.tensor_tensor(self.ewA[:, 32:64], self.st2m[:, :], self.st2m[:, :],
                            ALU.mult)
            E.drain()
            ti = E.tensor_tensor(self.ewA[:, 0:32], self.ewA[:, 0:32],
                                 self.ewA[:, 32:64], ALU.subtract)
            t(ti, V, "V1", 1)
        else:
            t(None, V, "V1", 1)
        if eng == "scalar":
            wt(E, "V1", V["V1"])
            ti = E.activation(self.ewA[:, 64:96], self.ewA[:, 0:32], AF.Sqrt,
                              bias=self.cst[:, CEPS5:CEPS5 + 1],
                              scale=self.cst[:, CONE:CONE + 1])
            t(ti, V, "SC", 1)
        else:
            t(None, V, "SC", 1)
        if eng == "vector":
            wt(E, "SC", V["SC"])
            ti = E.reciprocal(self.st2r[:, :], self.ewA[:, 64:96])
            t(ti, V, "V1", 1)
        else:
            t(None, V, "V1", 1)
        bswp = dict(V)
        for i in range(NT):
            xp = self.xt[:, 1024 * (i % 4):1024 * (i % 4) + 1024]
            ks = KS4[i % 4]
            if eng == "sync":
                if i == 0:
                    wt(E, "S1", bswp["S1"])
                wt(E, "V1", bswp["V1"])
                if i >= 4:
                    wt(E, "V1", bswp["V1"] + 2 * (i - 4) + 2)
                t(E.dma_start(out=xp, in_=self.xp_sp[i, :, :]), V, ks, 16)
            else:
                t(None, V, ks, 16)
            trash = self.ewA[:, 1024 * (i % 2):1024 * (i % 2) + 1024]
            if eng == "vector":
                wt(E, ks, bswp[ks] + 16 * (i // 4 + 1))
                E.tensor_scalar(trash, xp, self.st2m[:, i:i + 1],
                                self.st2r[:, i:i + 1], ALU.subtract, ALU.mult)
                if i >= 1:
                    pt_ = self.ewA[:, 1024 * ((i - 1) % 2):1024 * ((i - 1) % 2) + 1024]
                    E.tensor_reduce(self.dqt[:, (i - 1) % 2:(i - 1) % 2 + 1],
                                    pt_, AX, ALU.max, apply_absolute_value=True)
                if i >= 2:
                    ti = E.tensor_tensor(self.amx[:, 0:1], self.amx[:, 0:1],
                                         self.dqt[:, (i - 2) % 2:(i - 2) % 2 + 1],
                                         ALU.max)
                else:
                    ti = E.engine_nop()
                t(ti, V, "V1", 2)
            else:
                t(None, V, "V1", 2)
        if eng == "vector":
            last = self.ewA[:, 1024 * ((NT - 1) % 2):1024 * ((NT - 1) % 2) + 1024]
            E.tensor_reduce(self.dqt[:, (NT - 1) % 2:(NT - 1) % 2 + 1], last,
                            AX, ALU.max, apply_absolute_value=True)
            E.tensor_tensor(self.amx[:, 0:1], self.amx[:, 0:1],
                            self.dqt[:, (NT - 2) % 2:(NT - 2) % 2 + 1], ALU.max)
            E.drain()
            ti = E.tensor_tensor(self.amx[:, 0:1], self.amx[:, 0:1],
                                 self.dqt[:, (NT - 1) % 2:(NT - 1) % 2 + 1],
                                 ALU.max)
            self.tick(ti, V, "V1", 1)
        else:
            self.tick(None, V, "V1", 1)
        self._barrier(eng, E, V, self.amx[:, 0:1], 1, T_G1, None, None)
        self._scal_chain(eng, E, V, [
            ("ts2", T_G1, T_S1, 1.0 / QMAX, 1e-12),
            ("recip", T_S1, T_IS1),
        ])
        self._bcast(eng, E, V, [(T_IS1, SC_INV_S1), (T_S1, SC_S1)])
        if eng == "vector":
            wt(E, "V2", V["V2"])
            E.tensor_scalar(self.abuf[:, :], self.st2r[:, :],
                            self.scal[:, SC_INV_S1:SC_INV_S1 + 1], None, ALU.mult)
            E.drain()
            E.tensor_tensor(self.ewA[:, 0:32], self.st2m[:, :], self.abuf[:, :],
                            ALU.mult)
            E.drain()
            ti = E.tensor_scalar(self.bcbuf[:, :], self.ewA[:, 0:32], -1.0, C_MAG,
                                 ALU.mult, ALU.add)
            t(ti, V, "V1", 1)
        else:
            t(None, V, "V1", 1)

    def _barrier(self, eng, E, V, col_ap, ar_key, dst_tcol, col2_ap, dst_tcol2):
        t, wt = self.tick, self.wt
        if eng == "tensor":
            wt(E, "V1", V["V1"])
            E.transpose(self.pb[0][0:1, 0:128], col_ap, self.idf[:, :])
            c2 = col2_ap if col2_ap is not None else col_ap
            ti = E.transpose(self.pb[1][0:1, 0:128], c2, self.idf[:, :])
            t(ti, V, "T1", 1)
        else:
            t(None, V, "T1", 1)
        if eng == "vector":
            wt(E, "T1", V["T1"])
            E.tensor_reduce(self.kvstg[0:1, 0:1], self.pb[0][0:1, 0:128], AX, ALU.max)
            ti = E.tensor_reduce(self.kvstg[0:1, 1:2], self.pb[1][0:1, 0:128],
                                 AX, ALU.max)
            t(ti, V, "V2", 1)
        else:
            t(None, V, "V2", 1)
        if eng == "sync":
            wt(E, "V2", V["V2"])
            t(E.dma_start(out=self.ar_in[ar_key][:, :], in_=self.kvstg[0:1, 0:2]),
              V, "L2", 16)
        else:
            t(None, V, "L2", 16)
        if eng == "gpsimd":
            wt(E, "L2", V["L2"])
            ti = E.collective_compute("AllReduce", ALU.max,
                                      replica_groups=[list(range(NCORES))],
                                      ins=[self.ar_in[ar_key].ap().opt()],
                                      outs=[self.ar_out[ar_key].ap().opt()])
            t(ti, V, "CC", 1)
        else:
            t(None, V, "CC", 1)
        if eng == "sync":
            wt(E, "CC", V["CC"])
            t(E.dma_start(out=self.sct[0:1, dst_tcol:dst_tcol + 1],
                          in_=self.ar_out[ar_key][0:1, 0:1]), V, "L2", 16)
        else:
            t(None, V, "L2", 16)
        if dst_tcol2 is not None:
            if eng == "sync":
                t(E.dma_start(out=self.sct[0:1, dst_tcol2:dst_tcol2 + 1],
                              in_=self.ar_out[ar_key][0:1, 1:2]), V, "L2", 16)
            else:
                t(None, V, "L2", 16)

    def _scal_chain(self, eng, E, V, ops):
        t, wt = self.tick, self.wt
        if eng == "vector":
            wt(E, "L2", V["L2"])
            ti = None
            first = True
            for op in ops:
                if not first:
                    E.drain()
                first = False
                if op[0] == "ts2":
                    _, src, dst, m, mx = op
                    ti = E.tensor_scalar(self.sct[0:1, dst:dst + 1],
                                         self.sct[0:1, src:src + 1], m, mx,
                                         ALU.mult, ALU.max)
                elif op[0] == "recip":
                    ti = E.reciprocal(self.sct[0:1, op[2]:op[2] + 1],
                                      self.sct[0:1, op[1]:op[1] + 1])
                elif op[0] == "mul2":
                    _, a, bb, dst = op
                    ti = E.tensor_tensor(self.sct[0:1, dst:dst + 1],
                                         self.sct[0:1, a:a + 1],
                                         self.sct[0:1, bb:bb + 1], ALU.mult)
                elif op[0] == "mulc":
                    _, src, dst, cc = op
                    ti = E.tensor_scalar(self.sct[0:1, dst:dst + 1],
                                         self.sct[0:1, src:src + 1], cc, None,
                                         ALU.mult)
            t(ti, V, "V1", 1)
        else:
            t(None, V, "V1", 1)

    def _bcast(self, eng, E, V, pairs):
        t, wt = self.tick, self.wt
        if eng == "tensor":
            wt(E, "V1", V["V1"])
            for i, (tcol, scol) in enumerate(pairs):
                ti = E.matmul(self.pb[1][:, i:i + 1], self.onesf[0:1, 0:128],
                              self.sct[0:1, tcol:tcol + 1], start=True, stop=True)
            t(ti, V, "T2", 1)
        else:
            t(None, V, "T2", 1)
        if eng == "vector":
            wt(E, "T2", V["T2"])
            for i, (tcol, scol) in enumerate(pairs):
                ti = E.tensor_copy(self.scal[:, scol:scol + 1], self.pb[1][:, i:i + 1])
            t(ti, V, "V2", 1)
        else:
            t(None, V, "V2", 1)

    def ph_xq(self, eng, E, V):
        t, wt = self.tick, self.wt
        b = dict(V)
        for i in range(NT):
            xp = self.xt[:, 1024 * (i % 4):1024 * (i % 4) + 1024]
            y = self.ewA[:, 1024 * (i % 2):1024 * (i % 2) + 1024]
            xqb = self.ewB[:, 1024 * (i % 4):1024 * (i % 4) + 1024].bitcast(F16)
            ptt = self.pt if i % 2 == 0 else self.pt2
            ks = KS4[i % 4]
            if eng == "sync":
                if i == 0:
                    wt(E, "S1", b["S1"])
                if i >= 4:
                    wt(E, "V1", b["V1"] + 2 * (i - 4) + 2)
                t(E.dma_start(out=xp, in_=self.xp_sp[i, :, :]), V, ks, 16)
            else:
                t(None, V, ks, 16)
            if eng == "vector":
                wt(E, ks, b[ks] + 16 * (i // 4 + 1))
                E.tensor_scalar(y, xp, self.abuf[:, i:i + 1], self.bcbuf[:, i:i + 1],
                                ALU.mult, ALU.add)
                E.drain()
                if i >= 4:
                    wt(E, "T1", b["T1"] + (i - 4) + 1)
                ti = E.tensor_scalar(xqb, y, -C_MAG, self.scal[:, SC_S1:SC_S1 + 1],
                                     ALU.add, ALU.mult)
                t(ti, V, "V1", 2)
            else:
                t(None, V, "V1", 2)
            if eng == "tensor":
                wt(E, "V1", b["V1"] + 2 * i + 2)
                if i >= 2:
                    wt(E, "V2", b["V2"] + (i - 2) + 1)
                for kk in range(8):
                    ti = E.transpose(ptt[:, 128 * kk:128 * kk + 128].bitcast(F16),
                                     xqb[:, 128 * kk:128 * kk + 128], self.idh[:, :])
                t(ti, V, "T1", 1)
            else:
                t(None, V, "T1", 1)
            if eng == "vector":
                wt(E, "T1", b["T1"] + i + 1)
                ti = E.tensor_copy(
                    self.bigA[:, :].bitcast(F16).rearrange("p (k s) -> p k s", k=8)[
                        :, :, 128 * i:128 * i + 128],
                    ptt[:, :].bitcast(F16).rearrange("p (k s) -> p k s", k=8))
                t(ti, V, "V2", 1)
            else:
                t(None, V, "V2", 1)

    def ph_upgate(self, eng, E, V, which):
        t, wt = self.tick, self.wt
        ag_out = self.agu_out if which == "up" else self.agg_out
        spill = self.up_sp if which == "up" else self.gt_sp
        amxcol = 1 if which == "up" else 2
        b = dict(V)
        if eng == "sync":
            wt(E, "CC", 3)
            for k in range(8):
                src = ag_out.rearrange("(c r) j -> c r j", c=NCORES)[
                    :, 128 * k:128 * k + 128, :].transpose([1, 0, 2])
                t(E.dma_start(out=self.vaug[:, 3072 * k:3072 * k + 3072].bitcast(
                    F16).rearrange("p (c j) -> p c j", c=NCORES), in_=src),
                  V, "L1", 16)
        else:
            for k in range(8):
                t(None, V, "L1", 16)
        groups = [(0, 1, 2), (3, 4, 5), (6, 7)]
        evc = []
        tot = 0
        for ul in groups:
            tot += len(ul)
            evc.append(tot)
        q = 0
        gidx = 0
        for m in range(IC):
            for ulist in groups:
                banks = self.pb[0:3] if gidx % 2 == 0 else self.pb[3:6]
                if eng == "tensor":
                    if gidx == 0:
                        wt(E, "L1", b["L1"] + 128)
                    if gidx >= 2:
                        done = evc[(gidx - 2) % 3] + 8 * ((gidx - 2) // 3)
                        wt(E, "SC", b["SC"] + 8 * ((gidx - 2) // 3) +
                           evc[(gidx - 2) % 3])
                    for k in range(8):
                        lhs = self.vaug[:, 3072 * k + 128 * m:
                                        3072 * k + 128 * m + 128].bitcast(F16)
                        for ui, u in enumerate(ulist):
                            ti = E.matmul(banks[ui][:, :], lhs,
                                          self.bigA[:, 4096 * k + 512 * u:
                                                    4096 * k + 512 * u + 512].bitcast(F16),
                                          start=(k == 0), stop=(k == 7))
                    t(ti, V, "T1", 1)
                else:
                    t(None, V, "T1", 1)
                for ui, u in enumerate(ulist):
                    sslot = q % 8
                    stgt = self.stg[:, 512 * sslot:512 * sslot + 512]
                    if eng == "scalar":
                        wt(E, "T1", b["T1"] + gidx + 1)
                        if q >= 8:
                            wt(E, "S1", b["S1"] + 16 * (q - 8 + 1))
                        ti = E.activation(stgt, banks[ui][:, :], AF.Copy, bias=0.0,
                                          scale=self.cst[:, CONE:CONE + 1])
                        t(ti, V, "SC", 1)
                    else:
                        t(None, V, "SC", 1)
                    if eng == "vector":
                        wt(E, "SC", b["SC"] + q + 1)
                        if q >= 2:
                            E.tensor_tensor(self.amx[:, amxcol:amxcol + 1],
                                            self.amx[:, amxcol:amxcol + 1],
                                            self.dqt[:, (q - 2) % 3:(q - 2) % 3 + 1],
                                            ALU.max)
                        ti = E.tensor_reduce(self.dqt[:, q % 3:q % 3 + 1], stgt,
                                             AX, ALU.max, apply_absolute_value=True)
                        t(ti, V, "V1", 2)
                    else:
                        t(None, V, "V1", 2)
                    if eng == "gpsimd":
                        wt(E, "SC", b["SC"] + q + 1)
                        ti = E.dma_start(out=spill[m, u, :, :], in_=stgt)
                        t(ti, V, "S1", 16)
                    else:
                        t(None, V, "S1", 16)
                    q += 1
                gidx += 1
        if eng == "vector":
            E.drain()
            E.tensor_tensor(self.amx[:, amxcol:amxcol + 1],
                            self.amx[:, amxcol:amxcol + 1],
                            self.dqt[:, (q - 2) % 3:(q - 2) % 3 + 1], ALU.max)
            E.drain()
            ti = E.tensor_tensor(self.amx[:, amxcol:amxcol + 1],
                                 self.amx[:, amxcol:amxcol + 1],
                                 self.dqt[:, (q - 1) % 3:(q - 1) % 3 + 1], ALU.max)
            t(ti, V, "V1", 1)
        else:
            t(None, V, "V1", 1)

    def ph_b2(self, eng, E, V):
        self._barrier(eng, E, V, self.amx[:, 2:3], 2, T_GG, None, None)
        self._scal_chain(eng, E, V, [
            ("ts2", T_GG, T_S2G, 1.0 / QMAX, 1e-12),
            ("recip", T_S2G, T_IS2G),
            ("mulc", T_S2G, T_CG, 1.0 / 65536.0),
        ])
        self._bcast(eng, E, V, [(T_IS2G, SC_INV_S2G), (T_S2G, SC_S2G),
                                (T_CG, SC_CG)])

    def ph_upp2(self, eng, E, V):
        t, wt = self.tick, self.wt
        b = dict(V)
        NTT = IC * NU
        groups = [(0, 1, 2), (3, 4, 5), (6, 7)]
        glist = []
        q = 0
        for m in range(IC):
            for ulist in groups:
                glist.append((m, ulist, q))
                q += len(ulist)
        NG = len(glist)
        NQ = q
        qend = [g[2] + len(g[1]) for g in glist]
        sched = []
        gnext = 0
        for j in range(NTT + 2):
            while gnext < NG and gnext * (NTT + 2) <= j * NG:
                sched.append(("ug", gnext))
                gnext += 1
            sched.append(("p2", j))
        while gnext < NG:
            sched.append(("ug", gnext))
            gnext += 1
        SCe, SCy, SCs = {}, {}, {}
        V1r, V1t, V1b = {}, {}, {}
        S1u, S1p = {}, {}
        sc = v1 = s1 = 0
        for kind, i in sched:
            if kind == "ug":
                for ui in range(len(glist[i][1])):
                    qq = glist[i][2] + ui
                    sc += 1
                    SCe[qq] = sc
                    v1 += 2
                    V1r[qq] = v1
                    s1 += 16
                    S1u[qq] = s1
            else:
                n = i
                if n < NTT:
                    sc += 1
                    SCy[n] = sc
                    v1 += 1
                    V1t[n] = v1
                if 1 <= n <= NTT:
                    sc += 2
                    SCs[n - 1] = sc
                if n >= 2:
                    v1 += 3
                    V1b[n - 2] = v1
                    s1 += 16
                    S1p[n - 2] = s1
        if eng == "sync":
            wt(E, "CC", 3)
            for k in range(8):
                src = self.agu_out.rearrange("(c r) j -> c r j", c=NCORES)[
                    :, 128 * k:128 * k + 128, :].transpose([1, 0, 2])
                t(E.dma_start(out=self.vaug[:, 3072 * k:3072 * k + 3072].bitcast(
                    F16).rearrange("p (c j) -> p c j", c=NCORES), in_=src),
                  V, "L1", 16)
        else:
            for k in range(8):
                t(None, V, "L1", 16)
        for kind, i in sched:
            if kind == "ug":
                m, ulist, qs = glist[i]
                banks = self.pb[0:3] if i % 2 == 0 else self.pb[3:6]
                if eng == "tensor":
                    if i == 0:
                        wt(E, "L1", b["L1"] + 128)
                    if i >= 2:
                        wt(E, "SC", b["SC"] + SCe[qend[i - 2] - 1])
                    for k in range(8):
                        lhs = self.vaug[:, 3072 * k + 128 * m:
                                        3072 * k + 128 * m + 128].bitcast(F16)
                        for ui, u in enumerate(ulist):
                            ti = E.matmul(banks[ui][:, :], lhs,
                                          self.bigA[:, 4096 * k + 512 * u:
                                                    4096 * k + 512 * u + 512
                                                    ].bitcast(F16),
                                          start=(k == 0), stop=(k == 7))
                    t(ti, V, "T1", 1)
                else:
                    t(None, V, "T1", 1)
                for ui, u in enumerate(ulist):
                    qq = qs + ui
                    stgt = self.vaug[:, 24576 + 512 * (qq % 8):
                                     24576 + 512 * (qq % 8) + 512].bitcast(F16)
                    if eng == "scalar":
                        wt(E, "T1", b["T1"] + i + 1)
                        if qq >= 8:
                            wt(E, "S1", b["S1"] + S1u[qq - 8])
                        ti = E.activation(stgt, banks[ui][:, :], AF.Copy, bias=0.0,
                                          scale=self.cst[:, CONE:CONE + 1])
                        t(ti, V, "SC", 1)
                    else:
                        t(None, V, "SC", 1)
                    if eng == "vector":
                        wt(E, "SC", b["SC"] + SCe[qq])
                        if qq >= 2:
                            E.tensor_tensor(self.amx[:, 1:2], self.amx[:, 1:2],
                                            self.dqt[:, (qq - 2) % 3:
                                                     (qq - 2) % 3 + 1], ALU.max)
                        ti = E.tensor_reduce(self.dqt[:, qq % 3:qq % 3 + 1], stgt,
                                             AX, ALU.max, apply_absolute_value=True)
                        t(ti, V, "V1", 2)
                    else:
                        t(None, V, "V1", 2)
                    if eng == "gpsimd":
                        wt(E, "SC", b["SC"] + SCe[qq])
                        ti = E.dma_start(out=self.up_sp[m, u, :, :], in_=stgt)
                        t(ti, V, "S1", 16)
                    else:
                        t(None, V, "S1", 16)
                continue
            n = i
            if n < NTT:
                g16 = self.stg[:, 512 * (n % 4):512 * (n % 4) + 512]
                yg = self.xt[:, 512 * (n % 4):512 * (n % 4) + 512]
                tg = self.xt[:, 2048 + 512 * (n % 4):2048 + 512 * (n % 4) + 512]
                ks = KS4[n % 4]
                if eng == "sync":
                    if n == 0:
                        wt(E, "S1", b["S1"])
                    if n >= 4:
                        wt(E, "SC", b["SC"] + SCy[n - 4])
                    t(E.dma_start(out=g16, in_=self.gt_sp[n // NU, n % NU, :, :]),
                      V, ks, 16)
                else:
                    t(None, V, ks, 16)
                if eng == "scalar":
                    wt(E, ks, b[ks] + 16 * (n // 4 + 1))
                    if n >= 4:
                        wt(E, "V1", b["V1"] + V1t[n - 4])
                    ti = E.activation(yg, g16, AF.Copy, bias=C_MAG,
                                      scale=self.scal[:, SC_INV_S2G:SC_INV_S2G + 1])
                    t(ti, V, "SC", 1)
                else:
                    t(None, V, "SC", 1)
                if eng == "vector":
                    wt(E, "SC", b["SC"] + SCy[n])
                    ti = E.tensor_scalar(tg, yg, -C_MAG, None, ALU.add)
                    t(ti, V, "V1", 1)
                else:
                    t(None, V, "V1", 1)
            if 1 <= n <= NTT:
                k = n - 1
                sg = self.ewA[:, 2560:3072]
                ys = self.ewA[:, 512 * (k % 4):512 * (k % 4) + 512]
                tgk = self.xt[:, 2048 + 512 * (k % 4):2048 + 512 * (k % 4) + 512]
                if eng == "scalar":
                    wt(E, "V1", b["V1"] + V1t[k])
                    if k >= 4:
                        wt(E, "V1", b["V1"] + V1b[k - 4])
                    E.activation(sg, tgk, AF.Sigmoid,
                                 bias=self.cst[:, CZERO:CZERO + 1],
                                 scale=self.scal[:, SC_S2G:SC_S2G + 1])
                    ti = E.activation(ys, sg, AF.Copy, bias=C_MAG,
                                      scale=self.cst[:, C65536:C65536 + 1])
                    t(ti, V, "SC", 2)
                else:
                    t(None, V, "SC", 2)
            if n >= 2:
                k = n - 2
                ys = self.ewA[:, 512 * (k % 4):512 * (k % 4) + 512]
                ts5 = self.ewA[:, 2048:2560]
                tgk = self.xt[:, 2048 + 512 * (k % 4):2048 + 512 * (k % 4) + 512]
                m16 = self.stg[:, 2048 + 512 * (k % 4):2048 + 512 * (k % 4) + 512]
                if eng == "vector":
                    wt(E, "SC", b["SC"] + SCs[k])
                    if k >= 4:
                        wt(E, "S1", b["S1"] + S1p[k - 4])
                    E.tensor_scalar(ts5, ys, -C_MAG, 65535.0, ALU.add, ALU.min)
                    if k >= 1:
                        pm16 = self.stg[:, 2048 + 512 * ((k - 1) % 4):
                                        2048 + 512 * ((k - 1) % 4) + 512]
                        E.tensor_reduce(self.dqt[:, 4 + (k - 1) % 3:
                                                 4 + (k - 1) % 3 + 1],
                                        pm16, AX, ALU.max,
                                        apply_absolute_value=True)
                    else:
                        E.engine_nop()
                    E.scalar_tensor_tensor(m16, tgk, self.scal[:, SC_CG:SC_CG + 1],
                                           ts5, ALU.mult, ALU.mult)
                    if k >= 2:
                        ti = E.tensor_tensor(self.amx[:, 3:4], self.amx[:, 3:4],
                                             self.dqt[:, 4 + (k - 2) % 3:
                                                      4 + (k - 2) % 3 + 1],
                                             ALU.max)
                    else:
                        ti = E.engine_nop()
                    t(ti, V, "V1", 3)
                else:
                    t(None, V, "V1", 3)
                if eng == "gpsimd":
                    wt(E, "V1", b["V1"] + V1b[k])
                    ti = E.dma_start(out=self.m1_sp[k // NU, k % NU, :, :], in_=m16)
                    t(ti, V, "S1", 16)
                else:
                    t(None, V, "S1", 16)
        if eng == "vector":
            E.drain()
            E.tensor_tensor(self.amx[:, 1:2], self.amx[:, 1:2],
                            self.dqt[:, (NQ - 2) % 3:(NQ - 2) % 3 + 1], ALU.max)
            E.drain()
            ti = E.tensor_tensor(self.amx[:, 1:2], self.amx[:, 1:2],
                                 self.dqt[:, (NQ - 1) % 3:(NQ - 1) % 3 + 1],
                                 ALU.max)
            t(ti, V, "V1", 1)
            k = NTT - 1
            lm16 = self.stg[:, 2048 + 512 * (k % 4):2048 + 512 * (k % 4) + 512]
            E.tensor_reduce(self.dqt[:, 4 + k % 3:4 + k % 3 + 1], lm16,
                            AX, ALU.max, apply_absolute_value=True)
            E.tensor_tensor(self.amx[:, 3:4], self.amx[:, 3:4],
                            self.dqt[:, 4 + (k - 1) % 3:4 + (k - 1) % 3 + 1],
                            ALU.max)
            E.drain()
            ti = E.tensor_tensor(self.amx[:, 3:4], self.amx[:, 3:4],
                                 self.dqt[:, 4 + k % 3:4 + k % 3 + 1], ALU.max)
            t(ti, V, "V1", 1)
        else:
            t(None, V, "V1", 1)
            t(None, V, "V1", 1)

    def ph_b4(self, eng, E, V):
        self._barrier(eng, E, V, self.amx[:, 3:4], 4, T_G4, self.amx[:, 1:2], T_GU)
        self._scal_chain(eng, E, V, [
            ("ts2", T_GU, T_S2U, 1.0 / QMAX, 1e-12),
            ("recip", T_S2U, T_IS2U),
            ("ts2", T_G4, T_S4, 1.0 / QMAX, 1e-12),
            ("recip", T_S4, T_IS4),
            ("mul2", T_S4, T_S2U, T_C45),
        ])
        self._bcast(eng, E, V, [(T_IS4, SC_INV_S4), (T_C45, SC_C45),
                                (T_IS2U, SC_INV_S2U)])

    def ph_p3(self, eng, E, V):
        t, wt = self.tick, self.wt
        b = dict(V)
        if eng == "sync":
            wt(E, "CC", 3)
            for i in range(IC):
                src = self.agd_out.rearrange("(c r) j -> c r j", c=NCORES)[
                    :, 128 * i:128 * i + 128, :].transpose([1, 0, 2])
                t(E.dma_start(out=self.bigA[:, 1024 * i:1024 * i + 1024].bitcast(
                    F16).rearrange("p (c j) -> p c j", c=NCORES), in_=src),
                  V, "L2", 16)
        else:
            for i in range(IC):
                t(None, V, "L2", 16)
        n = 0
        for m in range(IC):
            for u in range(NU):
                a16 = self.stg[:, 512 * (n % 4):512 * (n % 4) + 512]
                b16 = self.ewB[:, 512 * (n % 4):512 * (n % 4) + 512].bitcast(F16)
                m16 = self.stg[:, 2048 + 512 * (n % 4):2048 + 512 * (n % 4) + 512]
                y2 = self.xt[:, 512 * (n % 4):512 * (n % 4) + 512]
                yu = self.xt[:, 2048 + 512 * (n % 4):2048 + 512 * (n % 4) + 512]
                tu = self.ewA[:, 512 * (n % 4):512 * (n % 4) + 512]
                ks = KS4[n % 4]
                if eng == "sync":
                    if n == 0:
                        wt(E, "S1", b["S1"])
                    if n >= 4:
                        wt(E, "SC", b["SC"] + (n - 4) + 1)
                    t(E.dma_start(out=a16, in_=self.m1_sp[m, u, :, :]), V, ks, 16)
                    t(E.dma_start(out=b16, in_=self.up_sp[m, u, :, :]), V, ks, 16)
                else:
                    t(None, V, ks, 16)
                    t(None, V, ks, 16)
                if eng == "scalar":
                    wt(E, ks, b[ks] + 32 * (n // 4 + 1))
                    if n >= 4:
                        wt(E, "V1", b["V1"] + 3 * (n - 4) + 2)
                    E.activation(y2, a16, AF.Copy, bias=C_MAG,
                                 scale=self.scal[:, SC_INV_S4:SC_INV_S4 + 1])
                    ti = E.activation(yu, b16, AF.Copy, bias=C_MAG,
                                      scale=self.scal[:, SC_INV_S2U:SC_INV_S2U + 1])
                    t(ti, V, "SC", 1)
                else:
                    t(None, V, "SC", 1)
                if eng == "vector":
                    wt(E, "SC", b["SC"] + n + 1)
                    if n >= 4:
                        wt(E, "S1", b["S1"] + 16 * (n - 4 + 1))
                    E.tensor_scalar(tu, yu, -C_MAG, self.scal[:, SC_C45:SC_C45 + 1],
                                    ALU.add, ALU.mult)
                    if n >= 1:
                        pm16 = self.stg[:, 2048 + 512 * ((n - 1) % 4):
                                        2048 + 512 * ((n - 1) % 4) + 512]
                        E.tensor_reduce(self.dqt[:, (n - 1) % 3:(n - 1) % 3 + 1],
                                        pm16, AX, ALU.max,
                                        apply_absolute_value=True)
                    E.scalar_tensor_tensor(m16, y2, -C_MAG, tu, ALU.add, ALU.mult)
                    if n >= 2:
                        ti = E.tensor_tensor(self.amx[:, 4:5], self.amx[:, 4:5],
                                             self.dqt[:, (n - 2) % 3:(n - 2) % 3 + 1],
                                             ALU.max)
                    else:
                        ti = E.engine_nop()
                    t(ti, V, "V1", 3)
                else:
                    t(None, V, "V1", 3)
                if eng == "gpsimd":
                    wt(E, "V1", b["V1"] + 3 * n + 3)
                    ti = E.dma_start(out=self.m2_sp[m, u, :, :], in_=m16)
                    t(ti, V, "S1", 16)
                else:
                    t(None, V, "S1", 16)
                n += 1
        if eng == "vector":
            lm16 = self.stg[:, 2048 + 512 * ((n - 1) % 4):
                            2048 + 512 * ((n - 1) % 4) + 512]
            E.tensor_reduce(self.dqt[:, (n - 1) % 3:(n - 1) % 3 + 1], lm16,
                            AX, ALU.max, apply_absolute_value=True)
            E.tensor_tensor(self.amx[:, 4:5], self.amx[:, 4:5],
                            self.dqt[:, (n - 2) % 3:(n - 2) % 3 + 1], ALU.max)
            E.drain()
            ti = E.tensor_tensor(self.amx[:, 4:5], self.amx[:, 4:5],
                                 self.dqt[:, (n - 1) % 3:(n - 1) % 3 + 1], ALU.max)
            t(ti, V, "V1", 1)
        else:
            t(None, V, "V1", 1)

    def ph_b5(self, eng, E, V):
        self._barrier(eng, E, V, self.amx[:, 4:5], 5, T_G5, None, None)
        self._scal_chain(eng, E, V, [
            ("ts2", T_G5, T_S5V, 1.0 / QMAX, 1e-12),
            ("recip", T_S5V, T_IS5),
        ])
        self._bcast(eng, E, V, [(T_IS5, SC_INV_S5), (T_S5V, SC_S5)])

    def ph_down(self, eng, E, V):
        t, wt = self.tick, self.wt
        b = dict(V)
        for u in range(NU):
            lb = b["L1"] + 28 * 16 * u
            osl = u % 2
            for i2 in range(IC):
                q = IC * u + i2
                m16 = self.stg[:, 512 * (i2 % 4):512 * (i2 % 4) + 512]
                yo = self.ewA[:, 512 * (i2 % 4):512 * (i2 % 4) + 512]
                oq = self.vaug[:, 12288 * osl + 512 * i2:
                               12288 * osl + 512 * i2 + 512].bitcast(F16)
                ks = KS4[q % 4]
                if eng == "sync":
                    if q == 0:
                        wt(E, "S1", b["S1"])
                    if q >= 4:
                        wt(E, "SC", b["SC"] + (q - 4) + 1)
                    if i2 == 0 and u >= 2:
                        wt(E, "T1", b["T1"] + 4 * (u - 1))
                    t(E.dma_start(out=m16, in_=self.m2_sp[i2, u, :, :]), V, ks, 16)
                else:
                    t(None, V, ks, 16)
                if eng == "scalar":
                    wt(E, ks, b[ks] + 16 * (q // 4 + 1))
                    if q >= 4:
                        wt(E, "V1", b["V1"] + (q - 4) + 1)
                    ti = E.activation(yo, m16, AF.Copy, bias=C_MAG,
                                      scale=self.scal[:, SC_INV_S5:SC_INV_S5 + 1])
                    t(ti, V, "SC", 1)
                else:
                    t(None, V, "SC", 1)
                if eng == "vector":
                    wt(E, "SC", b["SC"] + q + 1)
                    ti = E.tensor_scalar(oq, yo, -C_MAG, None, ALU.add)
                    t(ti, V, "V1", 1)
                else:
                    t(None, V, "V1", 1)
            if eng == "sync":
                for r in range(4):
                    gi_ = 4 * u + r
                    kx = "KE" if gi_ % 2 == 0 else "KF"
                    if gi_ >= 2:
                        wt(E, "V2", b["V2"] + (gi_ - 2) + 1)
                    xin = self.xt[:, 1024 * (r % 2):1024 * (r % 2) + 1024]
                    t(E.dma_start(out=xin, in_=self.xp_sp[4 * u + r, :, :]),
                      V, kx, 16)
            else:
                for r in range(4):
                    gi_ = 4 * u + r
                    kx = "KE" if gi_ % 2 == 0 else "KF"
                    t(None, V, kx, 16)
            for r in range(4):
                gi = 4 * u + r
                pa = self.pb[2 * (r % 2)]
                pc = self.pb[2 * (r % 2) + 1]
                xin = self.xt[:, 1024 * (r % 2):1024 * (r % 2) + 1024]
                ot = self.xt[:, 2048 + 1024 * (r % 2):2048 + 1024 * (r % 2) + 1024]
                if eng == "tensor":
                    wt(E, "V1", b["V1"] + IC * (u + 1))
                    wt(E, "L2", b["L2"])
                    if gi >= 2:
                        wt(E, "V2", b["V2"] + (gi - 2) + 1)
                    for i2 in range(IC):
                        lhs = self.vaug[:, 12288 * osl + 512 * i2 + 128 * r:
                                        12288 * osl + 512 * i2 + 128 * r + 128]
                        E.matmul(pa[:, :], lhs.bitcast(F16),
                                 self.bigA[:, 1024 * i2:1024 * i2 + 512].bitcast(F16),
                                 start=(i2 == 0), stop=(i2 == IC - 1))
                        ti = E.matmul(pc[:, :], lhs.bitcast(F16),
                                      self.bigA[:, 1024 * i2 + 512:
                                                1024 * i2 + 1024].bitcast(F16),
                                      start=(i2 == 0), stop=(i2 == IC - 1))
                    t(ti, V, "T1", 1)
                else:
                    t(None, V, "T1", 1)
                if eng == "vector":
                    wt(E, "T1", b["T1"] + 4 * u + r + 1)
                    kx = "KE" if gi % 2 == 0 else "KF"
                    wt(E, kx, b[kx] + 16 * (gi // 2 + 1))
                    if gi >= 2:
                        wt(E, "S1", b["S1"] + 16 * (gi - 1))
                    E.scalar_tensor_tensor(ot[:, 0:512], pa[:, :],
                                           self.scal[:, SC_S5:SC_S5 + 1],
                                           xin[:, 0:512], ALU.mult, ALU.add)
                    ti = E.scalar_tensor_tensor(ot[:, 512:1024], pc[:, :],
                                                self.scal[:, SC_S5:SC_S5 + 1],
                                                xin[:, 512:1024], ALU.mult, ALU.add)
                    t(ti, V, "V2", 1)
                else:
                    t(None, V, "V2", 1)
                if eng == "gpsimd":
                    wt(E, "V2", b["V2"] + gi + 1)
                    ti = E.dma_start(
                        out=self.out[512 * u + 128 * r:512 * u + 128 * r + 128, :],
                        in_=ot)
                    t(ti, V, "S1", 16)
                else:
                    t(None, V, "S1", 16)

    def ph_fin(self, eng, E, V):
        t = self.tick
        if "oq_dbg" in self.dbg:
            if eng == "gpsimd":
                ti = E.dma_start(out=self.oq_dbg[:, :],
                                 in_=self.vaug[:, 12288:24576].bitcast(F16))
                t(ti, V, "S1", 16)
            else:
                t(None, V, "S1", 16)
        if "sc_dbg" in self.dbg:
            if eng == "gpsimd":
                t(E.dma_start(out=self.sc_dbg[:, 0:16], in_=self.scal[:, :]), V, "S1", 16)
                t(E.dma_start(out=self.sc_dbg[0:1, 16:40], in_=self.sct[:, :]), V, "S1", 16)
                ti = E.dma_start(out=self.sc_dbg[:, 40:48], in_=self.amx[:, :])
                t(ti, V, "S1", 16)
            else:
                for _ in range(3):
                    t(None, V, "S1", 16)
        if "agu_out" in self.dbg:
            if eng == "gpsimd":
                self.wt(E, "CC", 3)
                t(E.dma_start(out=self.agu_dbg[:, :], in_=self.agu_out[:, :]),
                  V, "S1", 16)
                t(E.dma_start(out=self.agi_dbg[:, :], in_=self.agu_in[:, :]),
                  V, "S1", 16)
            else:
                t(None, V, "S1", 16)
                t(None, V, "S1", 16)
        if eng == "gpsimd":
            E.wait_ge(self.sems["S1"], V["S1"])

    PHASES = ["init", "dequant", "stats1", "norm1", "qkv", "kv", "ksbd", "attn",
              "ln2", "xq", "up", "gt", "b2", "p2", "b4", "p3", "b5", "down"]

    def build(self, upto=None):
        seq = [
            ("init", self.ph_init, ()),
            ("dequant", self.ph_dequant, ()),
            ("stats1", self.ph_stats1, ()),
            ("norm1", self.ph_norm1, ()),
            ("qkv", self.ph_qkv, ()),
            ("kv", self.ph_kv, ()),
            ("ksbd", self.ph_ksbd, ()),
            ("attn", self.ph_attn, ()),
            ("ln2", self.ph_ln2_derive, ()),
            ("xq", self.ph_xq, ()),
            ("gt", self.ph_upgate, ("gt",)),
            ("b2", self.ph_b2, ()),
            ("upp2", self.ph_upp2, ()),
            ("b4", self.ph_b4, ()),
            ("p3", self.ph_p3, ()),
            ("b5", self.ph_b5, ()),
            ("down", self.ph_down, ()),
        ]
        if "xn_dbg" in self.dbg:
            i = [n for n, _, _ in seq].index("norm1") + 1
            seq.insert(i, ("xndbg", self.ph_xndbg, ()))
        with self.es:
            for name, fn, args in seq:
                self.phase(fn, *args)
                if upto is not None and name == upto:
                    break
            self.phase(self.ph_fin)
        return self.nc


def make_in_maps(L0_attn_hidden_states, L0_attn_Wq, L0_attn_Wk, L0_attn_Wv,
                 L0_attn_Wo, L0_attn_Wf, L0_ffn_w_gate, L0_ffn_w_up,
                 L0_ffn_w_down):
    x = np.ascontiguousarray(np.asarray(L0_attn_hidden_states, dtype=np.float32))
    wqT = np.ascontiguousarray(np.asarray(L0_attn_Wq, np.float32).T)
    wkT = np.ascontiguousarray(np.asarray(L0_attn_Wk, np.float32).T)
    wvT = np.ascontiguousarray(np.asarray(L0_attn_Wv, np.float32).T)
    woT = np.ascontiguousarray(np.asarray(L0_attn_Wo, np.float32).T)
    wfT = np.ascontiguousarray(np.asarray(L0_attn_Wf, np.float32).T)
    wup = np.ascontiguousarray(np.asarray(L0_ffn_w_up, np.float32))
    wgt = np.ascontiguousarray(np.asarray(L0_ffn_w_gate, np.float32))
    wdn = np.ascontiguousarray(np.asarray(L0_ffn_w_down, np.float32))
    identbf = np.eye(128, dtype=ml_dtypes.bfloat16)
    identf32 = np.eye(128, dtype=np.float32)
    selmat = np.kron(np.eye(16, dtype=np.float32),
                     np.ones((1, 64), dtype=np.float32))
    identf16 = np.eye(128, dtype=np.float16)
    in_maps = []
    for c in range(NCORES):
        in_maps.append({
            "x": np.ascontiguousarray(x[c]),
            "wqT": wqT, "wkT": wkT, "wvT": wvT, "woT": woT, "wfT": wfT,
            "wup_nat": np.ascontiguousarray(wup[384 * c:384 * c + 384, :]),
            "wgt_nat": np.ascontiguousarray(wgt[384 * c:384 * c + 384, :]),
            "wdn_nat": np.ascontiguousarray(wdn[128 * c:128 * c + 128, :]),
            "identbf": identbf, "identf32": identf32, "selmat": selmat,
            "identf16": identf16,
        })
    return in_maps


def kernel(**inputs):
    in_maps = make_in_maps(**inputs)
    nc = Build().build()
    res = run_bass_kernel_spmd(nc, in_maps, core_ids=list(range(NCORES)))
    out = np.stack([res.results[c]["out"] for c in range(NCORES)], axis=0)
    return out.astype(np.float32)
